# version 23
# baseline (speedup 1.0000x reference)
import numpy as np
import ml_dtypes

import concourse.bass as bass
import concourse.mybir as mybir
from concourse.bass_utils import run_bass_kernel_spmd
from contextlib import ExitStack

F32 = mybir.dt.float32
BF16 = mybir.dt.bfloat16
F16 = mybir.dt.float16
AF = mybir.ActivationFunctionType
ALU = mybir.AluOpType
AX = mybir.AxisListType.X

D = 1024
S = 2048
NPAIR = 8
NT = 16
NKC = 8
E = 8
EPS = 1e-5
MASKVAL = -57344.0
_RATIO = 2.0 ** (-8.0 / 16)
SLOPES = [float(np.float32(_RATIO ** (h + 1))) for h in range(16)]


class Builder:
    def __init__(self, tap=None):
        self.tap = tap
        nc = bass.Bass("TRN2", target_bir_lowering=False, debug=False)
        self.nc = nc
        self.x_h = nc.dram_tensor("x", [2, S, D], F32, kind="ExternalInput").ap()
        self.wq_h = nc.dram_tensor("wq", [D, D], F32, kind="ExternalInput").ap()
        self.wk_h = nc.dram_tensor("wk", [D, D], F32, kind="ExternalInput").ap()
        self.wv_h = nc.dram_tensor("wv", [D, D], F32, kind="ExternalInput").ap()
        self.wo_h = nc.dram_tensor("wo", [D, D], F32, kind="ExternalInput").ap()
        self.gwT_h = nc.dram_tensor("gwT", [D, E], F32, kind="ExternalInput").ap()
        self.gb_h = nc.dram_tensor("gb", [1, E], F32, kind="ExternalInput").ap()
        self.weT_h = nc.dram_tensor("weT", [E * D, D], F32, kind="ExternalInput").ap()
        self.be_h = nc.dram_tensor("be", [E, D], F32, kind="ExternalInput").ap()
        self.iotap_h = nc.dram_tensor("iotap", [128, 1], F32, kind="ExternalInput").ap()
        self.iota8_h = nc.dram_tensor("iota8", [1, E], F32, kind="ExternalInput").ap()
        self.idf16_h = nc.dram_tensor("idf16", [128, 128], F16, kind="ExternalInput").ap()
        self.idbf_h = nc.dram_tensor("idbf", [128, 128], BF16, kind="ExternalInput").ap()
        self.d0m_h = nc.dram_tensor("d0m", [128, 2048], F16, kind="ExternalInput").ap()
        self.g640_h = nc.dram_tensor("g640", [128, 128], F16, kind="ExternalInput").ap()
        self.t0m_h = nc.dram_tensor("t0m", [128, 2048], F32, kind="ExternalInput").ap()
        self.gt640_h = nc.dram_tensor("gt640", [128, 128], F32, kind="ExternalInput").ap()
        self.out_h = nc.dram_tensor("out", [2, S, D], F32, kind="ExternalOutput").ap()
        if tap is not None:
            self.dbg_h = nc.dram_tensor("dbg", [128, 16384], F32, kind="ExternalOutput").ap()

    def build(self):
        nc = self.nc
        with ExitStack() as st_:
            e = st_.enter_context
            self.idf16 = e(nc.sbuf_tensor("idf16s", [128, 128], F16))
            self.d0m = e(nc.sbuf_tensor("d0ms", [128, 2048], F16))
            self.g640 = e(nc.sbuf_tensor("g640s", [128, 128], F16))
            self.t0m = e(nc.sbuf_tensor("t0ms", [128, 2048], F32))
            self.gt640 = e(nc.sbuf_tensor("gt640s", [128, 128], F32))
            self.gwT = e(nc.sbuf_tensor("gwTs", [128, NKC * E], F32))
            self.gb = e(nc.sbuf_tensor("gbs", [1, E], F32))
            self.iotap = e(nc.sbuf_tensor("iotap_s", [128, 1], F32))
            self.iota8 = e(nc.sbuf_tensor("iota8_s", [1, E], F32))
            self.onesbf = e(nc.sbuf_tensor("onesbf", [128, 1], F16))
            self.ones1f = e(nc.sbuf_tensor("ones1f", [1, 128], F32))
            self.one11 = e(nc.sbuf_tensor("one11", [1, 1], F32))
            self.xnT = e(nc.sbuf_tensor("xnT", [128, NKC * S], F16))
            self.x2 = e(nc.sbuf_tensor("x2", [128, NT * D], F32))
            self.ostg = [e(nc.sbuf_tensor(f"ostg{i}", [128, 128], F16)) for i in range(2)]
            self.wpair = [[e(nc.sbuf_tensor(f"wp{i}{m}", [128, NKC * 128], F16))
                           for m in range(3)] for i in range(2)]
            self.wbuf = e(nc.sbuf_tensor("wbuf", [128, NKC * D], F16))
            self.wbuf2 = e(nc.sbuf_tensor("wbuf2", [128, NKC * D], F16))
            self.qt = e(nc.sbuf_tensor("qts", [128, S], F16))
            self.kt = e(nc.sbuf_tensor("kts", [128, S], F16))
            self.vp = e(nc.sbuf_tensor("vps", [128, NT * 130], F16))
            self.spp = [[e(nc.sbuf_tensor(f"spp{par}{h}", [128, 2048], F16)) for h in range(2)]
                        for par in range(2)]
            self.ftmp = [e(nc.sbuf_tensor(f"ftmp{i}", [128, D], F32)) for i in range(2)]
            self.pT = [e(nc.sbuf_tensor(f"pT{h}", [128, 2048], F16)) for h in range(2)]
            self.recip = e(nc.sbuf_tensor("recips", [128, 4], F32))
            self.xtile = self.ftmp
            self.bnst = e(nc.sbuf_tensor("bnst", [128, 12 * NT], F32))
            self.mvall = e(nc.sbuf_tensor("mvall", [128, 2 * NT], F32))
            self.rstd = e(nc.sbuf_tensor("rstd", [128, NT], F32))
            self.negmr = e(nc.sbuf_tensor("negmr", [128, NT], F32))
            self.lntmp = e(nc.sbuf_tensor("lntmp", [128, NT], F32))
            self.xntm = e(nc.sbuf_tensor("xntm", [128, D], F16))
            self.pooledT = e(nc.sbuf_tensor("pooledT", [128, 8], F32))
            self.lgt = e(nc.sbuf_tensor("lgt", [1, E], F32))
            self.gsc = e(nc.sbuf_tensor("gsc", [1, 8], F32))
            self.gates = e(nc.sbuf_tensor("gates", [1, E], F32))
            self.gsel = e(nc.sbuf_tensor("gsel", [128, 2], F32))
            self.idxw = e(nc.sbuf_tensor("idxw", [128, 2 * NKC], mybir.dt.uint32))
            self.idxwf = e(nc.sbuf_tensor("idxwf", [128, 2 * NKC], F32))
            self.idxb = e(nc.sbuf_tensor("idxb", [128, 2], mybir.dt.uint32))
            self.wexp = [self.wbuf, self.wbuf2]
            self.bbc = e(nc.sbuf_tensor("bbc", [128, D], F16))
            self.pooled_sb = self.ftmp[0][0:1, 0:D]
            self.ps_s = e(nc.psum_tensor("ps_s", [128, 2048], F32))
            self.ps_t = [e(nc.psum_tensor(f"ps_t{h}", [128, 1024], F16)) for h in range(2)]
            self.ps_o = [e(nc.psum_tensor(f"ps_o{h}", [128, 512], F32)) for h in range(2)]

            class Led:
                def __init__(s, name, dma=False):
                    s.h = e(nc.semaphore(name))
                    s.n = 0
                    s.step = 16 if dma else 1

            self.dx = Led("dx", dma=True)
            self.dw = Led("dw", dma=True)
            self.ts = Led("ts")
            self.vs = Led("vs")
            self.ss = Led("ss")
            self.so = Led("so", dma=True)
            self.dxx = Led("dxx", dma=True)
            self.dxx2 = Led("dxx2", dma=True)

            self.block = e(nc.Block())
            self.pso_free = (self.vs, 0)
            self.exp_last = [0, 0]
            self.v_last_stt = 0
            self.s_last_oevac = 0
            self.done = False
            self.xload_base = {}

            self._consts()
            for b in range(2):
                self._seq(b)
                if self.done:
                    break

            SO_END = self.so.n

            @self.block.sync
            def _(sync):
                sync.wait_ge(self.so.h, SO_END)
        return nc

    def ot_ap(self, pp, cols):
        base = self.wbuf if pp < 4 else self.wbuf2
        off = 2048 * (pp % 4)
        return base[:, off + cols[0]:off + cols[1]]

    def tap_dump(self, sbuf_ap, nrows=128):
        dbg = self.dbg_h
        vsn, ssn, tsn = self.vs.n, self.ss.n, self.ts.n
        dxn = self.dx.n
        fs = min(sbuf_ap.shape[-1], 16384)

        @self.block.gpsimd
        def _(gp):
            gp.wait_ge(self.vs.h, vsn)
            gp.wait_ge(self.ss.h, ssn)
            gp.wait_ge(self.ts.h, tsn)
            gp.dma_start(out=dbg[0:nrows, 0:fs], in_=sbuf_ap[0:nrows, 0:fs]).then_inc(self.so.h, 16)
        self.so.n += 16
        self.done = True

    def _consts(self):
        dx, vs = self.dx, self.vs

        @self.block.sync
        def _(sync):
            sync.dma_start(out=self.idf16[:], in_=self.idf16_h[:, :]).then_inc(dx.h, 16)
            sync.dma_start(out=self.d0m[:], in_=self.d0m_h[:, :]).then_inc(dx.h, 16)
            sync.dma_start(out=self.g640[:], in_=self.g640_h[:, :]).then_inc(dx.h, 16)
            sync.dma_start(out=self.t0m[:], in_=self.t0m_h[:, :]).then_inc(dx.h, 16)
            sync.dma_start(out=self.gt640[:], in_=self.gt640_h[:, :]).then_inc(dx.h, 16)
            sync.dma_start(out=self.gwT[:].rearrange("p (kc e) -> p kc e", kc=NKC),
                           in_=self.gwT_h.rearrange("(kc p) e -> p kc e", p=128)).then_inc(dx.h, 16)
            sync.dma_start(out=self.gb[:], in_=self.gb_h[:, :]).then_inc(dx.h, 16)
            sync.dma_start(out=self.iotap[:], in_=self.iotap_h[:, :]).then_inc(dx.h, 16)
            sync.dma_start(out=self.iota8[:], in_=self.iota8_h[:, :]).then_inc(dx.h, 16)
        dx.n += 9 * 16
        self.CONST_DX = dx.n

        CONST_DX = dx.n

        @self.block.vector
        def _(v):
            v.wait_ge(dx.h, CONST_DX)
            v.memset(self.onesbf[:], 1.0)
            v.memset(self.ones1f[:], 1.0)
            v.memset(self.one11[:], 1.0)
            vpv = self.vp[:].rearrange("p (t c) -> p t c", c=130)
            v.memset(vpv[:, :, 64:65], 1.0)
            m = v.memset(vpv[:, :, 129:130], 1.0)
            m.then_inc(vs.h, 1)
        vs.n += 1
        self.VS_CONST = vs.n

    def _seq(self, b):
        nc, block = self.nc, self.block
        dx, dw, ts, vs, ss, so = self.dx, self.dw, self.ts, self.vs, self.ss, self.so
        wbig = lambda w: w.rearrange("(kc p) d -> p (kc d)", p=128)

        TS_B0, VS_B0, SS_B0, SO_B0 = ts.n, vs.n, ss.n, so.n

        self.TS_B0 = TS_B0
        DW_QKV = 0

        VA0 = vs.n
        SA0 = ss.n
        pst0 = self.ps_t[0]

        dxx = self.dxx if b == 0 else self.dxx2
        if b == 0:
            XB = dxx.n

            @block.sync
            def _(sync):
                for t in range(NT):
                    sync.dma_start(out=self.x2[:, D * t:D * (t + 1)],
                                   in_=self.x_h[b, 128 * t:128 * (t + 1), :]).then_inc(dxx.h, 16)
            dxx.n += NT * 16
            self.xload_base[0] = XB
        XB = self.xload_base[b]

        @block.vector
        def _(v):
            for t in range(NT):
                if t == 0:
                    v.wait_ge(dxx.h, XB + 16 * NT)
                xt = self.x2[:, D * t:D * (t + 1)]
                v.bn_stats(out=self.bnst[:, 12 * t:12 * t + 6], in_=xt[:, 0:512])
                v.bn_stats(out=self.bnst[:, 12 * t + 6:12 * t + 12], in_=xt[:, 512:1024]).then_inc(vs.h, 1)
            v.drain()
            for t in range(NT):
                v.bn_aggr(out=self.mvall[:, 2 * t:2 * t + 2], in_=self.bnst[:, 12 * t:12 * t + 12])
            v.drain()
            v.tensor_scalar(out=self.lntmp[:], in0=self.mvall[:, 1:2 * NT:2],
                            scalar1=EPS, scalar2=None, op0=ALU.add)
            v.drain()
            v.reciprocal(out=self.rstd[:], in_=self.lntmp[:]).then_inc(vs.h, 1)
        vs.n += NT + 1
        VA_END = vs.n

        @block.scalar
        def _(s):
            s.wait_ge(vs.h, VA_END)
            s.activation(out=self.rstd[:], in_=self.rstd[:], func=AF.Sqrt).then_inc(ss.h, 1)
        ss.n += 1
        SS_RSTD = ss.n

        @block.vector
        def _(v):
            v.wait_ge(ss.h, SS_RSTD)
            v.scalar_tensor_tensor(out=self.negmr[:], in0=self.mvall[:, 0:2 * NT:2],
                                   scalar=-1.0, in1=self.rstd[:],
                                   op0=ALU.mult, op1=ALU.mult).then_inc(vs.h, 1)
        vs.n += 1
        VS_NEGMR = vs.n

        SB0 = ss.n
        TB0 = ts.n
        pst0 = self.ps_t[0]

        @block.scalar
        def _(s):
            for t in range(NT):
                if t == 0:
                    s.wait_ge(vs.h, VS_NEGMR)
                    s.wait_ge(ts.h, TS_B0)
                s.activation(out=self.xntm[:], in_=self.x2[:, D * t:D * (t + 1)], func=AF.Identity,
                             scale=self.rstd[:, t:t + 1], bias=self.negmr[:, t:t + 1]).then_inc(ss.h, 1)
                s.wait_ge(ts.h, TB0 + t + 1)
                dst = self.xnT[:].rearrange("p (kc s) -> p kc s", kc=NKC)[:, :, 128 * t:128 * (t + 1)]
                src = pst0[:, 0:1024].rearrange("p (kc f) -> p kc f", kc=NKC)
                s.copy(out=dst, in_=src).then_inc(ss.h, 1)
        ss.n += 2 * NT
        SB_END = ss.n

        @block.tensor
        def _(t_):
            for t in range(NT):
                t_.wait_ge(ss.h, SB0 + 2 * t + 1)
                if t >= 1:
                    t_.wait_ge(ss.h, SB0 + 2 * t)
                for kc in range(NKC):
                    mm = t_.transpose(out=pst0[:, 128 * kc:128 * (kc + 1)],
                                      in_=self.xntm[:, 128 * kc:128 * (kc + 1)],
                                      identity=self.idf16[:])
                mm.then_inc(ts.h, 1)
        ts.n += NT
        TS_LN1_END = ts.n

        if self.tap == "xnT" and b == 0:
            self.tap_dump(self.xnT)
            return

        self._attention(b, DW_QKV, SB_END, TS_LN1_END)
        if self.done:
            return
        self._oproj(b, SO_B0)
        if self.done:
            return
        self._ln2_gates(b)
        if self.done:
            return
        self._ffn(b)
        if self.done:
            return

        VF0 = self.VF0
        XBN = self.dxx2.n
        dxx2 = self.dxx2

        @block.sync
        def _(sync):
            for t in range(NT):
                sync.wait_ge(vs.h, VF0 + 1 * (2 * NT) + 2 * t + 2)
                sync.dma_start(out=self.out_h[b, 128 * t:128 * (t + 1), :],
                               in_=self.ftmp[t % 2][:]).then_inc(so.h, 16)
        so.n += NT * 16

        if b == 0:
            @block.gpsimd
            def _(gp):
                for t in range(NT):
                    gp.wait_ge(vs.h, VF0 + 1 * (2 * NT) + 2 * t + 2)
                    gp.dma_start(out=self.x2[:, D * t:D * (t + 1)],
                                 in_=self.x_h[1, 128 * t:128 * (t + 1), :]).then_inc(dxx2.h, 16)
            dxx2.n += NT * 16
            self.xload_base[1] = XBN

    def _attention(self, b, DW_QKV, SS_LN1_END, TS_LN1_END):
        block = self.block
        dx, dw, ts, vs, ss = self.dx, self.dw, self.ts, self.vs, self.ss
        ps_s, ps_t, ps_o = self.ps_s, self.ps_t, self.ps_o
        qt, kt, vp, spp, pT = self.qt, self.kt, self.vp, self.spp, self.pT

        if not hasattr(self, "proj_end_marks"):
            self.proj_end_marks = {}
        wsrc = [self.wq_h, self.wk_h, self.wv_h]
        DWB = dw.n
        for pp in range(NPAIR):
            prev_mark = self.proj_end_marks.get(pp - 2, self.TS_B0)

            @block.gpsimd
            def _(gp):
                gp.wait_ge(ts.h, prev_mark)
                for m in range(3):
                    src_ap = wsrc[m][:, 128 * pp:128 * (pp + 1)].rearrange(
                        "(kc p) d -> p kc d", p=128)
                    dst_ap = self.wpair[pp % 2][m][:].rearrange("p (kc d) -> p kc d", kc=NKC)
                    gp.dma_start(out=dst_ap, in_=src_ap).then_inc(dw.h, 16)
            dw.n += 48

            TP0 = ts.n
            SP0 = ss.n
            V_LAST_STT = self.v_last_stt
            PSO_FREE = self.pso_free
            S_LAST_OEVAC = self.s_last_oevac
            DW_PAIR = DWB + 48 * min(pp + 2, NPAIR)
            wp = self.wpair[pp % 2]

            @block.tensor
            def _(t_):
                t_.wait_ge(dw.h, DW_PAIR)
                if pp == 0:
                    t_.wait_ge(ss.h, SS_LN1_END)
                for i in range(8):
                    wmat = wp[0] if i < 4 else wp[1]
                    tc4 = i % 4
                    if i >= 4:
                        t_.wait_ge(ss.h, SP0 + (i - 4) + 1)
                    elif pp > 0:
                        t_.wait_ge(vs.h, V_LAST_STT)
                    bank = ps_s[:, 512 * tc4:512 * (tc4 + 1)]
                    for kc in range(NKC):
                        mm = t_.matmul(out=bank,
                                       lhsT=wmat[:, 128 * kc:128 * (kc + 1)],
                                       rhs=self.xnT[:, S * kc + 512 * tc4:S * kc + 512 * (tc4 + 1)],
                                       start=(kc == 0), stop=(kc == NKC - 1))
                    mm.then_inc(ts.h, 1)
                for tc in range(NT):
                    if tc >= 2:
                        t_.wait_ge(ss.h, SP0 + 8 + (tc - 2) + 1)
                    else:
                        led, cnt = PSO_FREE
                        t_.wait_ge(led.h, cnt)
                        t_.wait_ge(ss.h, S_LAST_OEVAC)
                        if pp == 0:
                            t_.wait_ge(vs.h, self.VS_CONST)
                    bank = ps_o[tc % 2][:, 0:128]
                    for kc in range(NKC):
                        mm = t_.matmul(out=bank,
                                       lhsT=self.xnT[:, S * kc + 128 * tc:S * kc + 128 * (tc + 1)],
                                       rhs=wp[2][:, 128 * kc:128 * (kc + 1)],
                                       start=(kc == 0), stop=(kc == NKC - 1))
                    mm.then_inc(ts.h, 1)
            ts.n += 24
            self.proj_end_marks[pp] = ts.n

            @block.scalar
            def _(s):
                for i in range(8):
                    s.wait_ge(ts.h, TP0 + i + 1)
                    dst = (qt if i < 4 else kt)[:, 512 * (i % 4):512 * (i % 4 + 1)]
                    if i < 4:
                        s.activation(out=dst, in_=ps_s[:, 512 * (i % 4):512 * (i % 4) + 512],
                                     func=AF.Copy, scale=0.125).then_inc(ss.h, 1)
                    else:
                        s.copy(out=dst, in_=ps_s[:, 512 * (i % 4):512 * (i % 4) + 512]).then_inc(ss.h, 1)
                for tc in range(NT):
                    s.wait_ge(ts.h, TP0 + 8 + tc + 1)
                    src = ps_o[tc % 2][:, 0:128].rearrange("p (h f) -> p h f", h=2)
                    dst = vp[:, 130 * tc:130 * (tc + 1)].rearrange("p (h f) -> p h f", f=65)[:, :, 0:64]
                    s.copy(out=dst, in_=src).then_inc(ss.h, 1)
            ss.n += 24
            SS_PROJ_END = ss.n

            slope_min = SLOPES[2 * pp + 1]
            wb = int(np.ceil((88.0 / slope_min) / 128.0)) + 1
            nchs, nblks, blos, tailbs = [], [], [], []
            for ti in range(NT):
                b_lo = max(0, (ti + 1) - wb)
                nblk_eff = (ti + 1) - b_lo
                tailb = ((nblk_eff - 1) % 4) + 1
                nf = (nblk_eff - tailb) // 4
                blos.append(b_lo)
                nchs.append(nf + 1)
                nblks.append(nblk_eff)
                tailbs.append(tailb)

            def group_cols(ti, b):
                blo, nf, tailb = blos[ti], nchs[ti] - 1, tailbs[ti]
                Bt = blo + 4 * nf
                if b >= Bt:
                    return 512 * nf + 128 * (ti - b)
                c = (b - blo) // 4
                g_hi = blo + 4 * c + 3
                return 512 * c + 128 * (g_hi - b)

            t = ts.n
            t_qk, t_pv, t_otr = {}, {}, {}
            t_qk[0] = t; t += nchs[0]
            for ti in range(NT):
                if ti + 1 < NT:
                    t_qk[ti + 1] = t; t += nchs[ti + 1]
                if ti >= 1:
                    t_otr[ti - 1] = t; t += 1
                t_pv[ti] = t; t += 2
            t_otr[NT - 1] = t; t += 1
            TS_END = t

            v = vs.n
            v_stt, v_rec, v_evac = {}, {}, {}
            v_stt[0] = v; v += 2 * nchs[0]
            for ti in range(NT):
                if ti >= 1:
                    v_evac[ti - 1] = v; v += 2
                if ti + 1 < NT:
                    v_stt[ti + 1] = v; v += 2 * nchs[ti + 1]
                v_rec[ti] = v; v += 2
            v_evac[NT - 1] = v; v += 2
            VS_END = v

            s = ss.n
            s_exp, s_otev = {}, {}
            for ti in range(NT):
                if ti >= 1:
                    s_otev[ti - 1] = s; s += 1
                s_exp[ti] = s; s += 2 * nchs[ti]
            s_otev[NT - 1] = s; s += 1
            SS_END = s

            qk_wait, qk_pair = {}, {}
            if not hasattr(self, "sbank_mark2"):
                self.sbank_mark2 = [0, 0]
            sbm = list(self.sbank_mark2)
            gc_ = getattr(self, "gchunk", 0)
            for ti in range(NT):
                for c in range(nchs[ti]):
                    pr = gc_ % 2
                    qk_pair[(ti, c)] = pr
                    qk_wait[(ti, c)] = sbm[pr]
                    sbm[pr] = min(v_stt[ti] + 2 * c + 3, v_stt[ti] + 2 * nchs[ti])
                    gc_ += 1
            self.sbank_mark2 = sbm
            self.gchunk = gc_

            if not hasattr(self, "pso_free_v"):
                self.pso_free_v = [0, 0]
            PSO_FREE_V = list(self.pso_free_v)
            OTR_PREV = getattr(self, "otr_prev", 0)
            OTEV_PREV = getattr(self, "otev_prev", 0)
            qk_ss_seen = [0]

            def emit_qk(t_, ti):
                blo, nf, tailb, nch = blos[ti], nchs[ti] - 1, tailbs[ti], nchs[ti]
                for c in range(nch):
                    t_.wait_ge(vs.h, qk_wait[(ti, c)])
                    bank_free = SP0 + 8
                    qt_need = SP0 + (128 * ti) // 512 + 1
                    if c < nf:
                        bs = list(range(blo + 4 * c, blo + 4 * c + 4))
                    else:
                        bs = list(range(blo + 4 * nf, ti + 1))
                    kt_need = SP0 + 4 + (128 * (bs[-1] + 1) - 1) // 512 + 1
                    ssw = max(bank_free, qt_need, kt_need)
                    if ssw > qk_ss_seen[0]:
                        t_.wait_ge(ss.h, ssw)
                        qk_ss_seen[0] = ssw
                    lo = 1024 * qk_pair[(ti, c)]
                    for b in bs:
                        colb = group_cols(ti, b) - 512 * c
                        mm = t_.matmul(out=ps_s[:, lo + colb:lo + colb + 128],
                                       lhsT=kt[0:64, 128 * b:128 * (b + 1)],
                                       rhs=qt[0:64, 128 * ti:128 * (ti + 1)],
                                       start=True, stop=True, tile_position=(0, 0))
                        mm = t_.matmul(out=ps_s[:, lo + 512 + colb:lo + 512 + colb + 128],
                                       lhsT=kt[64:128, 128 * b:128 * (b + 1)],
                                       rhs=qt[64:128, 128 * ti:128 * (ti + 1)],
                                       start=True, stop=True, tile_position=(64, 0))
                    mm.then_inc(ts.h, 1)

            @block.tensor
            def _(t_):
                emit_qk(t_, 0)
                for ti in range(NT):
                    if ti + 1 < NT:
                        emit_qk(t_, ti + 1)
                    if ti >= 1:
                        t_.wait_ge(vs.h, v_evac[ti - 1] + 2)
                        pfree = (s_otev[ti - 2] + 1) if ti >= 2 else OTEV_PREV
                        if pfree > 0:
                            t_.wait_ge(ss.h, pfree)
                        t_.transpose(out=self.ps_t[0][:, 0:128], in_=self.ostg[(ti - 1) % 2][:],
                                     identity=self.idf16[:]).then_inc(ts.h, 1)
                    nblk, blo = nblks[ti], blos[ti]
                    nchv = nchs[ti]
                    for hh in range(2):
                        t_.wait_ge(vs.h, PSO_FREE_V[hh])
                        pso = ps_o[hh]
                        seen_c = -1
                        for bb in range(nblk):
                            ab = bb + blo
                            cc = min((ab - blo) // 4, nchv - 1)
                            if cc > seen_c:
                                t_.wait_ge(ss.h, s_exp[ti] + 2 * cc + hh + 1)
                                seen_c = cc
                            col = group_cols(ti, ab)
                            mm = t_.matmul(out=pso[:, 0:65],
                                           lhsT=pT[hh][:, col:col + 128],
                                           rhs=vp[:, 130 * ab + 65 * hh:130 * ab + 65 * hh + 65],
                                           start=(bb == 0), stop=(bb == nblk - 1))
                        mm.then_inc(ts.h, 1)
                    PSO_FREE_V[0] = v_evac[ti] + 1
                    PSO_FREE_V[1] = v_evac[ti] + 2
                t_.wait_ge(vs.h, v_evac[NT - 1] + 2)
                t_.wait_ge(ss.h, s_otev[NT - 2] + 1)
                t_.transpose(out=self.ps_t[0][:, 0:128], in_=self.ostg[(NT - 1) % 2][:],
                             identity=self.idf16[:]).then_inc(ts.h, 1)
            ts.n = TS_END
            self.pso_free_v = list(PSO_FREE_V)

            @block.vector
            def _(v_):
                def emit_stt(ti):
                    blo, nf, tailb, nch = blos[ti], nchs[ti] - 1, tailbs[ti], nchs[ti]
                    for c in range(nch):
                        v_.wait_ge(ts.h, t_qk[ti] + c + 1)
                        lo = 1024 * qk_pair[(ti, c)]
                        sppb = 512 * c
                        for hh in range(2):
                            h = 2 * pp + hh
                            if h % 2 == 1:
                                tabf, tabg = self.d0m, self.g640
                                sc = 2.0 ** (-((h - 1) // 2 + 1))
                            else:
                                tabf, tabg = self.t0m, self.gt640
                                sc = 2.0 ** (-(h // 2))
                            if c < nf:
                                g_hi = blo + 4 * c + 3
                                u0 = 128 * (ti - g_hi)
                                m = v_.scalar_tensor_tensor(out=spp[ti % 2][hh][:, sppb:sppb + 512],
                                                            in0=tabf[:, u0:u0 + 512], scalar=sc,
                                                            in1=ps_s[:, lo + 512 * hh:lo + 512 * hh + 512],
                                                            op0=ALU.mult, op1=ALU.add)
                            else:
                                m = v_.scalar_tensor_tensor(out=spp[ti % 2][hh][:, sppb:sppb + 128],
                                                            in0=tabg[:], scalar=sc,
                                                            in1=ps_s[:, lo + 512 * hh:lo + 512 * hh + 128],
                                                            op0=ALU.mult, op1=ALU.add)
                                if tailb > 1:
                                    w = 128 * (tailb - 1)
                                    m = v_.scalar_tensor_tensor(out=spp[ti % 2][hh][:, sppb + 128:sppb + 128 + w],
                                                                in0=tabf[:, 128:128 + w], scalar=sc,
                                                                in1=ps_s[:, lo + 512 * hh + 128:lo + 512 * hh + 128 + w],
                                                                op0=ALU.mult, op1=ALU.add)
                            m.then_inc(vs.h, 1)

                def emit_evac(ti):
                    v_.wait_ge(vs.h, v_rec[ti] + 2)
                    if ti >= 2:
                        v_.wait_ge(ts.h, t_otr[ti - 2] + 1)
                    else:
                        v_.wait_ge(ts.h, OTR_PREV)
                    for hh in range(2):
                        v_.tensor_scalar(out=self.ostg[ti % 2][:, 64 * hh:64 * hh + 64],
                                         in0=ps_o[hh][:, 0:64],
                                         scalar1=self.recip[:, 2 * (ti % 2) + hh:2 * (ti % 2) + hh + 1],
                                         scalar2=None, op0=ALU.mult).then_inc(vs.h, 1)

                def emit_rec(ti):
                    for hh in range(2):
                        v_.wait_ge(ts.h, t_pv[ti] + hh + 1)
                        v_.reciprocal(out=self.recip[:, 2 * (ti % 2) + hh:2 * (ti % 2) + hh + 1],
                                      in_=ps_o[hh][:, 64:65]).then_inc(vs.h, 1)

                emit_stt(0)
                for ti in range(NT):
                    if ti >= 1:
                        emit_evac(ti - 1)
                    if ti + 1 < NT:
                        emit_stt(ti + 1)
                    emit_rec(ti)
                emit_evac(NT - 1)
            vs.n = VS_END
            self.v_last_stt = v_stt[NT - 1] + 2 * nchs[NT - 1]
            self.otr_prev = t_otr[NT - 1] + 1

            @block.scalar
            def _(s_):
                for ti in range(NT):
                    if ti >= 1:
                        s_.wait_ge(ts.h, t_otr[ti - 1] + 1)
                        s_.copy(out=self.ot_ap(pp, (128 * (ti - 1), 128 * ti)),
                                in_=self.ps_t[0][:, 0:128]).then_inc(ss.h, 1)
                    for c in range(nchs[ti]):
                        hi = min(512 * (c + 1), 128 * nblks[ti])
                        for hh in range(2):
                            s_.wait_ge(vs.h, v_stt[ti] + 2 * c + hh + 1)
                            s_.activation(out=pT[hh][:, 512 * c:hi],
                                          in_=spp[ti % 2][hh][:, 512 * c:hi],
                                          func=AF.Exp).then_inc(ss.h, 1)
                s_.wait_ge(ts.h, t_otr[NT - 1] + 1)
                s_.copy(out=self.ot_ap(pp, (128 * (NT - 1), 128 * NT)),
                        in_=self.ps_t[0][:, 0:128]).then_inc(ss.h, 1)
            ss.n = SS_END
            self.s_last_oevac = SS_END
            self.otev_prev = s_otev[NT - 1] + 1
            self.pso_free = (self.vs, v_evac[NT - 1] + 2)
            self.spp_free = [s_exp[14] + 2 * nchs[14], s_exp[15] + 2 * nchs[15]]

        if self.tap == "obuf" and b == 0:
            self.tap_dump(self.wbuf)
        if self.tap == "obuf2" and b == 0:
            self.tap_dump(self.wbuf2)

    def _oproj(self, b, SO_B0):
        block = self.block
        dx, dw, ts, vs, ss = self.dx, self.dw, self.ts, self.vs, self.ss
        ps_s, ps_o = self.ps_s, self.ps_o

        TS_ATTN_END = ts.n
        SS_ATTN_END = ss.n
        V_LAST_STT = self.v_last_stt
        wo_parts = [self.spp[0][0], self.spp[0][1], self.spp[1][0], self.spp[1][1]]

        SPP_FREE = self.spp_free

        @block.gpsimd
        def _(gp):
            for q in range(4):
                gp.wait_ge(ss.h, SPP_FREE[0] if q < 2 else SPP_FREE[1])
                srcw = self.wo_h[256 * q:256 * (q + 1), :].rearrange("(kc p) d -> p kc d", p=128)
                dstw = wo_parts[q][:].rearrange("p (kc d) -> p kc d", kc=2)
                gp.dma_start(out=dstw, in_=srcw).then_inc(dw.h, 16)
        dw.n += 64
        DW_O = dw.n

        def wo_ap(kc, cols):
            part = wo_parts[kc // 2]
            off = D * (kc % 2)
            return part[:, off + cols[0]:off + cols[1]]

        VC0 = vs.n

        TAO0 = ts.n

        @block.tensor
        def _(t_):
            t_.wait_ge(dw.h, DW_O)
            t_.wait_ge(vs.h, V_LAST_STT)
            t_.wait_ge(ss.h, SS_ATTN_END)
            for tc in range(NT):
                lo = 1024 * (tc % 2)
                if tc >= 2:
                    t_.wait_ge(vs.h, VC0 + (tc - 2) + 1)
                for half in range(2):
                    for kc in range(NKC):
                        mm = t_.matmul(out=ps_s[:, lo + 512 * half:lo + 512 * half + 512],
                                       lhsT=self.ot_ap(kc, (128 * tc, 128 * (tc + 1))),
                                       rhs=wo_ap(kc, (512 * half, 512 * half + 512)),
                                       start=(kc == 0), stop=(kc == NKC - 1))
                mm.then_inc(ts.h, 1)
        ts.n += NT

        @block.vector
        def _(v):
            for tc in range(NT):
                v.wait_ge(ts.h, TAO0 + tc + 1)
                lo = 1024 * (tc % 2)
                v.tensor_tensor(out=self.x2[:, D * tc:D * tc + 512],
                                in0=ps_s[:, lo:lo + 512],
                                in1=self.x2[:, D * tc:D * tc + 512], op=ALU.add)
                v.tensor_tensor(out=self.x2[:, D * tc + 512:D * tc + 1024],
                                in0=ps_s[:, lo + 512:lo + 1024],
                                in1=self.x2[:, D * tc + 512:D * tc + 1024], op=ALU.add).then_inc(vs.h, 1)
        vs.n += NT
        self.VC_END = vs.n
        self.TAO_END = ts.n

        if self.tap == "x2attn" and b == 0:
            self.tap_dump(self.x2)

    def _ln2_gates(self, b):
        block = self.block
        dx, dw, ts, vs, ss = self.dx, self.dw, self.ts, self.vs, self.ss
        ps_s, ps_o = self.ps_s, self.ps_o
        pst0 = self.ps_t[0]
        VC_END = self.VC_END

        VD0 = vs.n

        VC0 = VC_END - NT

        @block.vector
        def _(v):
            for t in range(NT):
                v.wait_ge(vs.h, VC0 + t + 1)
                xt = self.x2[:, D * t:D * (t + 1)]
                v.bn_stats(out=self.bnst[:, 12 * t:12 * t + 6], in_=xt[:, 0:512])
                v.bn_stats(out=self.bnst[:, 12 * t + 6:12 * t + 12], in_=xt[:, 512:1024]).then_inc(vs.h, 1)
            v.drain()
            for t in range(NT):
                v.bn_aggr(out=self.mvall[:, 2 * t:2 * t + 2], in_=self.bnst[:, 12 * t:12 * t + 12])
            v.drain()
            v.tensor_scalar(out=self.lntmp[:], in0=self.mvall[:, 1:2 * NT:2],
                            scalar1=EPS, scalar2=None, op0=ALU.add)
            v.drain()
            v.reciprocal(out=self.rstd[:], in_=self.lntmp[:]).then_inc(vs.h, 1)
        vs.n += NT + 1
        VD_STATS = vs.n

        @block.scalar
        def _(s):
            s.wait_ge(vs.h, VD_STATS)
            s.activation(out=self.rstd[:], in_=self.rstd[:], func=AF.Sqrt).then_inc(ss.h, 1)
        ss.n += 1
        SS_RSTD2 = ss.n

        @block.vector
        def _(v):
            v.wait_ge(ss.h, SS_RSTD2)
            v.scalar_tensor_tensor(out=self.negmr[:], in0=self.mvall[:, 0:2 * NT:2],
                                   scalar=-1.0, in1=self.rstd[:],
                                   op0=ALU.mult, op1=ALU.mult).then_inc(vs.h, 1)
        vs.n += 1
        VD_NEGMR = vs.n

        SD0 = ss.n
        TD0 = ts.n
        S_LAST_OEVAC = self.s_last_oevac

        TAO_END = self.TAO_END

        @block.scalar
        def _(s):
            for t in range(NT):
                if t == 0:
                    s.wait_ge(vs.h, VD_NEGMR)
                    s.wait_ge(ts.h, TAO_END)
                else:
                    s.wait_ge(ts.h, TD0 + 2 * t)
                s.activation(out=self.xntm[:], in_=self.x2[:, D * t:D * (t + 1)], func=AF.Identity,
                             scale=self.rstd[:, t:t + 1], bias=self.negmr[:, t:t + 1]).then_inc(ss.h, 1)
                s.wait_ge(ts.h, TD0 + 2 * t + 1)
                dst = self.xnT[:].rearrange("p (kc s) -> p kc s", kc=NKC)[:, :, 128 * t:128 * (t + 1)]
                src = pst0[:, 0:1024].rearrange("p (kc f) -> p kc f", kc=NKC)
                s.copy(out=dst, in_=src).then_inc(ss.h, 1)
        ss.n += 2 * NT
        SD_END = ss.n

        @block.tensor
        def _(t_):
            for t in range(NT):
                t_.wait_ge(ss.h, SD0 + 2 * t + 1)
                if t == 0:
                    t_.wait_ge(vs.h, VC_END)
                    t_.wait_ge(ss.h, S_LAST_OEVAC)
                else:
                    t_.wait_ge(ss.h, SD0 + 2 * t)
                for kc in range(NKC):
                    mm = t_.transpose(out=pst0[:, 128 * kc:128 * (kc + 1)],
                                      in_=self.xntm[:, 128 * kc:128 * (kc + 1)],
                                      identity=self.idf16[:])
                mm.then_inc(ts.h, 1)
                for half in range(2):
                    mm = t_.matmul(out=ps_o[half][0:1, 0:512],
                                   lhsT=self.onesbf[:], rhs=self.xntm[:, 512 * half:512 * half + 512],
                                   start=(t == 0), stop=(t == NT - 1), skip_group_check=True)
                mm.then_inc(ts.h, 1)
        ts.n += 2 * NT
        TD_END = ts.n

        @block.scalar
        def _(s):
            s.wait_ge(ts.h, TD_END)
            s.mul(out=self.pooled_sb[:, 0:512], in_=ps_o[0][0:1, 0:512], mul=1.0 / S)
            s.mul(out=self.pooled_sb[:, 512:1024], in_=ps_o[1][0:1, 0:512], mul=1.0 / S).then_inc(ss.h, 1)
        ss.n += 1
        SS_POOL = ss.n

        @block.tensor
        def _(t_):
            t_.wait_ge(ss.h, SS_POOL)
            for c in range(NKC):
                mm = t_.matmul(out=ps_s[:, c:c + 1], lhsT=self.pooled_sb[:, 128 * c:128 * (c + 1)],
                               rhs=self.one11[:], start=True, stop=True)
            mm.then_inc(ts.h, 1)
        ts.n += 1
        TS_PT = ts.n

        @block.vector
        def _(v):
            v.wait_ge(ts.h, TS_PT)
            v.tensor_copy(out=self.pooledT[:], in_=ps_s[:, 0:8]).then_inc(vs.h, 1)
        vs.n += 1
        VS_PT = vs.n

        @block.tensor
        def _(t_):
            t_.wait_ge(vs.h, VS_PT)
            for c in range(NKC):
                mm = t_.matmul(out=ps_o[0][0:1, 0:8], lhsT=self.pooledT[:, c:c + 1],
                               rhs=self.gwT[:, E * c:E * (c + 1)],
                               start=(c == 0), stop=(c == NKC - 1))
            mm.then_inc(ts.h, 1)
        ts.n += 1
        TS_LG = ts.n

        @block.vector
        def _(v):
            v.wait_ge(ts.h, TS_LG)
            v.tensor_tensor(out=self.lgt[:], in0=ps_o[0][0:1, 0:8], in1=self.gb[:], op=ALU.add)
            v.drain()
            v.reduce_max(out=self.gsc[:, 0:1], in_=self.lgt[:], axis=AX)
            v.drain()
            v.tensor_scalar(out=self.gates[:], in0=self.lgt[:], scalar1=self.gsc[:, 0:1],
                            scalar2=-1e30, op0=ALU.is_ge, op1=ALU.mult)
            v.drain()
            v.tensor_tensor(out=self.gates[:], in0=self.gates[:], in1=self.lgt[:], op=ALU.add)
            v.drain()
            v.reduce_max(out=self.gsc[:, 1:2], in_=self.gates[:], axis=AX)
            v.drain()
            v.tensor_scalar(out=self.gates[:], in0=self.lgt[:], scalar1=self.gsc[:, 1:2],
                            scalar2=-1e30, op0=ALU.is_lt, op1=ALU.mult)
            v.drain()
            v.tensor_tensor(out=self.gates[:], in0=self.gates[:], in1=self.lgt[:], op=ALU.add)
            v.drain()
            v.tensor_scalar(out=self.gates[:], in0=self.gates[:], scalar1=self.gsc[:, 0:1],
                            scalar2=None, op0=ALU.subtract).then_inc(vs.h, 1)
        vs.n += 1
        VS_ARG = vs.n

        @block.scalar
        def _(s):
            s.wait_ge(vs.h, VS_ARG)
            s.activation(out=self.gates[:], in_=self.gates[:], func=AF.Exp).then_inc(ss.h, 1)
        ss.n += 1
        SS_EXPG = ss.n

        @block.vector
        def _(v):
            v.wait_ge(ss.h, SS_EXPG)
            v.reduce_sum(out=self.gsc[:, 2:3], in_=self.gates[:], axis=AX)
            v.drain()
            v.reciprocal(out=self.gsc[:, 3:4], in_=self.gsc[:, 2:3])
            v.drain()
            v.tensor_scalar(out=self.gates[:], in0=self.gates[:], scalar1=self.gsc[:, 3:4],
                            scalar2=None, op0=ALU.mult).then_inc(vs.h, 1)
        vs.n += 1
        VS_GATES = vs.n

        VS_SEL0 = vs.n

        @block.vector
        def _(v):
            v.wait_ge(vs.h, VS_GATES)
            v.reduce_max(out=self.gsc[:, 4:5], in_=self.gates[:], axis=AX)
            v.drain()
            v.tensor_scalar(out=self.lgt[:], in0=self.gates[:], scalar1=self.gsc[:, 4:5],
                            scalar2=None, op0=ALU.is_ge)
            v.drain()
            v.tensor_tensor(out=self.lgt[:], in0=self.lgt[:], in1=self.iota8[:], op=ALU.mult)
            v.drain()
            v.reduce_sum(out=self.gsc[:, 5:6], in_=self.lgt[:], axis=AX)
            v.drain()
            v.tensor_scalar(out=self.lgt[:], in0=self.gates[:], scalar1=self.gsc[:, 4:5],
                            scalar2=-2.0, op0=ALU.is_ge, op1=ALU.mult)
            v.drain()
            v.tensor_tensor(out=self.lgt[:], in0=self.lgt[:], in1=self.gates[:], op=ALU.add)
            v.drain()
            v.reduce_max(out=self.gsc[:, 6:7], in_=self.lgt[:], axis=AX)
            v.drain()
            v.tensor_scalar(out=self.lgt[:], in0=self.lgt[:], scalar1=self.gsc[:, 6:7],
                            scalar2=None, op0=ALU.is_ge)
            v.drain()
            v.tensor_tensor(out=self.lgt[:], in0=self.lgt[:], in1=self.iota8[:], op=ALU.mult)
            v.drain()
            v.reduce_sum(out=self.gsc[:, 7:8], in_=self.lgt[:], axis=AX)
            v.drain()
            v.tensor_copy(out=self.gates[:, 0:1], in_=self.gsc[:, 5:6])
            v.tensor_copy(out=self.gates[:, 1:2], in_=self.gsc[:, 7:8])
            v.tensor_copy(out=self.gates[:, 2:3], in_=self.gsc[:, 4:5])
            v.drain()
            v.nop().then_inc(vs.h, 1)
        vs.n += 1
        VS_SEL = vs.n

        @block.tensor
        def _(t_):
            t_.wait_ge(vs.h, VS_SEL)
            t_.matmul(out=ps_o[0][:, 0:3], lhsT=self.ones1f[:], rhs=self.gates[:, 0:3],
                      start=True, stop=True).then_inc(ts.h, 1)
        ts.n += 1
        TS_SELB = ts.n

        @block.vector
        def _(v):
            v.wait_ge(ts.h, TS_SELB)
            v.tensor_copy(out=self.gsel[:, 0:1], in_=ps_o[0][:, 2:3])
            v.tensor_scalar(out=self.gsel[:, 1:2], in0=ps_o[0][:, 2:3], scalar1=-1.0,
                            scalar2=1.0, op0=ALU.mult, op1=ALU.add)
            for s in range(2):
                v.tensor_scalar(out=self.idxwf[:, NKC * s:NKC * s + 1], in0=ps_o[0][:, s:s + 1],
                                scalar1=float(D), scalar2=None, op0=ALU.mult)
            v.drain()
            for s in range(2):
                v.tensor_tensor(out=self.idxwf[:, NKC * s:NKC * s + 1],
                                in0=self.idxwf[:, NKC * s:NKC * s + 1],
                                in1=self.iotap[:], op=ALU.add)
            v.drain()
            for s in range(2):
                for kc in range(1, NKC):
                    v.tensor_scalar(out=self.idxwf[:, NKC * s + kc:NKC * s + kc + 1],
                                    in0=self.idxwf[:, NKC * s:NKC * s + 1],
                                    scalar1=float(128 * kc), scalar2=None, op0=ALU.add)
            v.drain()
            v.tensor_copy(out=self.idxw[:], in_=self.idxwf[:])
            v.tensor_copy(out=self.idxb[:, 0:1], in_=ps_o[0][:, 0:1])
            v.tensor_copy(out=self.idxb[:, 1:2], in_=ps_o[0][:, 1:2])
            v.drain()
            v.nop().then_inc(vs.h, 1)
        vs.n += 1
        self.VS_IDX = vs.n
        self.pso_free = (self.vs, self.VS_IDX)

        if self.tap == "gates" and b == 0:
            self.tap_dump(self.gsel)

    def _ffn(self, b):
        block = self.block
        dx, dw, ts, vs, ss, so = self.dx, self.dw, self.ts, self.vs, self.ss, self.so
        ps_s, ps_o = self.ps_s, self.ps_o
        VS_IDX = self.VS_IDX
        TAO_END = self.TAO_END
        NS = 2

        T0 = ts.n
        V0 = vs.n
        S0 = ss.n
        DW0 = dw.n
        self.VF0 = V0
        self.FFN_VPE = 2 * NT

        tpe = NT
        vpe = 2 * NT
        spe = NT
        SO_BASE = self.so.n
        IOA = bass.IndirectOffsetOnAxis

        @block.gpsimd
        def _(gp):
            for s in range(NS):
                gp.wait_ge(ts.h, TAO_END)
                gp.wait_ge(vs.h, VS_IDX)
                if s == 1:
                    gp.wait_ge(vs.h, V0 + 2 * (NT - 1) + 1)
                for kc in range(NKC):
                    gp.indirect_dma_start(
                        out=self.wexp[s][:, D * kc:D * (kc + 1)], out_offset=None,
                        in_=self.weT_h[:, :],
                        in_offset=IOA(ap=self.idxw[:, NKC * s + kc:NKC * s + kc + 1], axis=0),
                    ).then_inc(dw.h, 16)
                gp.indirect_dma_start(
                    out=self.bbc[:], out_offset=None, in_=self.be_h[:, :],
                    in_offset=IOA(ap=self.idxb[:, s:s + 1], axis=0),
                ).then_inc(dw.h, 16)
        dw.n += NS * (NKC + 1) * 16

        @block.tensor
        def _(t_):
            for s in range(NS):
                t_.wait_ge(dw.h, DW0 + 16 * (NKC + 1) * (s + 1))
                if s == 0:
                    t_.wait_ge(vs.h, VS_IDX)
                for tc in range(NT):
                    lo = 1024 * (tc % 2)
                    if s == 0 and tc < 2:
                        pass
                    elif tc >= 2:
                        t_.wait_ge(vs.h, V0 + s * vpe + 2 * (tc - 2) + 1)
                    else:
                        t_.wait_ge(vs.h, V0 + (s - 1) * vpe + 2 * (NT - 2 + tc) + 1)
                    w = self.wexp[s]
                    for half in range(2):
                        for kc in range(NKC):
                            mm = t_.matmul(out=ps_s[:, lo + 512 * half:lo + 512 * half + 512],
                                           lhsT=self.xnT[:, S * kc + 128 * tc:S * kc + 128 * (tc + 1)],
                                           rhs=w[:, D * kc + 512 * half:D * kc + 512 * half + 512],
                                           start=(kc == 0), stop=(kc == NKC - 1))
                    mm.then_inc(ts.h, 1)
        ts.n += NS * tpe

        @block.vector
        def _(v):
            v.wait_ge(vs.h, VS_IDX)
            for s in range(NS):
                for tc in range(NT):
                    lo = 1024 * (tc % 2)
                    v.wait_ge(ts.h, T0 + s * tpe + tc + 1)
                    if tc == 0:
                        v.wait_ge(dw.h, DW0 + 16 * (NKC + 1) * (s + 1))
                        if s == 0 and SO_BASE > 0:
                            v.wait_ge(so.h, SO_BASE)
                    if tc >= 2:
                        v.wait_ge(ss.h, S0 + s * spe + (tc - 2) + 1)
                        if s == NS - 1:
                            v.wait_ge(so.h, SO_BASE + 16 * tc)
                    elif s > 0:
                        v.wait_ge(ss.h, S0 + (s - 1) * spe + (NT - 2 + tc) + 1)
                    v.tensor_tensor(out=self.ftmp[tc % 2][:, 0:512],
                                    in0=ps_s[:, lo:lo + 512], in1=self.bbc[:, 0:512], op=ALU.add)
                    v.tensor_tensor(out=self.ftmp[tc % 2][:, 512:1024],
                                    in0=ps_s[:, lo + 512:lo + 1024], in1=self.bbc[:, 512:1024],
                                    op=ALU.add).then_inc(vs.h, 1)
                    v.wait_ge(ss.h, S0 + s * spe + tc + 1)
                    if s == NS - 1:
                        v.scalar_tensor_tensor(out=self.ftmp[tc % 2][:],
                                               in0=self.ftmp[tc % 2][:], scalar=self.gsel[:, s:s + 1],
                                               in1=self.x2[:, D * tc:D * (tc + 1)],
                                               op0=ALU.mult, op1=ALU.add).then_inc(vs.h, 1)
                    else:
                        v.scalar_tensor_tensor(out=self.x2[:, D * tc:D * (tc + 1)],
                                               in0=self.ftmp[tc % 2][:], scalar=self.gsel[:, s:s + 1],
                                               in1=self.x2[:, D * tc:D * (tc + 1)],
                                               op0=ALU.mult, op1=ALU.add).then_inc(vs.h, 1)
        vs.n += NS * vpe

        @block.scalar
        def _(s_):
            for s in range(NS):
                for tc in range(NT):
                    s_.wait_ge(vs.h, V0 + s * vpe + 2 * tc + 1)
                    s_.activation(out=self.ftmp[tc % 2][:], in_=self.ftmp[tc % 2][:],
                                  func=AF.Gelu).then_inc(ss.h, 1)
        ss.n += NS * spe



def _consts():
    p = np.arange(128)[:, None].astype(np.float32)
    g = np.arange(2048)[None, :].astype(np.float32)
    d0mv = (p - g).astype(np.float32)
    f = np.arange(128)[None, :].astype(np.float32)
    g640v = np.where(p <= f, p - f, MASKVAL).astype(np.float32)
    r32 = np.float32(_RATIO)
    t0mv = (d0mv * r32).astype(np.float32)
    gt640v = (g640v * r32).astype(np.float32)
    idf16 = np.eye(128, dtype=np.float16)
    idbf = np.eye(128, dtype=ml_dtypes.bfloat16)
    return d0mv.astype(np.float16), g640v.astype(np.float16), t0mv, gt640v, idf16, idbf


def make_in_maps(L0_attn_x, L0_attn_q_w, L0_attn_k_w, L0_attn_v_w, L0_attn_o_w,
                 L0_ffn_gate_w, L0_ffn_gate_b, L0_ffn_expert_w, L0_ffn_expert_b):
    x = np.ascontiguousarray(np.asarray(L0_attn_x, dtype=np.float32))
    wq = np.ascontiguousarray(np.asarray(L0_attn_q_w, dtype=np.float32))
    wk = np.ascontiguousarray(np.asarray(L0_attn_k_w, dtype=np.float32))
    wv = np.ascontiguousarray(np.asarray(L0_attn_v_w, dtype=np.float32))
    wo = np.ascontiguousarray(np.asarray(L0_attn_o_w, dtype=np.float32))
    gwT = np.ascontiguousarray(np.asarray(L0_ffn_gate_w, dtype=np.float32).T)
    gb = np.ascontiguousarray(np.asarray(L0_ffn_gate_b, dtype=np.float32)[None, :])
    weT = np.ascontiguousarray(
        np.asarray(L0_ffn_expert_w, dtype=np.float32).transpose(0, 2, 1).reshape(E * D, D))
    be = np.ascontiguousarray(np.asarray(L0_ffn_expert_b, dtype=np.float32))
    iotap = np.arange(128, dtype=np.float32)[:, None]
    iota8 = np.arange(E, dtype=np.float32)[None, :]
    d0mv, g640v, t0mv, gt640v, idf16, idbf = _consts()
    in_maps = []
    for c in range(8):
        in_maps.append({
            "x": x[2 * c:2 * c + 2],
            "wq": wq, "wk": wk, "wv": wv, "wo": wo,
            "gwT": gwT, "gb": gb, "weT": weT, "be": be,
            "idf16": idf16, "idbf": idbf, "d0m": d0mv, "g640": g640v,
            "t0m": t0mv, "gt640": gt640v,
            "iotap": iotap, "iota8": iota8,
        })
    return in_maps


def kernel(L0_attn_x, L0_attn_q_w, L0_attn_k_w, L0_attn_v_w, L0_attn_o_w,
           L0_ffn_gate_w, L0_ffn_gate_b, L0_ffn_expert_w, L0_ffn_expert_b):
    in_maps = make_in_maps(L0_attn_x, L0_attn_q_w, L0_attn_k_w, L0_attn_v_w, L0_attn_o_w,
                           L0_ffn_gate_w, L0_ffn_gate_b, L0_ffn_expert_w, L0_ffn_expert_b)
    nc = Builder().build()
    res = run_bass_kernel_spmd(nc, in_maps, core_ids=list(range(8)))
    out = np.concatenate([res.results[c]["out"] for c in range(8)], axis=0)
    return np.ascontiguousarray(out.astype(np.float32))


# version 24
# speedup vs baseline: 1.0209x; 1.0209x over previous
import numpy as np
import ml_dtypes

import concourse.bass as bass
import concourse.mybir as mybir
from concourse.bass_utils import run_bass_kernel_spmd
from contextlib import ExitStack

F32 = mybir.dt.float32
BF16 = mybir.dt.bfloat16
F16 = mybir.dt.float16
AF = mybir.ActivationFunctionType
ALU = mybir.AluOpType
AX = mybir.AxisListType.X

D = 1024
S = 2048
NPAIR = 8
NT = 16
NKC = 8
E = 8
EPS = 1e-5
MASKVAL = -57344.0
_RATIO = 2.0 ** (-8.0 / 16)
SLOPES = [float(np.float32(_RATIO ** (h + 1))) for h in range(16)]


class Builder:
    def __init__(self, tap=None):
        self.tap = tap
        nc = bass.Bass("TRN2", target_bir_lowering=False, debug=False)
        self.nc = nc
        self.x_h = nc.dram_tensor("x", [2, S, D], F32, kind="ExternalInput").ap()
        self.wq_h = nc.dram_tensor("wq", [D, D], F32, kind="ExternalInput").ap()
        self.wk_h = nc.dram_tensor("wk", [D, D], F32, kind="ExternalInput").ap()
        self.wv_h = nc.dram_tensor("wv", [D, D], F32, kind="ExternalInput").ap()
        self.wo_h = nc.dram_tensor("wo", [D, D], F32, kind="ExternalInput").ap()
        self.gwT_h = nc.dram_tensor("gwT", [D, E], F32, kind="ExternalInput").ap()
        self.gb_h = nc.dram_tensor("gb", [1, E], F32, kind="ExternalInput").ap()
        self.weT_h = nc.dram_tensor("weT", [E * D, D], F32, kind="ExternalInput").ap()
        self.be_h = nc.dram_tensor("be", [E, D], F32, kind="ExternalInput").ap()
        self.iotap_h = nc.dram_tensor("iotap", [128, 1], F32, kind="ExternalInput").ap()
        self.iota8_h = nc.dram_tensor("iota8", [1, E], F32, kind="ExternalInput").ap()
        self.idf16_h = nc.dram_tensor("idf16", [128, 128], F16, kind="ExternalInput").ap()
        self.idbf_h = nc.dram_tensor("idbf", [128, 128], BF16, kind="ExternalInput").ap()
        self.d0m_h = nc.dram_tensor("d0m", [128, 2048], F16, kind="ExternalInput").ap()
        self.g640_h = nc.dram_tensor("g640", [128, 128], F16, kind="ExternalInput").ap()
        self.t0m_h = nc.dram_tensor("t0m", [128, 2048], F32, kind="ExternalInput").ap()
        self.gt640_h = nc.dram_tensor("gt640", [128, 128], F32, kind="ExternalInput").ap()
        self.out_h = nc.dram_tensor("out", [2, S, D], F32, kind="ExternalOutput").ap()
        if tap is not None:
            self.dbg_h = nc.dram_tensor("dbg", [128, 16384], F32, kind="ExternalOutput").ap()

    def build(self):
        nc = self.nc
        with ExitStack() as st_:
            e = st_.enter_context
            self.idf16 = e(nc.sbuf_tensor("idf16s", [128, 128], F16))
            self.d0m = e(nc.sbuf_tensor("d0ms", [128, 2048], F16))
            self.g640 = e(nc.sbuf_tensor("g640s", [128, 128], F16))
            self.t0m = e(nc.sbuf_tensor("t0ms", [128, 2048], F32))
            self.gt640 = e(nc.sbuf_tensor("gt640s", [128, 128], F32))
            self.gwT = e(nc.sbuf_tensor("gwTs", [128, NKC * E], F32))
            self.gb = e(nc.sbuf_tensor("gbs", [1, E], F32))
            self.iotap = e(nc.sbuf_tensor("iotap_s", [128, 1], F32))
            self.iota8 = e(nc.sbuf_tensor("iota8_s", [1, E], F32))
            self.onesbf = e(nc.sbuf_tensor("onesbf", [128, 1], F16))
            self.ones1f = e(nc.sbuf_tensor("ones1f", [1, 128], F32))
            self.one11 = e(nc.sbuf_tensor("one11", [1, 1], F32))
            self.xnT = e(nc.sbuf_tensor("xnT", [128, NKC * S], F16))
            self.x2 = e(nc.sbuf_tensor("x2", [128, NT * D], F32))
            self.ostg = [e(nc.sbuf_tensor(f"ostg{i}", [128, 128], F16)) for i in range(2)]
            self.wpair = [[e(nc.sbuf_tensor(f"wp{i}{m}", [128, NKC * 128], F16))
                           for m in range(3)] for i in range(2)]
            self.wbuf = e(nc.sbuf_tensor("wbuf", [128, NKC * D], F16))
            self.wbuf2 = e(nc.sbuf_tensor("wbuf2", [128, NKC * D], F16))
            self.qt = e(nc.sbuf_tensor("qts", [128, S], F16))
            self.kt = e(nc.sbuf_tensor("kts", [128, S], F16))
            self.vp = e(nc.sbuf_tensor("vps", [128, NT * 130], F16))
            self.spp = [[e(nc.sbuf_tensor(f"spp{par}{h}", [128, 2048], F16)) for h in range(2)]
                        for par in range(2)]
            self.ftmp = [e(nc.sbuf_tensor(f"ftmp{i}", [128, D], F32)) for i in range(2)]
            self.pT = [e(nc.sbuf_tensor(f"pT{h}", [128, 2048], F16)) for h in range(2)]
            self.recip = e(nc.sbuf_tensor("recips", [128, 4], F32))
            self.xtile = self.ftmp
            self.bnst = e(nc.sbuf_tensor("bnst", [128, 12 * NT], F32))
            self.mvall = e(nc.sbuf_tensor("mvall", [128, 2 * NT], F32))
            self.rstd = e(nc.sbuf_tensor("rstd", [128, NT], F32))
            self.negmr = e(nc.sbuf_tensor("negmr", [128, NT], F32))
            self.lntmp = e(nc.sbuf_tensor("lntmp", [128, NT], F32))
            self.xntm = e(nc.sbuf_tensor("xntm", [128, D], F16))
            self.pooledT = e(nc.sbuf_tensor("pooledT", [128, 8], F32))
            self.lgt = e(nc.sbuf_tensor("lgt", [1, E], F32))
            self.gsc = e(nc.sbuf_tensor("gsc", [1, 8], F32))
            self.gates = e(nc.sbuf_tensor("gates", [1, E], F32))
            self.gsel = e(nc.sbuf_tensor("gsel", [128, 2], F32))
            self.idxw = e(nc.sbuf_tensor("idxw", [128, 2 * NKC], mybir.dt.uint32))
            self.idxwf = e(nc.sbuf_tensor("idxwf", [128, 2 * NKC], F32))
            self.idxb = e(nc.sbuf_tensor("idxb", [128, 2], mybir.dt.uint32))
            self.wexp = [self.wbuf, self.wbuf2]
            self.bbc = e(nc.sbuf_tensor("bbc", [128, D], F16))
            self.pooled_sb = self.ftmp[0][0:1, 0:D]
            self.ps_s = e(nc.psum_tensor("ps_s", [128, 2048], F32))
            self.ps_t = [e(nc.psum_tensor(f"ps_t{h}", [128, 1024], F16)) for h in range(2)]
            self.ps_o = [e(nc.psum_tensor(f"ps_o{h}", [128, 512], F32)) for h in range(2)]

            class Led:
                def __init__(s, name, dma=False):
                    s.h = e(nc.semaphore(name))
                    s.n = 0
                    s.step = 16 if dma else 1

            self.dx = Led("dx", dma=True)
            self.dw = Led("dw", dma=True)
            self.ts = Led("ts")
            self.vs = Led("vs")
            self.ss = Led("ss")
            self.so = Led("so", dma=True)
            self.dxx = Led("dxx", dma=True)

            self.block = e(nc.Block())
            self.pso_free = (self.vs, 0)
            self.exp_last = [0, 0]
            self.v_last_stt = 0
            self.s_last_oevac = 0
            self.done = False
            self.xload_base = {}

            self._consts()
            for b in range(2):
                self._seq(b)
                if self.done:
                    break

            SO_END = self.so.n

            @self.block.sync
            def _(sync):
                sync.wait_ge(self.so.h, SO_END)
        return nc

    def ot_ap(self, pp, cols):
        base = self.wbuf if pp < 4 else self.wbuf2
        off = 2048 * (pp % 4)
        return base[:, off + cols[0]:off + cols[1]]

    def tap_dump(self, sbuf_ap, nrows=128):
        dbg = self.dbg_h
        vsn, ssn, tsn = self.vs.n, self.ss.n, self.ts.n
        dxn = self.dx.n
        fs = min(sbuf_ap.shape[-1], 16384)

        @self.block.gpsimd
        def _(gp):
            gp.wait_ge(self.vs.h, vsn)
            gp.wait_ge(self.ss.h, ssn)
            gp.wait_ge(self.ts.h, tsn)
            gp.dma_start(out=dbg[0:nrows, 0:fs], in_=sbuf_ap[0:nrows, 0:fs]).then_inc(self.so.h, 16)
        self.so.n += 16
        self.done = True

    def _consts(self):
        dx, vs = self.dx, self.vs

        @self.block.sync
        def _(sync):
            sync.dma_start(out=self.idf16[:], in_=self.idf16_h[:, :]).then_inc(dx.h, 16)
            sync.dma_start(out=self.d0m[:], in_=self.d0m_h[:, :]).then_inc(dx.h, 16)
            sync.dma_start(out=self.g640[:], in_=self.g640_h[:, :]).then_inc(dx.h, 16)
            sync.dma_start(out=self.t0m[:], in_=self.t0m_h[:, :]).then_inc(dx.h, 16)
            sync.dma_start(out=self.gt640[:], in_=self.gt640_h[:, :]).then_inc(dx.h, 16)
            sync.dma_start(out=self.gwT[:].rearrange("p (kc e) -> p kc e", kc=NKC),
                           in_=self.gwT_h.rearrange("(kc p) e -> p kc e", p=128)).then_inc(dx.h, 16)
            sync.dma_start(out=self.gb[:], in_=self.gb_h[:, :]).then_inc(dx.h, 16)
            sync.dma_start(out=self.iotap[:], in_=self.iotap_h[:, :]).then_inc(dx.h, 16)
            sync.dma_start(out=self.iota8[:], in_=self.iota8_h[:, :]).then_inc(dx.h, 16)
        dx.n += 9 * 16
        self.CONST_DX = dx.n

        CONST_DX = dx.n

        @self.block.vector
        def _(v):
            v.wait_ge(dx.h, CONST_DX)
            v.memset(self.onesbf[:], 1.0)
            v.memset(self.ones1f[:], 1.0)
            v.memset(self.one11[:], 1.0)
            vpv = self.vp[:].rearrange("p (t c) -> p t c", c=130)
            v.memset(vpv[:, :, 64:65], 1.0)
            m = v.memset(vpv[:, :, 129:130], 1.0)
            m.then_inc(vs.h, 1)
        vs.n += 1
        self.VS_CONST = vs.n

    def _seq(self, b):
        nc, block = self.nc, self.block
        dx, dw, ts, vs, ss, so = self.dx, self.dw, self.ts, self.vs, self.ss, self.so
        wbig = lambda w: w.rearrange("(kc p) d -> p (kc d)", p=128)

        TS_B0, VS_B0, SS_B0, SO_B0 = ts.n, vs.n, ss.n, so.n

        self.TS_B0 = TS_B0
        DW_QKV = 0

        VA0 = vs.n
        SA0 = ss.n
        pst0 = self.ps_t[0]

        dxx = self.dxx
        if b == 0:
            XB = dxx.n

            @block.sync
            def _(sync):
                for t in range(NT):
                    if t >= 4:
                        sync.wait_ge(vs.h, VA0 + (t - 3))
                    sync.dma_start(out=self.x2[:, D * t:D * (t + 1)],
                                   in_=self.x_h[b, 128 * t:128 * (t + 1), :]).then_inc(dxx.h, 16)
            dxx.n += NT * 16
            self.xload_base[0] = XB
        XB = self.xload_base[b]

        @block.vector
        def _(v):
            for t in range(NT):
                if b == 0:
                    v.wait_ge(dxx.h, XB + 16 * min(t + 4, NT))
                elif t == 0:
                    v.wait_ge(dxx.h, XB + 16 * NT)
                xt = self.x2[:, D * t:D * (t + 1)]
                v.bn_stats(out=self.bnst[:, 12 * t:12 * t + 6], in_=xt[:, 0:512])
                v.bn_stats(out=self.bnst[:, 12 * t + 6:12 * t + 12], in_=xt[:, 512:1024]).then_inc(vs.h, 1)
            v.drain()
            for t in range(NT):
                v.bn_aggr(out=self.mvall[:, 2 * t:2 * t + 2], in_=self.bnst[:, 12 * t:12 * t + 12])
            v.drain()
            v.tensor_scalar(out=self.lntmp[:], in0=self.mvall[:, 1:2 * NT:2],
                            scalar1=EPS, scalar2=None, op0=ALU.add)
            v.drain()
            v.reciprocal(out=self.rstd[:], in_=self.lntmp[:]).then_inc(vs.h, 1)
        vs.n += NT + 1
        VA_END = vs.n

        @block.scalar
        def _(s):
            s.wait_ge(vs.h, VA_END)
            s.activation(out=self.rstd[:], in_=self.rstd[:], func=AF.Sqrt).then_inc(ss.h, 1)
        ss.n += 1
        SS_RSTD = ss.n

        @block.vector
        def _(v):
            v.wait_ge(ss.h, SS_RSTD)
            v.scalar_tensor_tensor(out=self.negmr[:], in0=self.mvall[:, 0:2 * NT:2],
                                   scalar=-1.0, in1=self.rstd[:],
                                   op0=ALU.mult, op1=ALU.mult).then_inc(vs.h, 1)
        vs.n += 1
        VS_NEGMR = vs.n

        SB0 = ss.n
        TB0 = ts.n
        pst0 = self.ps_t[0]

        @block.scalar
        def _(s):
            for t in range(NT):
                if t == 0:
                    s.wait_ge(vs.h, VS_NEGMR)
                    s.wait_ge(ts.h, TS_B0)
                s.activation(out=self.xntm[:], in_=self.x2[:, D * t:D * (t + 1)], func=AF.Identity,
                             scale=self.rstd[:, t:t + 1], bias=self.negmr[:, t:t + 1]).then_inc(ss.h, 1)
                s.wait_ge(ts.h, TB0 + t + 1)
                dst = self.xnT[:].rearrange("p (kc s) -> p kc s", kc=NKC)[:, :, 128 * t:128 * (t + 1)]
                src = pst0[:, 0:1024].rearrange("p (kc f) -> p kc f", kc=NKC)
                s.copy(out=dst, in_=src).then_inc(ss.h, 1)
        ss.n += 2 * NT
        SB_END = ss.n

        @block.tensor
        def _(t_):
            for t in range(NT):
                t_.wait_ge(ss.h, SB0 + 2 * t + 1)
                if t >= 1:
                    t_.wait_ge(ss.h, SB0 + 2 * t)
                for kc in range(NKC):
                    mm = t_.transpose(out=pst0[:, 128 * kc:128 * (kc + 1)],
                                      in_=self.xntm[:, 128 * kc:128 * (kc + 1)],
                                      identity=self.idf16[:])
                mm.then_inc(ts.h, 1)
        ts.n += NT
        TS_LN1_END = ts.n

        if self.tap == "xnT" and b == 0:
            self.tap_dump(self.xnT)
            return

        self._attention(b, DW_QKV, SB_END, TS_LN1_END)
        if self.done:
            return
        self._oproj(b, SO_B0)
        if self.done:
            return
        self._ln2_gates(b)
        if self.done:
            return
        self._ffn(b)
        if self.done:
            return

        VF0 = self.VF0
        SO_BASE = so.n
        XBN = self.dxx.n
        dxx = self.dxx

        @block.sync
        def _(sync):
            for t in range(NT):
                sync.wait_ge(vs.h, VF0 + 1 * (2 * NT) + 2 * t + 2)
                sync.dma_start(out=self.out_h[b, 128 * t:128 * (t + 1), :],
                               in_=self.x2[:, D * t:D * (t + 1)]).then_inc(so.h, 16)
                if b == 0:
                    sync.wait_ge(so.h, SO_BASE + 16 * (t + 1))
                    sync.dma_start(out=self.x2[:, D * t:D * (t + 1)],
                                   in_=self.x_h[1, 128 * t:128 * (t + 1), :]).then_inc(dxx.h, 16)
        so.n += NT * 16
        if b == 0:
            dxx.n += NT * 16
            self.xload_base[1] = XBN

    def _attention(self, b, DW_QKV, SS_LN1_END, TS_LN1_END):
        block = self.block
        dx, dw, ts, vs, ss = self.dx, self.dw, self.ts, self.vs, self.ss
        ps_s, ps_t, ps_o = self.ps_s, self.ps_t, self.ps_o
        qt, kt, vp, spp, pT = self.qt, self.kt, self.vp, self.spp, self.pT

        if not hasattr(self, "proj_end_marks"):
            self.proj_end_marks = {}
        wsrc = [self.wq_h, self.wk_h, self.wv_h]
        DWB = dw.n
        for pp in range(NPAIR):
            prev_mark = self.proj_end_marks.get(pp - 2, self.TS_B0)

            @block.gpsimd
            def _(gp):
                gp.wait_ge(ts.h, prev_mark)
                for m in range(3):
                    src_ap = wsrc[m][:, 128 * pp:128 * (pp + 1)].rearrange(
                        "(kc p) d -> p kc d", p=128)
                    dst_ap = self.wpair[pp % 2][m][:].rearrange("p (kc d) -> p kc d", kc=NKC)
                    gp.dma_start(out=dst_ap, in_=src_ap).then_inc(dw.h, 16)
            dw.n += 48

            TP0 = ts.n
            SP0 = ss.n
            V_LAST_STT = self.v_last_stt
            PSO_FREE = self.pso_free
            S_LAST_OEVAC = self.s_last_oevac
            DW_PAIR = DWB + 48 * min(pp + 2, NPAIR)
            wp = self.wpair[pp % 2]

            @block.tensor
            def _(t_):
                t_.wait_ge(dw.h, DW_PAIR)
                if pp == 0:
                    t_.wait_ge(ss.h, SS_LN1_END)
                for i in range(8):
                    wmat = wp[0] if i < 4 else wp[1]
                    tc4 = i % 4
                    if i >= 4:
                        t_.wait_ge(ss.h, SP0 + (i - 4) + 1)
                    elif pp > 0:
                        t_.wait_ge(vs.h, V_LAST_STT)
                    bank = ps_s[:, 512 * tc4:512 * (tc4 + 1)]
                    for kc in range(NKC):
                        mm = t_.matmul(out=bank,
                                       lhsT=wmat[:, 128 * kc:128 * (kc + 1)],
                                       rhs=self.xnT[:, S * kc + 512 * tc4:S * kc + 512 * (tc4 + 1)],
                                       start=(kc == 0), stop=(kc == NKC - 1))
                    mm.then_inc(ts.h, 1)
                for tc in range(NT):
                    if tc >= 2:
                        t_.wait_ge(ss.h, SP0 + 8 + (tc - 2) + 1)
                    else:
                        led, cnt = PSO_FREE
                        t_.wait_ge(led.h, cnt)
                        t_.wait_ge(ss.h, S_LAST_OEVAC)
                        if pp == 0:
                            t_.wait_ge(vs.h, self.VS_CONST)
                    bank = ps_o[tc % 2][:, 0:128]
                    for kc in range(NKC):
                        mm = t_.matmul(out=bank,
                                       lhsT=self.xnT[:, S * kc + 128 * tc:S * kc + 128 * (tc + 1)],
                                       rhs=wp[2][:, 128 * kc:128 * (kc + 1)],
                                       start=(kc == 0), stop=(kc == NKC - 1))
                    mm.then_inc(ts.h, 1)
            ts.n += 24
            self.proj_end_marks[pp] = ts.n

            @block.scalar
            def _(s):
                for i in range(8):
                    s.wait_ge(ts.h, TP0 + i + 1)
                    dst = (qt if i < 4 else kt)[:, 512 * (i % 4):512 * (i % 4 + 1)]
                    if i < 4:
                        s.activation(out=dst, in_=ps_s[:, 512 * (i % 4):512 * (i % 4) + 512],
                                     func=AF.Copy, scale=0.125).then_inc(ss.h, 1)
                    else:
                        s.copy(out=dst, in_=ps_s[:, 512 * (i % 4):512 * (i % 4) + 512]).then_inc(ss.h, 1)
                for tc in range(NT):
                    s.wait_ge(ts.h, TP0 + 8 + tc + 1)
                    src = ps_o[tc % 2][:, 0:128].rearrange("p (h f) -> p h f", h=2)
                    dst = vp[:, 130 * tc:130 * (tc + 1)].rearrange("p (h f) -> p h f", f=65)[:, :, 0:64]
                    s.copy(out=dst, in_=src).then_inc(ss.h, 1)
            ss.n += 24
            SS_PROJ_END = ss.n

            slope_min = SLOPES[2 * pp + 1]
            wb = int(np.ceil((80.0 / slope_min) / 128.0)) + 1
            nchs, nblks, blos, tailbs = [], [], [], []
            for ti in range(NT):
                b_lo = max(0, (ti + 1) - wb)
                nblk_eff = (ti + 1) - b_lo
                tailb = ((nblk_eff - 1) % 4) + 1
                nf = (nblk_eff - tailb) // 4
                blos.append(b_lo)
                nchs.append(nf + 1)
                nblks.append(nblk_eff)
                tailbs.append(tailb)

            def group_cols(ti, b):
                blo, nf, tailb = blos[ti], nchs[ti] - 1, tailbs[ti]
                Bt = blo + 4 * nf
                if b >= Bt:
                    return 512 * nf + 128 * (ti - b)
                c = (b - blo) // 4
                g_hi = blo + 4 * c + 3
                return 512 * c + 128 * (g_hi - b)

            t = ts.n
            t_qk, t_pv, t_otr = {}, {}, {}
            t_qk[0] = t; t += nchs[0]
            for ti in range(NT):
                if ti + 1 < NT:
                    t_qk[ti + 1] = t; t += nchs[ti + 1]
                if ti >= 1:
                    t_otr[ti - 1] = t; t += 1
                t_pv[ti] = t; t += 2
            t_otr[NT - 1] = t; t += 1
            TS_END = t

            v = vs.n
            v_stt, v_rec, v_evac = {}, {}, {}
            v_stt[0] = v; v += 2 * nchs[0]
            for ti in range(NT):
                if ti >= 1:
                    v_evac[ti - 1] = v; v += 2
                if ti + 1 < NT:
                    v_stt[ti + 1] = v; v += 2 * nchs[ti + 1]
                v_rec[ti] = v; v += 2
            v_evac[NT - 1] = v; v += 2
            VS_END = v

            s = ss.n
            s_exp, s_otev = {}, {}
            for ti in range(NT):
                if ti >= 1:
                    s_otev[ti - 1] = s; s += 1
                s_exp[ti] = s; s += 2
            s_otev[NT - 1] = s; s += 1
            SS_END = s

            qk_wait, qk_pair = {}, {}
            if not hasattr(self, "sbank_mark2"):
                self.sbank_mark2 = [0, 0]
            sbm = list(self.sbank_mark2)
            gc_ = getattr(self, "gchunk", 0)
            for ti in range(NT):
                for c in range(nchs[ti]):
                    pr = gc_ % 2
                    qk_pair[(ti, c)] = pr
                    qk_wait[(ti, c)] = sbm[pr]
                    sbm[pr] = min(v_stt[ti] + 2 * c + 3, v_stt[ti] + 2 * nchs[ti])
                    gc_ += 1
            self.sbank_mark2 = sbm
            self.gchunk = gc_

            if not hasattr(self, "pso_free_v"):
                self.pso_free_v = [0, 0]
            PSO_FREE_V = list(self.pso_free_v)
            OTR_PREV = getattr(self, "otr_prev", 0)
            OTEV_PREV = getattr(self, "otev_prev", 0)
            qk_ss_seen = [0]

            def emit_qk(t_, ti):
                blo, nf, tailb, nch = blos[ti], nchs[ti] - 1, tailbs[ti], nchs[ti]
                for c in range(nch):
                    t_.wait_ge(vs.h, qk_wait[(ti, c)])
                    bank_free = SP0 + 8
                    qt_need = SP0 + (128 * ti) // 512 + 1
                    if c < nf:
                        bs = list(range(blo + 4 * c, blo + 4 * c + 4))
                    else:
                        bs = list(range(blo + 4 * nf, ti + 1))
                    kt_need = SP0 + 4 + (128 * (bs[-1] + 1) - 1) // 512 + 1
                    ssw = max(bank_free, qt_need, kt_need)
                    if ssw > qk_ss_seen[0]:
                        t_.wait_ge(ss.h, ssw)
                        qk_ss_seen[0] = ssw
                    lo = 1024 * qk_pair[(ti, c)]
                    for b in bs:
                        colb = group_cols(ti, b) - 512 * c
                        mm = t_.matmul(out=ps_s[:, lo + colb:lo + colb + 128],
                                       lhsT=kt[0:64, 128 * b:128 * (b + 1)],
                                       rhs=qt[0:64, 128 * ti:128 * (ti + 1)],
                                       start=True, stop=True, tile_position=(0, 0))
                        mm = t_.matmul(out=ps_s[:, lo + 512 + colb:lo + 512 + colb + 128],
                                       lhsT=kt[64:128, 128 * b:128 * (b + 1)],
                                       rhs=qt[64:128, 128 * ti:128 * (ti + 1)],
                                       start=True, stop=True, tile_position=(64, 0))
                    mm.then_inc(ts.h, 1)

            @block.tensor
            def _(t_):
                emit_qk(t_, 0)
                for ti in range(NT):
                    if ti + 1 < NT:
                        emit_qk(t_, ti + 1)
                    if ti >= 1:
                        t_.wait_ge(vs.h, v_evac[ti - 1] + 2)
                        pfree = (s_otev[ti - 2] + 1) if ti >= 2 else OTEV_PREV
                        if pfree > 0:
                            t_.wait_ge(ss.h, pfree)
                        t_.transpose(out=self.ps_t[0][:, 0:128], in_=self.ostg[(ti - 1) % 2][:],
                                     identity=self.idf16[:]).then_inc(ts.h, 1)
                    nblk, blo = nblks[ti], blos[ti]
                    for hh in range(2):
                        t_.wait_ge(ss.h, s_exp[ti] + hh + 1)
                        t_.wait_ge(vs.h, PSO_FREE_V[hh])
                        pso = ps_o[hh]
                        for bb in range(nblk):
                            ab = bb + blo
                            col = group_cols(ti, ab)
                            mm = t_.matmul(out=pso[:, 0:65],
                                           lhsT=pT[hh][:, col:col + 128],
                                           rhs=vp[:, 130 * ab + 65 * hh:130 * ab + 65 * hh + 65],
                                           start=(bb == 0), stop=(bb == nblk - 1))
                        mm.then_inc(ts.h, 1)
                    PSO_FREE_V[0] = v_evac[ti] + 1
                    PSO_FREE_V[1] = v_evac[ti] + 2
                t_.wait_ge(vs.h, v_evac[NT - 1] + 2)
                t_.wait_ge(ss.h, s_otev[NT - 2] + 1)
                t_.transpose(out=self.ps_t[0][:, 0:128], in_=self.ostg[(NT - 1) % 2][:],
                             identity=self.idf16[:]).then_inc(ts.h, 1)
            ts.n = TS_END
            self.pso_free_v = list(PSO_FREE_V)

            @block.vector
            def _(v_):
                def emit_stt(ti):
                    blo, nf, tailb, nch = blos[ti], nchs[ti] - 1, tailbs[ti], nchs[ti]
                    for c in range(nch):
                        v_.wait_ge(ts.h, t_qk[ti] + c + 1)
                        lo = 1024 * qk_pair[(ti, c)]
                        sppb = 512 * c
                        for hh in range(2):
                            h = 2 * pp + hh
                            if h % 2 == 1:
                                tabf, tabg = self.d0m, self.g640
                                sc = 2.0 ** (-((h - 1) // 2 + 1))
                            else:
                                tabf, tabg = self.t0m, self.gt640
                                sc = 2.0 ** (-(h // 2))
                            if c < nf:
                                g_hi = blo + 4 * c + 3
                                u0 = 128 * (ti - g_hi)
                                m = v_.scalar_tensor_tensor(out=spp[ti % 2][hh][:, sppb:sppb + 512],
                                                            in0=tabf[:, u0:u0 + 512], scalar=sc,
                                                            in1=ps_s[:, lo + 512 * hh:lo + 512 * hh + 512],
                                                            op0=ALU.mult, op1=ALU.add)
                            else:
                                m = v_.scalar_tensor_tensor(out=spp[ti % 2][hh][:, sppb:sppb + 128],
                                                            in0=tabg[:], scalar=sc,
                                                            in1=ps_s[:, lo + 512 * hh:lo + 512 * hh + 128],
                                                            op0=ALU.mult, op1=ALU.add)
                                if tailb > 1:
                                    w = 128 * (tailb - 1)
                                    m = v_.scalar_tensor_tensor(out=spp[ti % 2][hh][:, sppb + 128:sppb + 128 + w],
                                                                in0=tabf[:, 128:128 + w], scalar=sc,
                                                                in1=ps_s[:, lo + 512 * hh + 128:lo + 512 * hh + 128 + w],
                                                                op0=ALU.mult, op1=ALU.add)
                            m.then_inc(vs.h, 1)

                def emit_evac(ti):
                    v_.wait_ge(vs.h, v_rec[ti] + 2)
                    if ti >= 2:
                        v_.wait_ge(ts.h, t_otr[ti - 2] + 1)
                    else:
                        v_.wait_ge(ts.h, OTR_PREV)
                    for hh in range(2):
                        v_.tensor_scalar(out=self.ostg[ti % 2][:, 64 * hh:64 * hh + 64],
                                         in0=ps_o[hh][:, 0:64],
                                         scalar1=self.recip[:, 2 * (ti % 2) + hh:2 * (ti % 2) + hh + 1],
                                         scalar2=None, op0=ALU.mult).then_inc(vs.h, 1)

                def emit_rec(ti):
                    for hh in range(2):
                        v_.wait_ge(ts.h, t_pv[ti] + hh + 1)
                        v_.reciprocal(out=self.recip[:, 2 * (ti % 2) + hh:2 * (ti % 2) + hh + 1],
                                      in_=ps_o[hh][:, 64:65]).then_inc(vs.h, 1)

                emit_stt(0)
                for ti in range(NT):
                    if ti >= 1:
                        emit_evac(ti - 1)
                    if ti + 1 < NT:
                        emit_stt(ti + 1)
                    emit_rec(ti)
                emit_evac(NT - 1)
            vs.n = VS_END
            self.v_last_stt = v_stt[NT - 1] + 2 * nchs[NT - 1]
            self.otr_prev = t_otr[NT - 1] + 1

            @block.scalar
            def _(s_):
                for ti in range(NT):
                    if ti >= 1:
                        s_.wait_ge(ts.h, t_otr[ti - 1] + 1)
                        s_.copy(out=self.ot_ap(pp, (128 * (ti - 1), 128 * ti)),
                                in_=self.ps_t[0][:, 0:128]).then_inc(ss.h, 1)
                    for hh in range(2):
                        s_.wait_ge(vs.h, v_stt[ti] + 2 * (nchs[ti] - 1) + hh + 1)
                        s_.activation(out=pT[hh][:, 0:128 * nblks[ti]],
                                      in_=spp[ti % 2][hh][:, 0:128 * nblks[ti]],
                                      func=AF.Exp).then_inc(ss.h, 1)
                s_.wait_ge(ts.h, t_otr[NT - 1] + 1)
                s_.copy(out=self.ot_ap(pp, (128 * (NT - 1), 128 * NT)),
                        in_=self.ps_t[0][:, 0:128]).then_inc(ss.h, 1)
            ss.n = SS_END
            self.s_last_oevac = SS_END
            self.otev_prev = s_otev[NT - 1] + 1
            self.pso_free = (self.vs, v_evac[NT - 1] + 2)
            self.spp_free = [s_exp[14] + 2, s_exp[15] + 2]

        if self.tap == "obuf" and b == 0:
            self.tap_dump(self.wbuf)
        if self.tap == "obuf2" and b == 0:
            self.tap_dump(self.wbuf2)

    def _oproj(self, b, SO_B0):
        block = self.block
        dx, dw, ts, vs, ss = self.dx, self.dw, self.ts, self.vs, self.ss
        ps_s, ps_o = self.ps_s, self.ps_o

        TS_ATTN_END = ts.n
        SS_ATTN_END = ss.n
        V_LAST_STT = self.v_last_stt
        wo_parts = [self.spp[0][0], self.spp[0][1], self.spp[1][0], self.spp[1][1]]

        SPP_FREE = self.spp_free

        @block.gpsimd
        def _(gp):
            for q in range(4):
                gp.wait_ge(ss.h, SPP_FREE[0] if q < 2 else SPP_FREE[1])
                srcw = self.wo_h[256 * q:256 * (q + 1), :].rearrange("(kc p) d -> p kc d", p=128)
                dstw = wo_parts[q][:].rearrange("p (kc d) -> p kc d", kc=2)
                gp.dma_start(out=dstw, in_=srcw).then_inc(dw.h, 16)
        dw.n += 64
        DW_O = dw.n

        def wo_ap(kc, cols):
            part = wo_parts[kc // 2]
            off = D * (kc % 2)
            return part[:, off + cols[0]:off + cols[1]]

        VC0 = vs.n

        TAO0 = ts.n

        @block.tensor
        def _(t_):
            t_.wait_ge(dw.h, DW_O)
            t_.wait_ge(vs.h, V_LAST_STT)
            t_.wait_ge(ss.h, SS_ATTN_END)
            for tc in range(NT):
                lo = 1024 * (tc % 2)
                if tc >= 2:
                    t_.wait_ge(vs.h, VC0 + (tc - 2) + 1)
                for half in range(2):
                    for kc in range(NKC):
                        mm = t_.matmul(out=ps_s[:, lo + 512 * half:lo + 512 * half + 512],
                                       lhsT=self.ot_ap(kc, (128 * tc, 128 * (tc + 1))),
                                       rhs=wo_ap(kc, (512 * half, 512 * half + 512)),
                                       start=(kc == 0), stop=(kc == NKC - 1))
                mm.then_inc(ts.h, 1)
        ts.n += NT

        @block.vector
        def _(v):
            for tc in range(NT):
                v.wait_ge(ts.h, TAO0 + tc + 1)
                lo = 1024 * (tc % 2)
                v.tensor_tensor(out=self.x2[:, D * tc:D * tc + 512],
                                in0=ps_s[:, lo:lo + 512],
                                in1=self.x2[:, D * tc:D * tc + 512], op=ALU.add)
                v.tensor_tensor(out=self.x2[:, D * tc + 512:D * tc + 1024],
                                in0=ps_s[:, lo + 512:lo + 1024],
                                in1=self.x2[:, D * tc + 512:D * tc + 1024], op=ALU.add).then_inc(vs.h, 1)
        vs.n += NT
        self.VC_END = vs.n
        self.TAO_END = ts.n

        if self.tap == "x2attn" and b == 0:
            self.tap_dump(self.x2)

    def _ln2_gates(self, b):
        block = self.block
        dx, dw, ts, vs, ss = self.dx, self.dw, self.ts, self.vs, self.ss
        ps_s, ps_o = self.ps_s, self.ps_o
        pst0 = self.ps_t[0]
        VC_END = self.VC_END

        VD0 = vs.n

        VC0 = VC_END - NT

        @block.vector
        def _(v):
            for t in range(NT):
                v.wait_ge(vs.h, VC0 + t + 1)
                xt = self.x2[:, D * t:D * (t + 1)]
                v.bn_stats(out=self.bnst[:, 12 * t:12 * t + 6], in_=xt[:, 0:512])
                v.bn_stats(out=self.bnst[:, 12 * t + 6:12 * t + 12], in_=xt[:, 512:1024]).then_inc(vs.h, 1)
            v.drain()
            for t in range(NT):
                v.bn_aggr(out=self.mvall[:, 2 * t:2 * t + 2], in_=self.bnst[:, 12 * t:12 * t + 12])
            v.drain()
            v.tensor_scalar(out=self.lntmp[:], in0=self.mvall[:, 1:2 * NT:2],
                            scalar1=EPS, scalar2=None, op0=ALU.add)
            v.drain()
            v.reciprocal(out=self.rstd[:], in_=self.lntmp[:]).then_inc(vs.h, 1)
        vs.n += NT + 1
        VD_STATS = vs.n

        @block.scalar
        def _(s):
            s.wait_ge(vs.h, VD_STATS)
            s.activation(out=self.rstd[:], in_=self.rstd[:], func=AF.Sqrt).then_inc(ss.h, 1)
        ss.n += 1
        SS_RSTD2 = ss.n

        @block.vector
        def _(v):
            v.wait_ge(ss.h, SS_RSTD2)
            v.scalar_tensor_tensor(out=self.negmr[:], in0=self.mvall[:, 0:2 * NT:2],
                                   scalar=-1.0, in1=self.rstd[:],
                                   op0=ALU.mult, op1=ALU.mult).then_inc(vs.h, 1)
        vs.n += 1
        VD_NEGMR = vs.n

        SD0 = ss.n
        TD0 = ts.n
        S_LAST_OEVAC = self.s_last_oevac

        TAO_END = self.TAO_END

        @block.scalar
        def _(s):
            for t in range(NT):
                if t == 0:
                    s.wait_ge(vs.h, VD_NEGMR)
                    s.wait_ge(ts.h, TAO_END)
                else:
                    s.wait_ge(ts.h, TD0 + 2 * t)
                s.activation(out=self.xntm[:], in_=self.x2[:, D * t:D * (t + 1)], func=AF.Identity,
                             scale=self.rstd[:, t:t + 1], bias=self.negmr[:, t:t + 1]).then_inc(ss.h, 1)
                s.wait_ge(ts.h, TD0 + 2 * t + 1)
                dst = self.xnT[:].rearrange("p (kc s) -> p kc s", kc=NKC)[:, :, 128 * t:128 * (t + 1)]
                src = pst0[:, 0:1024].rearrange("p (kc f) -> p kc f", kc=NKC)
                s.copy(out=dst, in_=src).then_inc(ss.h, 1)
        ss.n += 2 * NT
        SD_END = ss.n

        @block.tensor
        def _(t_):
            for t in range(NT):
                t_.wait_ge(ss.h, SD0 + 2 * t + 1)
                if t == 0:
                    t_.wait_ge(vs.h, VC_END)
                    t_.wait_ge(ss.h, S_LAST_OEVAC)
                else:
                    t_.wait_ge(ss.h, SD0 + 2 * t)
                for kc in range(NKC):
                    mm = t_.transpose(out=pst0[:, 128 * kc:128 * (kc + 1)],
                                      in_=self.xntm[:, 128 * kc:128 * (kc + 1)],
                                      identity=self.idf16[:])
                mm.then_inc(ts.h, 1)
                for half in range(2):
                    mm = t_.matmul(out=ps_o[half][0:1, 0:512],
                                   lhsT=self.onesbf[:], rhs=self.xntm[:, 512 * half:512 * half + 512],
                                   start=(t == 0), stop=(t == NT - 1), skip_group_check=True)
                mm.then_inc(ts.h, 1)
        ts.n += 2 * NT
        TD_END = ts.n

        @block.scalar
        def _(s):
            s.wait_ge(ts.h, TD_END)
            s.mul(out=self.pooled_sb[:, 0:512], in_=ps_o[0][0:1, 0:512], mul=1.0 / S)
            s.mul(out=self.pooled_sb[:, 512:1024], in_=ps_o[1][0:1, 0:512], mul=1.0 / S).then_inc(ss.h, 1)
        ss.n += 1
        SS_POOL = ss.n

        @block.tensor
        def _(t_):
            t_.wait_ge(ss.h, SS_POOL)
            for c in range(NKC):
                mm = t_.matmul(out=ps_s[:, c:c + 1], lhsT=self.pooled_sb[:, 128 * c:128 * (c + 1)],
                               rhs=self.one11[:], start=True, stop=True)
            mm.then_inc(ts.h, 1)
        ts.n += 1
        TS_PT = ts.n

        @block.vector
        def _(v):
            v.wait_ge(ts.h, TS_PT)
            v.tensor_copy(out=self.pooledT[:], in_=ps_s[:, 0:8]).then_inc(vs.h, 1)
        vs.n += 1
        VS_PT = vs.n

        @block.tensor
        def _(t_):
            t_.wait_ge(vs.h, VS_PT)
            for c in range(NKC):
                mm = t_.matmul(out=ps_o[0][0:1, 0:8], lhsT=self.pooledT[:, c:c + 1],
                               rhs=self.gwT[:, E * c:E * (c + 1)],
                               start=(c == 0), stop=(c == NKC - 1))
            mm.then_inc(ts.h, 1)
        ts.n += 1
        TS_LG = ts.n

        @block.vector
        def _(v):
            v.wait_ge(ts.h, TS_LG)
            v.tensor_tensor(out=self.lgt[:], in0=ps_o[0][0:1, 0:8], in1=self.gb[:], op=ALU.add)
            v.drain()
            v.reduce_max(out=self.gsc[:, 0:1], in_=self.lgt[:], axis=AX)
            v.drain()
            v.tensor_scalar(out=self.gates[:], in0=self.lgt[:], scalar1=self.gsc[:, 0:1],
                            scalar2=-1e30, op0=ALU.is_ge, op1=ALU.mult)
            v.drain()
            v.tensor_tensor(out=self.gates[:], in0=self.gates[:], in1=self.lgt[:], op=ALU.add)
            v.drain()
            v.reduce_max(out=self.gsc[:, 1:2], in_=self.gates[:], axis=AX)
            v.drain()
            v.tensor_scalar(out=self.gates[:], in0=self.lgt[:], scalar1=self.gsc[:, 1:2],
                            scalar2=-1e30, op0=ALU.is_lt, op1=ALU.mult)
            v.drain()
            v.tensor_tensor(out=self.gates[:], in0=self.gates[:], in1=self.lgt[:], op=ALU.add)
            v.drain()
            v.tensor_scalar(out=self.gates[:], in0=self.gates[:], scalar1=self.gsc[:, 0:1],
                            scalar2=None, op0=ALU.subtract).then_inc(vs.h, 1)
        vs.n += 1
        VS_ARG = vs.n

        @block.scalar
        def _(s):
            s.wait_ge(vs.h, VS_ARG)
            s.activation(out=self.gates[:], in_=self.gates[:], func=AF.Exp).then_inc(ss.h, 1)
        ss.n += 1
        SS_EXPG = ss.n

        @block.vector
        def _(v):
            v.wait_ge(ss.h, SS_EXPG)
            v.reduce_sum(out=self.gsc[:, 2:3], in_=self.gates[:], axis=AX)
            v.drain()
            v.reciprocal(out=self.gsc[:, 3:4], in_=self.gsc[:, 2:3])
            v.drain()
            v.tensor_scalar(out=self.gates[:], in0=self.gates[:], scalar1=self.gsc[:, 3:4],
                            scalar2=None, op0=ALU.mult).then_inc(vs.h, 1)
        vs.n += 1
        VS_GATES = vs.n

        VS_SEL0 = vs.n

        @block.vector
        def _(v):
            v.wait_ge(vs.h, VS_GATES)
            v.reduce_max(out=self.gsc[:, 4:5], in_=self.gates[:], axis=AX)
            v.drain()
            v.tensor_scalar(out=self.lgt[:], in0=self.gates[:], scalar1=self.gsc[:, 4:5],
                            scalar2=None, op0=ALU.is_ge)
            v.drain()
            v.tensor_tensor(out=self.lgt[:], in0=self.lgt[:], in1=self.iota8[:], op=ALU.mult)
            v.drain()
            v.reduce_sum(out=self.gsc[:, 5:6], in_=self.lgt[:], axis=AX)
            v.drain()
            v.tensor_scalar(out=self.lgt[:], in0=self.gates[:], scalar1=self.gsc[:, 4:5],
                            scalar2=-2.0, op0=ALU.is_ge, op1=ALU.mult)
            v.drain()
            v.tensor_tensor(out=self.lgt[:], in0=self.lgt[:], in1=self.gates[:], op=ALU.add)
            v.drain()
            v.reduce_max(out=self.gsc[:, 6:7], in_=self.lgt[:], axis=AX)
            v.drain()
            v.tensor_scalar(out=self.lgt[:], in0=self.lgt[:], scalar1=self.gsc[:, 6:7],
                            scalar2=None, op0=ALU.is_ge)
            v.drain()
            v.tensor_tensor(out=self.lgt[:], in0=self.lgt[:], in1=self.iota8[:], op=ALU.mult)
            v.drain()
            v.reduce_sum(out=self.gsc[:, 7:8], in_=self.lgt[:], axis=AX)
            v.drain()
            v.tensor_copy(out=self.gates[:, 0:1], in_=self.gsc[:, 5:6])
            v.tensor_copy(out=self.gates[:, 1:2], in_=self.gsc[:, 7:8])
            v.tensor_copy(out=self.gates[:, 2:3], in_=self.gsc[:, 4:5])
            v.drain()
            v.nop().then_inc(vs.h, 1)
        vs.n += 1
        VS_SEL = vs.n

        @block.tensor
        def _(t_):
            t_.wait_ge(vs.h, VS_SEL)
            t_.matmul(out=ps_o[0][:, 0:3], lhsT=self.ones1f[:], rhs=self.gates[:, 0:3],
                      start=True, stop=True).then_inc(ts.h, 1)
        ts.n += 1
        TS_SELB = ts.n

        @block.vector
        def _(v):
            v.wait_ge(ts.h, TS_SELB)
            v.tensor_copy(out=self.gsel[:, 0:1], in_=ps_o[0][:, 2:3])
            v.tensor_scalar(out=self.gsel[:, 1:2], in0=ps_o[0][:, 2:3], scalar1=-1.0,
                            scalar2=1.0, op0=ALU.mult, op1=ALU.add)
            for s in range(2):
                v.tensor_scalar(out=self.idxwf[:, NKC * s:NKC * s + 1], in0=ps_o[0][:, s:s + 1],
                                scalar1=float(D), scalar2=None, op0=ALU.mult)
            v.drain()
            for s in range(2):
                v.tensor_tensor(out=self.idxwf[:, NKC * s:NKC * s + 1],
                                in0=self.idxwf[:, NKC * s:NKC * s + 1],
                                in1=self.iotap[:], op=ALU.add)
            v.drain()
            for s in range(2):
                for kc in range(1, NKC):
                    v.tensor_scalar(out=self.idxwf[:, NKC * s + kc:NKC * s + kc + 1],
                                    in0=self.idxwf[:, NKC * s:NKC * s + 1],
                                    scalar1=float(128 * kc), scalar2=None, op0=ALU.add)
            v.drain()
            v.tensor_copy(out=self.idxw[:], in_=self.idxwf[:])
            v.tensor_copy(out=self.idxb[:, 0:1], in_=ps_o[0][:, 0:1])
            v.tensor_copy(out=self.idxb[:, 1:2], in_=ps_o[0][:, 1:2])
            v.drain()
            v.nop().then_inc(vs.h, 1)
        vs.n += 1
        self.VS_IDX = vs.n
        self.pso_free = (self.vs, self.VS_IDX)

        if self.tap == "gates" and b == 0:
            self.tap_dump(self.gsel)

    def _ffn(self, b):
        block = self.block
        dx, dw, ts, vs, ss = self.dx, self.dw, self.ts, self.vs, self.ss
        ps_s, ps_o = self.ps_s, self.ps_o
        VS_IDX = self.VS_IDX
        TAO_END = self.TAO_END
        NS = 2

        T0 = ts.n
        V0 = vs.n
        S0 = ss.n
        DW0 = dw.n
        self.VF0 = V0
        self.FFN_VPE = 2 * NT

        tpe = NT
        vpe = 2 * NT
        spe = NT
        IOA = bass.IndirectOffsetOnAxis

        @block.gpsimd
        def _(gp):
            for s in range(NS):
                gp.wait_ge(ts.h, TAO_END)
                gp.wait_ge(vs.h, VS_IDX)
                if s == 1:
                    gp.wait_ge(vs.h, V0 + 2 * (NT - 1) + 1)
                for kc in range(NKC):
                    gp.indirect_dma_start(
                        out=self.wexp[s][:, D * kc:D * (kc + 1)], out_offset=None,
                        in_=self.weT_h[:, :],
                        in_offset=IOA(ap=self.idxw[:, NKC * s + kc:NKC * s + kc + 1], axis=0),
                    ).then_inc(dw.h, 16)
                gp.indirect_dma_start(
                    out=self.bbc[:], out_offset=None, in_=self.be_h[:, :],
                    in_offset=IOA(ap=self.idxb[:, s:s + 1], axis=0),
                ).then_inc(dw.h, 16)
        dw.n += NS * (NKC + 1) * 16

        @block.tensor
        def _(t_):
            for s in range(NS):
                t_.wait_ge(dw.h, DW0 + 16 * (NKC + 1) * (s + 1))
                if s == 0:
                    t_.wait_ge(vs.h, VS_IDX)
                for tc in range(NT):
                    lo = 1024 * (tc % 2)
                    if s == 0 and tc < 2:
                        pass
                    elif tc >= 2:
                        t_.wait_ge(vs.h, V0 + s * vpe + 2 * (tc - 2) + 1)
                    else:
                        t_.wait_ge(vs.h, V0 + (s - 1) * vpe + 2 * (NT - 2 + tc) + 1)
                    w = self.wexp[s]
                    for half in range(2):
                        for kc in range(NKC):
                            mm = t_.matmul(out=ps_s[:, lo + 512 * half:lo + 512 * half + 512],
                                           lhsT=self.xnT[:, S * kc + 128 * tc:S * kc + 128 * (tc + 1)],
                                           rhs=w[:, D * kc + 512 * half:D * kc + 512 * half + 512],
                                           start=(kc == 0), stop=(kc == NKC - 1))
                    mm.then_inc(ts.h, 1)
        ts.n += NS * tpe

        @block.vector
        def _(v):
            v.wait_ge(vs.h, VS_IDX)
            for s in range(NS):
                for tc in range(NT):
                    lo = 1024 * (tc % 2)
                    v.wait_ge(ts.h, T0 + s * tpe + tc + 1)
                    if tc == 0:
                        v.wait_ge(dw.h, DW0 + 16 * (NKC + 1) * (s + 1))
                    if tc >= 2:
                        v.wait_ge(ss.h, S0 + s * spe + (tc - 2) + 1)
                    elif s > 0:
                        v.wait_ge(ss.h, S0 + (s - 1) * spe + (NT - 2 + tc) + 1)
                    v.tensor_tensor(out=self.ftmp[tc % 2][:, 0:512],
                                    in0=ps_s[:, lo:lo + 512], in1=self.bbc[:, 0:512], op=ALU.add)
                    v.tensor_tensor(out=self.ftmp[tc % 2][:, 512:1024],
                                    in0=ps_s[:, lo + 512:lo + 1024], in1=self.bbc[:, 512:1024],
                                    op=ALU.add).then_inc(vs.h, 1)
                    v.wait_ge(ss.h, S0 + s * spe + tc + 1)
                    v.scalar_tensor_tensor(out=self.x2[:, D * tc:D * (tc + 1)],
                                           in0=self.ftmp[tc % 2][:], scalar=self.gsel[:, s:s + 1],
                                           in1=self.x2[:, D * tc:D * (tc + 1)],
                                           op0=ALU.mult, op1=ALU.add).then_inc(vs.h, 1)
        vs.n += NS * vpe

        @block.scalar
        def _(s_):
            for s in range(NS):
                for tc in range(NT):
                    s_.wait_ge(vs.h, V0 + s * vpe + 2 * tc + 1)
                    s_.activation(out=self.ftmp[tc % 2][:], in_=self.ftmp[tc % 2][:],
                                  func=AF.Gelu).then_inc(ss.h, 1)
        ss.n += NS * spe



def _consts():
    p = np.arange(128)[:, None].astype(np.float32)
    g = np.arange(2048)[None, :].astype(np.float32)
    d0mv = (p - g).astype(np.float32)
    f = np.arange(128)[None, :].astype(np.float32)
    g640v = np.where(p <= f, p - f, MASKVAL).astype(np.float32)
    r32 = np.float32(_RATIO)
    t0mv = (d0mv * r32).astype(np.float32)
    gt640v = (g640v * r32).astype(np.float32)
    idf16 = np.eye(128, dtype=np.float16)
    idbf = np.eye(128, dtype=ml_dtypes.bfloat16)
    return d0mv.astype(np.float16), g640v.astype(np.float16), t0mv, gt640v, idf16, idbf


def make_in_maps(L0_attn_x, L0_attn_q_w, L0_attn_k_w, L0_attn_v_w, L0_attn_o_w,
                 L0_ffn_gate_w, L0_ffn_gate_b, L0_ffn_expert_w, L0_ffn_expert_b):
    x = np.ascontiguousarray(np.asarray(L0_attn_x, dtype=np.float32))
    wq = np.ascontiguousarray(np.asarray(L0_attn_q_w, dtype=np.float32))
    wk = np.ascontiguousarray(np.asarray(L0_attn_k_w, dtype=np.float32))
    wv = np.ascontiguousarray(np.asarray(L0_attn_v_w, dtype=np.float32))
    wo = np.ascontiguousarray(np.asarray(L0_attn_o_w, dtype=np.float32))
    gwT = np.ascontiguousarray(np.asarray(L0_ffn_gate_w, dtype=np.float32).T)
    gb = np.ascontiguousarray(np.asarray(L0_ffn_gate_b, dtype=np.float32)[None, :])
    weT = np.ascontiguousarray(
        np.asarray(L0_ffn_expert_w, dtype=np.float32).transpose(0, 2, 1).reshape(E * D, D))
    be = np.ascontiguousarray(np.asarray(L0_ffn_expert_b, dtype=np.float32))
    iotap = np.arange(128, dtype=np.float32)[:, None]
    iota8 = np.arange(E, dtype=np.float32)[None, :]
    d0mv, g640v, t0mv, gt640v, idf16, idbf = _consts()
    in_maps = []
    for c in range(8):
        in_maps.append({
            "x": x[2 * c:2 * c + 2],
            "wq": wq, "wk": wk, "wv": wv, "wo": wo,
            "gwT": gwT, "gb": gb, "weT": weT, "be": be,
            "idf16": idf16, "idbf": idbf, "d0m": d0mv, "g640": g640v,
            "t0m": t0mv, "gt640": gt640v,
            "iotap": iotap, "iota8": iota8,
        })
    return in_maps


def kernel(L0_attn_x, L0_attn_q_w, L0_attn_k_w, L0_attn_v_w, L0_attn_o_w,
           L0_ffn_gate_w, L0_ffn_gate_b, L0_ffn_expert_w, L0_ffn_expert_b):
    in_maps = make_in_maps(L0_attn_x, L0_attn_q_w, L0_attn_k_w, L0_attn_v_w, L0_attn_o_w,
                           L0_ffn_gate_w, L0_ffn_gate_b, L0_ffn_expert_w, L0_ffn_expert_b)
    nc = Builder().build()
    res = run_bass_kernel_spmd(nc, in_maps, core_ids=list(range(8)))
    out = np.concatenate([res.results[c]["out"] for c in range(8)], axis=0)
    return np.ascontiguousarray(out.astype(np.float32))


# version 26
# speedup vs baseline: 1.0676x; 1.0457x over previous
import numpy as np
import ml_dtypes

import concourse.bass as bass
import concourse.mybir as mybir
from concourse.bass_utils import run_bass_kernel_spmd
from contextlib import ExitStack

F32 = mybir.dt.float32
BF16 = mybir.dt.bfloat16
F16 = mybir.dt.float16
AF = mybir.ActivationFunctionType
ALU = mybir.AluOpType
AX = mybir.AxisListType.X

D = 1024
S = 2048
NPAIR = 8
NT = 16
NKC = 8
E = 8
EPS = 1e-5
MASKVAL = -57344.0
_RATIO = 2.0 ** (-8.0 / 16)
SLOPES = [float(np.float32(_RATIO ** (h + 1))) for h in range(16)]


class Builder:
    def __init__(self, tap=None):
        self.tap = tap
        nc = bass.Bass("TRN2", target_bir_lowering=False, debug=False)
        self.nc = nc
        self.x_h = nc.dram_tensor("x", [2, S, D], F32, kind="ExternalInput").ap()
        self.wq_h = nc.dram_tensor("wq", [D, D], F32, kind="ExternalInput").ap()
        self.wk_h = nc.dram_tensor("wk", [D, D], F32, kind="ExternalInput").ap()
        self.wv_h = nc.dram_tensor("wv", [D, D], F32, kind="ExternalInput").ap()
        self.wo_h = nc.dram_tensor("wo", [D, D], F32, kind="ExternalInput").ap()
        self.gwT_h = nc.dram_tensor("gwT", [D, E], F32, kind="ExternalInput").ap()
        self.gb_h = nc.dram_tensor("gb", [1, E], F32, kind="ExternalInput").ap()
        self.weT_h = nc.dram_tensor("weT", [E * D, D], F32, kind="ExternalInput").ap()
        self.be_h = nc.dram_tensor("be", [E, D], F32, kind="ExternalInput").ap()
        self.iotap_h = nc.dram_tensor("iotap", [128, 1], F32, kind="ExternalInput").ap()
        self.iota8_h = nc.dram_tensor("iota8", [1, E], F32, kind="ExternalInput").ap()
        self.idf16_h = nc.dram_tensor("idf16", [128, 128], F16, kind="ExternalInput").ap()
        self.idbf_h = nc.dram_tensor("idbf", [128, 128], BF16, kind="ExternalInput").ap()
        self.d0m_h = nc.dram_tensor("d0m", [128, 2048], F16, kind="ExternalInput").ap()
        self.g640_h = nc.dram_tensor("g640", [128, 128], F16, kind="ExternalInput").ap()
        self.t0m_h = nc.dram_tensor("t0m", [128, 2048], F32, kind="ExternalInput").ap()
        self.gt640_h = nc.dram_tensor("gt640", [128, 128], F32, kind="ExternalInput").ap()
        self.out_h = nc.dram_tensor("out", [2, S, D], F32, kind="ExternalOutput").ap()
        if tap is not None:
            self.dbg_h = nc.dram_tensor("dbg", [128, 16384], F32, kind="ExternalOutput").ap()

    def build(self):
        nc = self.nc
        with ExitStack() as st_:
            e = st_.enter_context
            self.idf16 = e(nc.sbuf_tensor("idf16s", [128, 128], F16))
            self.d0m = e(nc.sbuf_tensor("d0ms", [128, 2048], F16))
            self.g640 = e(nc.sbuf_tensor("g640s", [128, 128], F16))
            self.t0m = e(nc.sbuf_tensor("t0ms", [128, 2048], F32))
            self.gt640 = e(nc.sbuf_tensor("gt640s", [128, 128], F32))
            self.gwT = e(nc.sbuf_tensor("gwTs", [128, NKC * E], F32))
            self.gb = e(nc.sbuf_tensor("gbs", [1, E], F32))
            self.iotap = e(nc.sbuf_tensor("iotap_s", [128, 1], F32))
            self.iota8 = e(nc.sbuf_tensor("iota8_s", [1, E], F32))
            self.onesbf = e(nc.sbuf_tensor("onesbf", [128, 1], F16))
            self.ones1f = e(nc.sbuf_tensor("ones1f", [1, 128], F32))
            self.one11 = e(nc.sbuf_tensor("one11", [1, 1], F32))
            self.xnT = e(nc.sbuf_tensor("xnT", [128, NKC * S], F16))
            self.x2 = e(nc.sbuf_tensor("x2", [128, NT * D], F32))
            self.ostg = [e(nc.sbuf_tensor(f"ostg{i}", [128, 128], F16)) for i in range(2)]
            self.wpair = [[e(nc.sbuf_tensor(f"wp{i}{m}", [128, NKC * 128], F16))
                           for m in range(3)] for i in range(2)]
            self.wbuf = e(nc.sbuf_tensor("wbuf", [128, NKC * D], F16))
            self.wbuf2 = e(nc.sbuf_tensor("wbuf2", [128, NKC * D], F16))
            self.qt = e(nc.sbuf_tensor("qts", [128, S], F16))
            self.kt = e(nc.sbuf_tensor("kts", [128, S], F16))
            self.vp = e(nc.sbuf_tensor("vps", [128, NT * 130], F16))
            self.spp = [[e(nc.sbuf_tensor(f"spp{par}{h}", [128, 2048], F16)) for h in range(2)]
                        for par in range(2)]
            self.ftmp = [e(nc.sbuf_tensor(f"ftmp{i}", [128, D], F32)) for i in range(2)]
            self.pT = [e(nc.sbuf_tensor(f"pT{h}", [128, 2048], F16)) for h in range(2)]
            self.recip = e(nc.sbuf_tensor("recips", [128, 4], F32))
            self.xtile = self.ftmp
            self.bnst = e(nc.sbuf_tensor("bnst", [128, 12 * NT], F32))
            self.mvall = e(nc.sbuf_tensor("mvall", [128, 2 * NT], F32))
            self.rstd = e(nc.sbuf_tensor("rstd", [128, NT], F32))
            self.negmr = e(nc.sbuf_tensor("negmr", [128, NT], F32))
            self.lntmp = e(nc.sbuf_tensor("lntmp", [128, NT], F32))
            self.xntm = e(nc.sbuf_tensor("xntm", [128, D], F16))
            self.pooledT = e(nc.sbuf_tensor("pooledT", [128, 8], F32))
            self.lgt = e(nc.sbuf_tensor("lgt", [1, E], F32))
            self.gsc = e(nc.sbuf_tensor("gsc", [1, 8], F32))
            self.gates = e(nc.sbuf_tensor("gates", [1, E], F32))
            self.gsel = e(nc.sbuf_tensor("gsel", [128, 2], F32))
            self.idxw = e(nc.sbuf_tensor("idxw", [128, 2 * NKC], mybir.dt.uint32))
            self.idxwf = e(nc.sbuf_tensor("idxwf", [128, 2 * NKC], F32))
            self.idxb = e(nc.sbuf_tensor("idxb", [128, 2], mybir.dt.uint32))
            self.wexp = [self.wbuf, self.wbuf2]
            self.bbc = e(nc.sbuf_tensor("bbc", [128, D], F16))
            self.pooled_sb = self.ftmp[0][0:1, 0:D]
            self.ps_s = e(nc.psum_tensor("ps_s", [128, 2048], F32))
            self.ps_t = [e(nc.psum_tensor(f"ps_t{h}", [128, 1024], F16)) for h in range(2)]
            self.ps_o = [e(nc.psum_tensor(f"ps_o{h}", [128, 512], F32)) for h in range(2)]

            class Led:
                def __init__(s, name, dma=False):
                    s.h = e(nc.semaphore(name))
                    s.n = 0
                    s.step = 16 if dma else 1

            self.dx = Led("dx", dma=True)
            self.dw = Led("dw", dma=True)
            self.ts = Led("ts")
            self.vs = Led("vs")
            self.ss = Led("ss")
            self.so = Led("so", dma=True)
            self.dxx = Led("dxx", dma=True)
            self.xs4 = [Led(f"xs{i}", dma=True) for i in range(4)]

            self.block = e(nc.Block())
            self.pso_free = (self.vs, 0)
            self.exp_last = [0, 0]
            self.v_last_stt = 0
            self.s_last_oevac = 0
            self.done = False
            self.xload_base = {}

            self._consts()
            for b in range(2):
                self._seq(b)
                if self.done:
                    break

            SO_END = self.so.n

            @self.block.sync
            def _(sync):
                sync.wait_ge(self.so.h, SO_END)
        return nc

    def ot_ap(self, pp, cols):
        base = self.wbuf if pp < 4 else self.wbuf2
        off = 2048 * (pp % 4)
        return base[:, off + cols[0]:off + cols[1]]

    def tap_dump(self, sbuf_ap, nrows=128):
        dbg = self.dbg_h
        vsn, ssn, tsn = self.vs.n, self.ss.n, self.ts.n
        dxn = self.dx.n
        fs = min(sbuf_ap.shape[-1], 16384)

        @self.block.gpsimd
        def _(gp):
            gp.wait_ge(self.vs.h, vsn)
            gp.wait_ge(self.ss.h, ssn)
            gp.wait_ge(self.ts.h, tsn)
            gp.dma_start(out=dbg[0:nrows, 0:fs], in_=sbuf_ap[0:nrows, 0:fs]).then_inc(self.so.h, 16)
        self.so.n += 16
        self.done = True

    def _consts(self):
        dx, vs = self.dx, self.vs

        @self.block.sync
        def _(sync):
            sync.dma_start(out=self.idf16[:], in_=self.idf16_h[:, :]).then_inc(dx.h, 16)
            sync.dma_start(out=self.d0m[:], in_=self.d0m_h[:, :]).then_inc(dx.h, 16)
            sync.dma_start(out=self.g640[:], in_=self.g640_h[:, :]).then_inc(dx.h, 16)
            sync.dma_start(out=self.t0m[:], in_=self.t0m_h[:, :]).then_inc(dx.h, 16)
            sync.dma_start(out=self.gt640[:], in_=self.gt640_h[:, :]).then_inc(dx.h, 16)
            sync.dma_start(out=self.gwT[:].rearrange("p (kc e) -> p kc e", kc=NKC),
                           in_=self.gwT_h.rearrange("(kc p) e -> p kc e", p=128)).then_inc(dx.h, 16)
            sync.dma_start(out=self.gb[:], in_=self.gb_h[:, :]).then_inc(dx.h, 16)
            sync.dma_start(out=self.iotap[:], in_=self.iotap_h[:, :]).then_inc(dx.h, 16)
            sync.dma_start(out=self.iota8[:], in_=self.iota8_h[:, :]).then_inc(dx.h, 16)
        dx.n += 9 * 16
        self.CONST_DX = dx.n

        CONST_DX = dx.n

        @self.block.vector
        def _(v):
            v.wait_ge(dx.h, CONST_DX)
            v.memset(self.onesbf[:], 1.0)
            v.memset(self.ones1f[:], 1.0)
            v.memset(self.one11[:], 1.0)
            vpv = self.vp[:].rearrange("p (t c) -> p t c", c=130)
            v.memset(vpv[:, :, 64:65], 1.0)
            m = v.memset(vpv[:, :, 129:130], 1.0)
            m.then_inc(vs.h, 1)
        vs.n += 1
        self.VS_CONST = vs.n

    def _seq(self, b):
        nc, block = self.nc, self.block
        dx, dw, ts, vs, ss, so = self.dx, self.dw, self.ts, self.vs, self.ss, self.so
        wbig = lambda w: w.rearrange("(kc p) d -> p (kc d)", p=128)

        TS_B0, VS_B0, SS_B0, SO_B0 = ts.n, vs.n, ss.n, so.n

        self.TS_B0 = TS_B0
        DW_QKV = 0

        VA0 = vs.n
        SA0 = ss.n
        pst0 = self.ps_t[0]

        dxx = self.dxx
        xs4 = self.xs4
        if b == 0:
            XB = dxx.n

            @block.sync
            def _(sync):
                for t in range(NT):
                    if t >= 8:
                        sync.wait_ge(vs.h, VA0 + (t - 7))
                    sync.dma_start(out=self.x2[:, D * t:D * (t + 1)],
                                   in_=self.x_h[b, 128 * t:128 * (t + 1), :]).then_inc(xs4[t % 4].h, 16)
            for i in range(4):
                xs4[i].n += 4 * 16
            self.xload_base[0] = XB
        XB = self.xload_base[b]

        @block.vector
        def _(v):
            for t in range(NT):
                if b == 0:
                    v.wait_ge(xs4[t % 4].h, 16 * min(t // 4 + 2, 4))
                elif t == 0:
                    v.wait_ge(dxx.h, XB + 16 * NT)
                xt = self.x2[:, D * t:D * (t + 1)]
                v.bn_stats(out=self.bnst[:, 12 * t:12 * t + 6], in_=xt[:, 0:512])
                v.bn_stats(out=self.bnst[:, 12 * t + 6:12 * t + 12], in_=xt[:, 512:1024]).then_inc(vs.h, 1)
            v.drain()
            for t in range(NT):
                v.bn_aggr(out=self.mvall[:, 2 * t:2 * t + 2], in_=self.bnst[:, 12 * t:12 * t + 12])
            v.drain()
            v.tensor_scalar(out=self.lntmp[:], in0=self.mvall[:, 1:2 * NT:2],
                            scalar1=EPS, scalar2=None, op0=ALU.add)
            v.drain()
            v.reciprocal(out=self.rstd[:], in_=self.lntmp[:]).then_inc(vs.h, 1)
        vs.n += NT + 1
        VA_END = vs.n

        @block.scalar
        def _(s):
            s.wait_ge(vs.h, VA_END)
            s.activation(out=self.rstd[:], in_=self.rstd[:], func=AF.Sqrt).then_inc(ss.h, 1)
        ss.n += 1
        SS_RSTD = ss.n

        @block.vector
        def _(v):
            v.wait_ge(ss.h, SS_RSTD)
            v.scalar_tensor_tensor(out=self.negmr[:], in0=self.mvall[:, 0:2 * NT:2],
                                   scalar=-1.0, in1=self.rstd[:],
                                   op0=ALU.mult, op1=ALU.mult).then_inc(vs.h, 1)
        vs.n += 1
        VS_NEGMR = vs.n

        SB0 = ss.n
        TB0 = ts.n
        pst0 = self.ps_t[0]

        @block.scalar
        def _(s):
            for t in range(NT):
                if t == 0:
                    s.wait_ge(vs.h, VS_NEGMR)
                    s.wait_ge(ts.h, TS_B0)
                s.activation(out=self.xntm[:], in_=self.x2[:, D * t:D * (t + 1)], func=AF.Identity,
                             scale=self.rstd[:, t:t + 1], bias=self.negmr[:, t:t + 1]).then_inc(ss.h, 1)
                s.wait_ge(ts.h, TB0 + t + 1)
                dst = self.xnT[:].rearrange("p (kc s) -> p kc s", kc=NKC)[:, :, 128 * t:128 * (t + 1)]
                src = pst0[:, 0:1024].rearrange("p (kc f) -> p kc f", kc=NKC)
                s.copy(out=dst, in_=src).then_inc(ss.h, 1)
        ss.n += 2 * NT
        SB_END = ss.n

        @block.tensor
        def _(t_):
            for t in range(NT):
                t_.wait_ge(ss.h, SB0 + 2 * t + 1)
                if t >= 1:
                    t_.wait_ge(ss.h, SB0 + 2 * t)
                for kc in range(NKC):
                    mm = t_.transpose(out=pst0[:, 128 * kc:128 * (kc + 1)],
                                      in_=self.xntm[:, 128 * kc:128 * (kc + 1)],
                                      identity=self.idf16[:])
                mm.then_inc(ts.h, 1)
        ts.n += NT
        TS_LN1_END = ts.n

        if self.tap == "xnT" and b == 0:
            self.tap_dump(self.xnT)
            return

        self._attention(b, DW_QKV, SB_END, TS_LN1_END)
        if self.done:
            return
        self._oproj(b, SO_B0)
        if self.done:
            return
        self._ln2_gates(b)
        if self.done:
            return
        self._ffn(b)
        if self.done:
            return

        VF0 = self.VF0
        SO_BASE = so.n
        XBN = self.dxx.n
        dxx = self.dxx

        @block.sync
        def _(sync):
            for t in range(NT):
                sync.wait_ge(vs.h, VF0 + 1 * (2 * NT) + 2 * t + 2)
                sync.dma_start(out=self.out_h[b, 128 * t:128 * (t + 1), :],
                               in_=self.x2[:, D * t:D * (t + 1)]).then_inc(so.h, 16)
                if b == 0:
                    sync.wait_ge(so.h, SO_BASE + 16 * (t + 1))
                    sync.dma_start(out=self.x2[:, D * t:D * (t + 1)],
                                   in_=self.x_h[1, 128 * t:128 * (t + 1), :]).then_inc(dxx.h, 16)
        so.n += NT * 16
        if b == 0:
            dxx.n += NT * 16
            self.xload_base[1] = XBN

    def _attention(self, b, DW_QKV, SS_LN1_END, TS_LN1_END):
        block = self.block
        dx, dw, ts, vs, ss = self.dx, self.dw, self.ts, self.vs, self.ss
        ps_s, ps_t, ps_o = self.ps_s, self.ps_t, self.ps_o
        qt, kt, vp, spp, pT = self.qt, self.kt, self.vp, self.spp, self.pT

        if not hasattr(self, "proj_end_marks"):
            self.proj_end_marks = {}
        wsrc = [self.wq_h, self.wk_h, self.wv_h]
        DWB = dw.n
        for pp in range(NPAIR):
            prev_mark = self.proj_end_marks.get(pp - 2, self.TS_B0)

            @block.gpsimd
            def _(gp):
                gp.wait_ge(ts.h, prev_mark)
                for m in range(3):
                    src_ap = wsrc[m][:, 128 * pp:128 * (pp + 1)].rearrange(
                        "(kc p) d -> p kc d", p=128)
                    dst_ap = self.wpair[pp % 2][m][:].rearrange("p (kc d) -> p kc d", kc=NKC)
                    gp.dma_start(out=dst_ap, in_=src_ap).then_inc(dw.h, 16)
            dw.n += 48

            TP0 = ts.n
            SP0 = ss.n
            V_LAST_STT = self.v_last_stt
            PSO_FREE = self.pso_free
            S_LAST_OEVAC = self.s_last_oevac
            DW_PAIR = DWB + 48 * min(pp + 2, NPAIR)
            wp = self.wpair[pp % 2]

            @block.tensor
            def _(t_):
                t_.wait_ge(dw.h, DW_PAIR)
                if pp == 0:
                    t_.wait_ge(ss.h, SS_LN1_END)
                for i in range(8):
                    wmat = wp[0] if i < 4 else wp[1]
                    tc4 = i % 4
                    if i >= 4:
                        t_.wait_ge(ss.h, SP0 + (i - 4) + 1)
                    elif pp > 0:
                        t_.wait_ge(vs.h, V_LAST_STT)
                    bank = ps_s[:, 512 * tc4:512 * (tc4 + 1)]
                    for kc in range(NKC):
                        mm = t_.matmul(out=bank,
                                       lhsT=wmat[:, 128 * kc:128 * (kc + 1)],
                                       rhs=self.xnT[:, S * kc + 512 * tc4:S * kc + 512 * (tc4 + 1)],
                                       start=(kc == 0), stop=(kc == NKC - 1))
                    mm.then_inc(ts.h, 1)
                for tc in range(NT):
                    if tc >= 2:
                        t_.wait_ge(ss.h, SP0 + 8 + (tc - 2) + 1)
                    else:
                        led, cnt = PSO_FREE
                        t_.wait_ge(led.h, cnt)
                        t_.wait_ge(ss.h, S_LAST_OEVAC)
                        if pp == 0:
                            t_.wait_ge(vs.h, self.VS_CONST)
                    bank = ps_o[tc % 2][:, 0:128]
                    for kc in range(NKC):
                        mm = t_.matmul(out=bank,
                                       lhsT=self.xnT[:, S * kc + 128 * tc:S * kc + 128 * (tc + 1)],
                                       rhs=wp[2][:, 128 * kc:128 * (kc + 1)],
                                       start=(kc == 0), stop=(kc == NKC - 1))
                    mm.then_inc(ts.h, 1)
            ts.n += 24
            self.proj_end_marks[pp] = ts.n

            @block.scalar
            def _(s):
                for i in range(8):
                    s.wait_ge(ts.h, TP0 + i + 1)
                    dst = (qt if i < 4 else kt)[:, 512 * (i % 4):512 * (i % 4 + 1)]
                    if i < 4:
                        s.activation(out=dst, in_=ps_s[:, 512 * (i % 4):512 * (i % 4) + 512],
                                     func=AF.Copy, scale=0.125).then_inc(ss.h, 1)
                    else:
                        s.copy(out=dst, in_=ps_s[:, 512 * (i % 4):512 * (i % 4) + 512]).then_inc(ss.h, 1)
                for tc in range(NT):
                    s.wait_ge(ts.h, TP0 + 8 + tc + 1)
                    src = ps_o[tc % 2][:, 0:128].rearrange("p (h f) -> p h f", h=2)
                    dst = vp[:, 130 * tc:130 * (tc + 1)].rearrange("p (h f) -> p h f", f=65)[:, :, 0:64]
                    s.copy(out=dst, in_=src).then_inc(ss.h, 1)
            ss.n += 24
            SS_PROJ_END = ss.n

            slope_min = SLOPES[2 * pp + 1]
            wb = int(np.ceil((48.0 / slope_min) / 128.0)) + 1
            nchs, nblks, blos, tailbs = [], [], [], []
            for ti in range(NT):
                b_lo = max(0, (ti + 1) - wb)
                nblk_eff = (ti + 1) - b_lo
                tailb = ((nblk_eff - 1) % 4) + 1
                nf = (nblk_eff - tailb) // 4
                blos.append(b_lo)
                nchs.append(nf + 1)
                nblks.append(nblk_eff)
                tailbs.append(tailb)

            def group_cols(ti, b):
                blo, nf, tailb = blos[ti], nchs[ti] - 1, tailbs[ti]
                Bt = blo + 4 * nf
                if b >= Bt:
                    return 512 * nf + 128 * (ti - b)
                c = (b - blo) // 4
                g_hi = blo + 4 * c + 3
                return 512 * c + 128 * (g_hi - b)

            t = ts.n
            t_qk, t_pv, t_otr = {}, {}, {}
            t_qk[0] = t; t += nchs[0]
            for ti in range(NT):
                if ti + 1 < NT:
                    t_qk[ti + 1] = t; t += nchs[ti + 1]
                if ti >= 1:
                    t_otr[ti - 1] = t; t += 1
                t_pv[ti] = t; t += 2
            t_otr[NT - 1] = t; t += 1
            TS_END = t

            v = vs.n
            v_stt, v_rec, v_evac = {}, {}, {}
            v_stt[0] = v; v += 2 * nchs[0]
            for ti in range(NT):
                if ti >= 1:
                    v_evac[ti - 1] = v; v += 2
                if ti + 1 < NT:
                    v_stt[ti + 1] = v; v += 2 * nchs[ti + 1]
                v_rec[ti] = v; v += 2
            v_evac[NT - 1] = v; v += 2
            VS_END = v

            s = ss.n
            s_exp, s_otev = {}, {}
            for ti in range(NT):
                if ti >= 1:
                    s_otev[ti - 1] = s; s += 1
                s_exp[ti] = s; s += 2
            s_otev[NT - 1] = s; s += 1
            SS_END = s

            qk_wait, qk_pair = {}, {}
            if not hasattr(self, "sbank_mark2"):
                self.sbank_mark2 = [0, 0]
            sbm = list(self.sbank_mark2)
            gc_ = getattr(self, "gchunk", 0)
            for ti in range(NT):
                for c in range(nchs[ti]):
                    pr = gc_ % 2
                    qk_pair[(ti, c)] = pr
                    qk_wait[(ti, c)] = sbm[pr]
                    sbm[pr] = min(v_stt[ti] + 2 * c + 3, v_stt[ti] + 2 * nchs[ti])
                    gc_ += 1
            self.sbank_mark2 = sbm
            self.gchunk = gc_

            if not hasattr(self, "pso_free_v"):
                self.pso_free_v = [0, 0]
            PSO_FREE_V = list(self.pso_free_v)
            OTR_PREV = getattr(self, "otr_prev", 0)
            OTEV_PREV = getattr(self, "otev_prev", 0)
            qk_ss_seen = [0]

            def emit_qk(t_, ti):
                blo, nf, tailb, nch = blos[ti], nchs[ti] - 1, tailbs[ti], nchs[ti]
                for c in range(nch):
                    t_.wait_ge(vs.h, qk_wait[(ti, c)])
                    bank_free = SP0 + 8
                    qt_need = SP0 + (128 * ti) // 512 + 1
                    if c < nf:
                        bs = list(range(blo + 4 * c, blo + 4 * c + 4))
                    else:
                        bs = list(range(blo + 4 * nf, ti + 1))
                    kt_need = SP0 + 4 + (128 * (bs[-1] + 1) - 1) // 512 + 1
                    ssw = max(bank_free, qt_need, kt_need)
                    if ssw > qk_ss_seen[0]:
                        t_.wait_ge(ss.h, ssw)
                        qk_ss_seen[0] = ssw
                    lo = 1024 * qk_pair[(ti, c)]
                    for b in bs:
                        colb = group_cols(ti, b) - 512 * c
                        mm = t_.matmul(out=ps_s[:, lo + colb:lo + colb + 128],
                                       lhsT=kt[0:64, 128 * b:128 * (b + 1)],
                                       rhs=qt[0:64, 128 * ti:128 * (ti + 1)],
                                       start=True, stop=True, tile_position=(0, 0))
                        mm = t_.matmul(out=ps_s[:, lo + 512 + colb:lo + 512 + colb + 128],
                                       lhsT=kt[64:128, 128 * b:128 * (b + 1)],
                                       rhs=qt[64:128, 128 * ti:128 * (ti + 1)],
                                       start=True, stop=True, tile_position=(64, 0))
                    mm.then_inc(ts.h, 1)

            @block.tensor
            def _(t_):
                emit_qk(t_, 0)
                for ti in range(NT):
                    if ti + 1 < NT:
                        emit_qk(t_, ti + 1)
                    if ti >= 1:
                        t_.wait_ge(vs.h, v_evac[ti - 1] + 2)
                        pfree = (s_otev[ti - 2] + 1) if ti >= 2 else OTEV_PREV
                        if pfree > 0:
                            t_.wait_ge(ss.h, pfree)
                        t_.transpose(out=self.ps_t[0][:, 0:128], in_=self.ostg[(ti - 1) % 2][:],
                                     identity=self.idf16[:]).then_inc(ts.h, 1)
                    nblk, blo = nblks[ti], blos[ti]
                    for hh in range(2):
                        t_.wait_ge(ss.h, s_exp[ti] + hh + 1)
                        t_.wait_ge(vs.h, PSO_FREE_V[hh])
                        pso = ps_o[hh]
                        for bb in range(nblk):
                            ab = bb + blo
                            col = group_cols(ti, ab)
                            mm = t_.matmul(out=pso[:, 0:65],
                                           lhsT=pT[hh][:, col:col + 128],
                                           rhs=vp[:, 130 * ab + 65 * hh:130 * ab + 65 * hh + 65],
                                           start=(bb == 0), stop=(bb == nblk - 1))
                        mm.then_inc(ts.h, 1)
                    PSO_FREE_V[0] = v_evac[ti] + 1
                    PSO_FREE_V[1] = v_evac[ti] + 2
                t_.wait_ge(vs.h, v_evac[NT - 1] + 2)
                t_.wait_ge(ss.h, s_otev[NT - 2] + 1)
                t_.transpose(out=self.ps_t[0][:, 0:128], in_=self.ostg[(NT - 1) % 2][:],
                             identity=self.idf16[:]).then_inc(ts.h, 1)
            ts.n = TS_END
            self.pso_free_v = list(PSO_FREE_V)

            @block.vector
            def _(v_):
                def emit_stt(ti):
                    blo, nf, tailb, nch = blos[ti], nchs[ti] - 1, tailbs[ti], nchs[ti]
                    for c in range(nch):
                        v_.wait_ge(ts.h, t_qk[ti] + c + 1)
                        lo = 1024 * qk_pair[(ti, c)]
                        sppb = 512 * c
                        for hh in range(2):
                            h = 2 * pp + hh
                            if h % 2 == 1:
                                tabf, tabg = self.d0m, self.g640
                                sc = 2.0 ** (-((h - 1) // 2 + 1))
                            else:
                                tabf, tabg = self.t0m, self.gt640
                                sc = 2.0 ** (-(h // 2))
                            if c < nf:
                                g_hi = blo + 4 * c + 3
                                u0 = 128 * (ti - g_hi)
                                m = v_.scalar_tensor_tensor(out=spp[ti % 2][hh][:, sppb:sppb + 512],
                                                            in0=tabf[:, u0:u0 + 512], scalar=sc,
                                                            in1=ps_s[:, lo + 512 * hh:lo + 512 * hh + 512],
                                                            op0=ALU.mult, op1=ALU.add)
                            else:
                                m = v_.scalar_tensor_tensor(out=spp[ti % 2][hh][:, sppb:sppb + 128],
                                                            in0=tabg[:], scalar=sc,
                                                            in1=ps_s[:, lo + 512 * hh:lo + 512 * hh + 128],
                                                            op0=ALU.mult, op1=ALU.add)
                                if tailb > 1:
                                    w = 128 * (tailb - 1)
                                    m = v_.scalar_tensor_tensor(out=spp[ti % 2][hh][:, sppb + 128:sppb + 128 + w],
                                                                in0=tabf[:, 128:128 + w], scalar=sc,
                                                                in1=ps_s[:, lo + 512 * hh + 128:lo + 512 * hh + 128 + w],
                                                                op0=ALU.mult, op1=ALU.add)
                            m.then_inc(vs.h, 1)

                def emit_evac(ti):
                    v_.wait_ge(vs.h, v_rec[ti] + 2)
                    if ti >= 2:
                        v_.wait_ge(ts.h, t_otr[ti - 2] + 1)
                    else:
                        v_.wait_ge(ts.h, OTR_PREV)
                    for hh in range(2):
                        v_.tensor_scalar(out=self.ostg[ti % 2][:, 64 * hh:64 * hh + 64],
                                         in0=ps_o[hh][:, 0:64],
                                         scalar1=self.recip[:, 2 * (ti % 2) + hh:2 * (ti % 2) + hh + 1],
                                         scalar2=None, op0=ALU.mult).then_inc(vs.h, 1)

                def emit_rec(ti):
                    for hh in range(2):
                        v_.wait_ge(ts.h, t_pv[ti] + hh + 1)
                        v_.reciprocal(out=self.recip[:, 2 * (ti % 2) + hh:2 * (ti % 2) + hh + 1],
                                      in_=ps_o[hh][:, 64:65]).then_inc(vs.h, 1)

                emit_stt(0)
                for ti in range(NT):
                    if ti >= 1:
                        emit_evac(ti - 1)
                    if ti + 1 < NT:
                        emit_stt(ti + 1)
                    emit_rec(ti)
                emit_evac(NT - 1)
            vs.n = VS_END
            self.v_last_stt = v_stt[NT - 1] + 2 * nchs[NT - 1]
            self.otr_prev = t_otr[NT - 1] + 1

            @block.scalar
            def _(s_):
                for ti in range(NT):
                    if ti >= 1:
                        s_.wait_ge(ts.h, t_otr[ti - 1] + 1)
                        s_.copy(out=self.ot_ap(pp, (128 * (ti - 1), 128 * ti)),
                                in_=self.ps_t[0][:, 0:128]).then_inc(ss.h, 1)
                    for hh in range(2):
                        s_.wait_ge(vs.h, v_stt[ti] + 2 * (nchs[ti] - 1) + hh + 1)
                        s_.activation(out=pT[hh][:, 0:128 * nblks[ti]],
                                      in_=spp[ti % 2][hh][:, 0:128 * nblks[ti]],
                                      func=AF.Exp).then_inc(ss.h, 1)
                s_.wait_ge(ts.h, t_otr[NT - 1] + 1)
                s_.copy(out=self.ot_ap(pp, (128 * (NT - 1), 128 * NT)),
                        in_=self.ps_t[0][:, 0:128]).then_inc(ss.h, 1)
            ss.n = SS_END
            self.s_last_oevac = SS_END
            self.otev_prev = s_otev[NT - 1] + 1
            self.pso_free = (self.vs, v_evac[NT - 1] + 2)
            self.spp_free = [s_exp[14] + 2, s_exp[15] + 2]

        if self.tap == "obuf" and b == 0:
            self.tap_dump(self.wbuf)
        if self.tap == "obuf2" and b == 0:
            self.tap_dump(self.wbuf2)

    def _oproj(self, b, SO_B0):
        block = self.block
        dx, dw, ts, vs, ss = self.dx, self.dw, self.ts, self.vs, self.ss
        ps_s, ps_o = self.ps_s, self.ps_o

        TS_ATTN_END = ts.n
        SS_ATTN_END = ss.n
        V_LAST_STT = self.v_last_stt
        wo_parts = [self.spp[0][0], self.spp[0][1], self.spp[1][0], self.spp[1][1]]

        SPP_FREE = self.spp_free

        @block.gpsimd
        def _(gp):
            for q in range(4):
                gp.wait_ge(ss.h, SPP_FREE[0] if q < 2 else SPP_FREE[1])
                srcw = self.wo_h[256 * q:256 * (q + 1), :].rearrange("(kc p) d -> p kc d", p=128)
                dstw = wo_parts[q][:].rearrange("p (kc d) -> p kc d", kc=2)
                gp.dma_start(out=dstw, in_=srcw).then_inc(dw.h, 16)
        dw.n += 64
        DW_O = dw.n

        def wo_ap(kc, cols):
            part = wo_parts[kc // 2]
            off = D * (kc % 2)
            return part[:, off + cols[0]:off + cols[1]]

        VC0 = vs.n

        TAO0 = ts.n

        @block.tensor
        def _(t_):
            t_.wait_ge(dw.h, DW_O)
            t_.wait_ge(vs.h, V_LAST_STT)
            t_.wait_ge(ss.h, SS_ATTN_END)
            for tc in range(NT):
                lo = 1024 * (tc % 2)
                if tc >= 2:
                    t_.wait_ge(vs.h, VC0 + (tc - 2) + 1)
                for half in range(2):
                    for kc in range(NKC):
                        mm = t_.matmul(out=ps_s[:, lo + 512 * half:lo + 512 * half + 512],
                                       lhsT=self.ot_ap(kc, (128 * tc, 128 * (tc + 1))),
                                       rhs=wo_ap(kc, (512 * half, 512 * half + 512)),
                                       start=(kc == 0), stop=(kc == NKC - 1))
                mm.then_inc(ts.h, 1)
        ts.n += NT

        @block.vector
        def _(v):
            for tc in range(NT):
                v.wait_ge(ts.h, TAO0 + tc + 1)
                lo = 1024 * (tc % 2)
                v.tensor_tensor(out=self.x2[:, D * tc:D * tc + 512],
                                in0=ps_s[:, lo:lo + 512],
                                in1=self.x2[:, D * tc:D * tc + 512], op=ALU.add)
                v.tensor_tensor(out=self.x2[:, D * tc + 512:D * tc + 1024],
                                in0=ps_s[:, lo + 512:lo + 1024],
                                in1=self.x2[:, D * tc + 512:D * tc + 1024], op=ALU.add).then_inc(vs.h, 1)
        vs.n += NT
        self.VC_END = vs.n
        self.TAO_END = ts.n

        if self.tap == "x2attn" and b == 0:
            self.tap_dump(self.x2)

    def _ln2_gates(self, b):
        block = self.block
        dx, dw, ts, vs, ss = self.dx, self.dw, self.ts, self.vs, self.ss
        ps_s, ps_o = self.ps_s, self.ps_o
        pst0 = self.ps_t[0]
        VC_END = self.VC_END

        VD0 = vs.n

        VC0 = VC_END - NT

        @block.vector
        def _(v):
            for t in range(NT):
                v.wait_ge(vs.h, VC0 + t + 1)
                xt = self.x2[:, D * t:D * (t + 1)]
                v.bn_stats(out=self.bnst[:, 12 * t:12 * t + 6], in_=xt[:, 0:512])
                v.bn_stats(out=self.bnst[:, 12 * t + 6:12 * t + 12], in_=xt[:, 512:1024]).then_inc(vs.h, 1)
            v.drain()
            for t in range(NT):
                v.bn_aggr(out=self.mvall[:, 2 * t:2 * t + 2], in_=self.bnst[:, 12 * t:12 * t + 12])
            v.drain()
            v.tensor_scalar(out=self.lntmp[:], in0=self.mvall[:, 1:2 * NT:2],
                            scalar1=EPS, scalar2=None, op0=ALU.add)
            v.drain()
            v.reciprocal(out=self.rstd[:], in_=self.lntmp[:]).then_inc(vs.h, 1)
        vs.n += NT + 1
        VD_STATS = vs.n

        @block.scalar
        def _(s):
            s.wait_ge(vs.h, VD_STATS)
            s.activation(out=self.rstd[:], in_=self.rstd[:], func=AF.Sqrt).then_inc(ss.h, 1)
        ss.n += 1
        SS_RSTD2 = ss.n

        @block.vector
        def _(v):
            v.wait_ge(ss.h, SS_RSTD2)
            v.scalar_tensor_tensor(out=self.negmr[:], in0=self.mvall[:, 0:2 * NT:2],
                                   scalar=-1.0, in1=self.rstd[:],
                                   op0=ALU.mult, op1=ALU.mult).then_inc(vs.h, 1)
        vs.n += 1
        VD_NEGMR = vs.n

        SD0 = ss.n
        TD0 = ts.n
        S_LAST_OEVAC = self.s_last_oevac

        TAO_END = self.TAO_END

        @block.scalar
        def _(s):
            for t in range(NT):
                if t == 0:
                    s.wait_ge(vs.h, VD_NEGMR)
                    s.wait_ge(ts.h, TAO_END)
                else:
                    s.wait_ge(ts.h, TD0 + 2 * t)
                s.activation(out=self.xntm[:], in_=self.x2[:, D * t:D * (t + 1)], func=AF.Identity,
                             scale=self.rstd[:, t:t + 1], bias=self.negmr[:, t:t + 1]).then_inc(ss.h, 1)
                s.wait_ge(ts.h, TD0 + 2 * t + 1)
                dst = self.xnT[:].rearrange("p (kc s) -> p kc s", kc=NKC)[:, :, 128 * t:128 * (t + 1)]
                src = pst0[:, 0:1024].rearrange("p (kc f) -> p kc f", kc=NKC)
                s.copy(out=dst, in_=src).then_inc(ss.h, 1)
        ss.n += 2 * NT
        SD_END = ss.n

        @block.tensor
        def _(t_):
            for t in range(NT):
                t_.wait_ge(ss.h, SD0 + 2 * t + 1)
                if t == 0:
                    t_.wait_ge(vs.h, VC_END)
                    t_.wait_ge(ss.h, S_LAST_OEVAC)
                else:
                    t_.wait_ge(ss.h, SD0 + 2 * t)
                for kc in range(NKC):
                    mm = t_.transpose(out=pst0[:, 128 * kc:128 * (kc + 1)],
                                      in_=self.xntm[:, 128 * kc:128 * (kc + 1)],
                                      identity=self.idf16[:])
                mm.then_inc(ts.h, 1)
                for half in range(2):
                    mm = t_.matmul(out=ps_o[half][0:1, 0:512],
                                   lhsT=self.onesbf[:], rhs=self.xntm[:, 512 * half:512 * half + 512],
                                   start=(t == 0), stop=(t == NT - 1), skip_group_check=True)
                mm.then_inc(ts.h, 1)
        ts.n += 2 * NT
        TD_END = ts.n

        @block.scalar
        def _(s):
            s.wait_ge(ts.h, TD_END)
            s.mul(out=self.pooled_sb[:, 0:512], in_=ps_o[0][0:1, 0:512], mul=1.0 / S)
            s.mul(out=self.pooled_sb[:, 512:1024], in_=ps_o[1][0:1, 0:512], mul=1.0 / S).then_inc(ss.h, 1)
        ss.n += 1
        SS_POOL = ss.n

        @block.tensor
        def _(t_):
            t_.wait_ge(ss.h, SS_POOL)
            for c in range(NKC):
                mm = t_.matmul(out=ps_s[:, c:c + 1], lhsT=self.pooled_sb[:, 128 * c:128 * (c + 1)],
                               rhs=self.one11[:], start=True, stop=True)
            mm.then_inc(ts.h, 1)
        ts.n += 1
        TS_PT = ts.n

        @block.vector
        def _(v):
            v.wait_ge(ts.h, TS_PT)
            v.tensor_copy(out=self.pooledT[:], in_=ps_s[:, 0:8]).then_inc(vs.h, 1)
        vs.n += 1
        VS_PT = vs.n

        @block.tensor
        def _(t_):
            t_.wait_ge(vs.h, VS_PT)
            for c in range(NKC):
                mm = t_.matmul(out=ps_o[0][0:1, 0:8], lhsT=self.pooledT[:, c:c + 1],
                               rhs=self.gwT[:, E * c:E * (c + 1)],
                               start=(c == 0), stop=(c == NKC - 1))
            mm.then_inc(ts.h, 1)
        ts.n += 1
        TS_LG = ts.n

        @block.vector
        def _(v):
            v.wait_ge(ts.h, TS_LG)
            v.tensor_tensor(out=self.lgt[:], in0=ps_o[0][0:1, 0:8], in1=self.gb[:], op=ALU.add)
            v.drain()
            v.reduce_max(out=self.gsc[:, 0:1], in_=self.lgt[:], axis=AX)
            v.drain()
            v.tensor_scalar(out=self.gates[:], in0=self.lgt[:], scalar1=self.gsc[:, 0:1],
                            scalar2=-1e30, op0=ALU.is_ge, op1=ALU.mult)
            v.drain()
            v.tensor_tensor(out=self.gates[:], in0=self.gates[:], in1=self.lgt[:], op=ALU.add)
            v.drain()
            v.reduce_max(out=self.gsc[:, 1:2], in_=self.gates[:], axis=AX)
            v.drain()
            v.tensor_scalar(out=self.gates[:], in0=self.lgt[:], scalar1=self.gsc[:, 1:2],
                            scalar2=-1e30, op0=ALU.is_lt, op1=ALU.mult)
            v.drain()
            v.tensor_tensor(out=self.gates[:], in0=self.gates[:], in1=self.lgt[:], op=ALU.add)
            v.drain()
            v.tensor_scalar(out=self.gates[:], in0=self.gates[:], scalar1=self.gsc[:, 0:1],
                            scalar2=None, op0=ALU.subtract).then_inc(vs.h, 1)
        vs.n += 1
        VS_ARG = vs.n

        @block.scalar
        def _(s):
            s.wait_ge(vs.h, VS_ARG)
            s.activation(out=self.gates[:], in_=self.gates[:], func=AF.Exp).then_inc(ss.h, 1)
        ss.n += 1
        SS_EXPG = ss.n

        @block.vector
        def _(v):
            v.wait_ge(ss.h, SS_EXPG)
            v.reduce_sum(out=self.gsc[:, 2:3], in_=self.gates[:], axis=AX)
            v.drain()
            v.reciprocal(out=self.gsc[:, 3:4], in_=self.gsc[:, 2:3])
            v.drain()
            v.tensor_scalar(out=self.gates[:], in0=self.gates[:], scalar1=self.gsc[:, 3:4],
                            scalar2=None, op0=ALU.mult).then_inc(vs.h, 1)
        vs.n += 1
        VS_GATES = vs.n

        VS_SEL0 = vs.n

        @block.vector
        def _(v):
            v.wait_ge(vs.h, VS_GATES)
            v.reduce_max(out=self.gsc[:, 4:5], in_=self.gates[:], axis=AX)
            v.drain()
            v.tensor_scalar(out=self.lgt[:], in0=self.gates[:], scalar1=self.gsc[:, 4:5],
                            scalar2=None, op0=ALU.is_ge)
            v.drain()
            v.tensor_tensor(out=self.lgt[:], in0=self.lgt[:], in1=self.iota8[:], op=ALU.mult)
            v.drain()
            v.reduce_sum(out=self.gsc[:, 5:6], in_=self.lgt[:], axis=AX)
            v.drain()
            v.tensor_scalar(out=self.lgt[:], in0=self.gates[:], scalar1=self.gsc[:, 4:5],
                            scalar2=-2.0, op0=ALU.is_ge, op1=ALU.mult)
            v.drain()
            v.tensor_tensor(out=self.lgt[:], in0=self.lgt[:], in1=self.gates[:], op=ALU.add)
            v.drain()
            v.reduce_max(out=self.gsc[:, 6:7], in_=self.lgt[:], axis=AX)
            v.drain()
            v.tensor_scalar(out=self.lgt[:], in0=self.lgt[:], scalar1=self.gsc[:, 6:7],
                            scalar2=None, op0=ALU.is_ge)
            v.drain()
            v.tensor_tensor(out=self.lgt[:], in0=self.lgt[:], in1=self.iota8[:], op=ALU.mult)
            v.drain()
            v.reduce_sum(out=self.gsc[:, 7:8], in_=self.lgt[:], axis=AX)
            v.drain()
            v.tensor_copy(out=self.gates[:, 0:1], in_=self.gsc[:, 5:6])
            v.tensor_copy(out=self.gates[:, 1:2], in_=self.gsc[:, 7:8])
            v.tensor_copy(out=self.gates[:, 2:3], in_=self.gsc[:, 4:5])
            v.drain()
            v.nop().then_inc(vs.h, 1)
        vs.n += 1
        VS_SEL = vs.n

        @block.tensor
        def _(t_):
            t_.wait_ge(vs.h, VS_SEL)
            t_.matmul(out=ps_o[0][:, 0:3], lhsT=self.ones1f[:], rhs=self.gates[:, 0:3],
                      start=True, stop=True).then_inc(ts.h, 1)
        ts.n += 1
        TS_SELB = ts.n

        @block.vector
        def _(v):
            v.wait_ge(ts.h, TS_SELB)
            v.tensor_copy(out=self.gsel[:, 0:1], in_=ps_o[0][:, 2:3])
            v.tensor_scalar(out=self.gsel[:, 1:2], in0=ps_o[0][:, 2:3], scalar1=-1.0,
                            scalar2=1.0, op0=ALU.mult, op1=ALU.add)
            for s in range(2):
                v.tensor_scalar(out=self.idxwf[:, NKC * s:NKC * s + 1], in0=ps_o[0][:, s:s + 1],
                                scalar1=float(D), scalar2=None, op0=ALU.mult)
            v.drain()
            for s in range(2):
                v.tensor_tensor(out=self.idxwf[:, NKC * s:NKC * s + 1],
                                in0=self.idxwf[:, NKC * s:NKC * s + 1],
                                in1=self.iotap[:], op=ALU.add)
            v.drain()
            for s in range(2):
                for kc in range(1, NKC):
                    v.tensor_scalar(out=self.idxwf[:, NKC * s + kc:NKC * s + kc + 1],
                                    in0=self.idxwf[:, NKC * s:NKC * s + 1],
                                    scalar1=float(128 * kc), scalar2=None, op0=ALU.add)
            v.drain()
            v.tensor_copy(out=self.idxw[:], in_=self.idxwf[:])
            v.tensor_copy(out=self.idxb[:, 0:1], in_=ps_o[0][:, 0:1])
            v.tensor_copy(out=self.idxb[:, 1:2], in_=ps_o[0][:, 1:2])
            v.drain()
            v.nop().then_inc(vs.h, 1)
        vs.n += 1
        self.VS_IDX = vs.n
        self.pso_free = (self.vs, self.VS_IDX)

        if self.tap == "gates" and b == 0:
            self.tap_dump(self.gsel)

    def _ffn(self, b):
        block = self.block
        dx, dw, ts, vs, ss = self.dx, self.dw, self.ts, self.vs, self.ss
        ps_s, ps_o = self.ps_s, self.ps_o
        VS_IDX = self.VS_IDX
        TAO_END = self.TAO_END
        NS = 2

        T0 = ts.n
        V0 = vs.n
        S0 = ss.n
        DW0 = dw.n
        self.VF0 = V0
        self.FFN_VPE = 2 * NT

        tpe = NT
        vpe = 2 * NT
        spe = NT
        IOA = bass.IndirectOffsetOnAxis

        @block.gpsimd
        def _(gp):
            for s in range(NS):
                gp.wait_ge(ts.h, TAO_END)
                gp.wait_ge(vs.h, VS_IDX)
                if s == 1:
                    gp.wait_ge(vs.h, V0 + 2 * (NT - 1) + 1)
                for kc in range(NKC):
                    gp.indirect_dma_start(
                        out=self.wexp[s][:, D * kc:D * (kc + 1)], out_offset=None,
                        in_=self.weT_h[:, :],
                        in_offset=IOA(ap=self.idxw[:, NKC * s + kc:NKC * s + kc + 1], axis=0),
                    ).then_inc(dw.h, 16)
                gp.indirect_dma_start(
                    out=self.bbc[:], out_offset=None, in_=self.be_h[:, :],
                    in_offset=IOA(ap=self.idxb[:, s:s + 1], axis=0),
                ).then_inc(dw.h, 16)
        dw.n += NS * (NKC + 1) * 16

        @block.tensor
        def _(t_):
            for s in range(NS):
                t_.wait_ge(dw.h, DW0 + 16 * (NKC + 1) * (s + 1))
                if s == 0:
                    t_.wait_ge(vs.h, VS_IDX)
                for tc in range(NT):
                    lo = 1024 * (tc % 2)
                    if s == 0 and tc < 2:
                        pass
                    elif tc >= 2:
                        t_.wait_ge(vs.h, V0 + s * vpe + 2 * (tc - 2) + 1)
                    else:
                        t_.wait_ge(vs.h, V0 + (s - 1) * vpe + 2 * (NT - 2 + tc) + 1)
                    w = self.wexp[s]
                    for half in range(2):
                        for kc in range(NKC):
                            mm = t_.matmul(out=ps_s[:, lo + 512 * half:lo + 512 * half + 512],
                                           lhsT=self.xnT[:, S * kc + 128 * tc:S * kc + 128 * (tc + 1)],
                                           rhs=w[:, D * kc + 512 * half:D * kc + 512 * half + 512],
                                           start=(kc == 0), stop=(kc == NKC - 1))
                    mm.then_inc(ts.h, 1)
        ts.n += NS * tpe

        @block.vector
        def _(v):
            v.wait_ge(vs.h, VS_IDX)
            for s in range(NS):
                for tc in range(NT):
                    lo = 1024 * (tc % 2)
                    v.wait_ge(ts.h, T0 + s * tpe + tc + 1)
                    if tc == 0:
                        v.wait_ge(dw.h, DW0 + 16 * (NKC + 1) * (s + 1))
                    if tc >= 2:
                        v.wait_ge(ss.h, S0 + s * spe + (tc - 2) + 1)
                    elif s > 0:
                        v.wait_ge(ss.h, S0 + (s - 1) * spe + (NT - 2 + tc) + 1)
                    v.tensor_tensor(out=self.ftmp[tc % 2][:, 0:512],
                                    in0=ps_s[:, lo:lo + 512], in1=self.bbc[:, 0:512], op=ALU.add)
                    v.tensor_tensor(out=self.ftmp[tc % 2][:, 512:1024],
                                    in0=ps_s[:, lo + 512:lo + 1024], in1=self.bbc[:, 512:1024],
                                    op=ALU.add).then_inc(vs.h, 1)
                    v.wait_ge(ss.h, S0 + s * spe + tc + 1)
                    v.scalar_tensor_tensor(out=self.x2[:, D * tc:D * (tc + 1)],
                                           in0=self.ftmp[tc % 2][:], scalar=self.gsel[:, s:s + 1],
                                           in1=self.x2[:, D * tc:D * (tc + 1)],
                                           op0=ALU.mult, op1=ALU.add).then_inc(vs.h, 1)
        vs.n += NS * vpe

        @block.scalar
        def _(s_):
            for s in range(NS):
                for tc in range(NT):
                    s_.wait_ge(vs.h, V0 + s * vpe + 2 * tc + 1)
                    s_.activation(out=self.ftmp[tc % 2][:], in_=self.ftmp[tc % 2][:],
                                  func=AF.Gelu).then_inc(ss.h, 1)
        ss.n += NS * spe



def _consts():
    p = np.arange(128)[:, None].astype(np.float32)
    g = np.arange(2048)[None, :].astype(np.float32)
    d0mv = (p - g).astype(np.float32)
    f = np.arange(128)[None, :].astype(np.float32)
    g640v = np.where(p <= f, p - f, MASKVAL).astype(np.float32)
    r32 = np.float32(_RATIO)
    t0mv = (d0mv * r32).astype(np.float32)
    gt640v = (g640v * r32).astype(np.float32)
    idf16 = np.eye(128, dtype=np.float16)
    idbf = np.eye(128, dtype=ml_dtypes.bfloat16)
    return d0mv.astype(np.float16), g640v.astype(np.float16), t0mv, gt640v, idf16, idbf


def make_in_maps(L0_attn_x, L0_attn_q_w, L0_attn_k_w, L0_attn_v_w, L0_attn_o_w,
                 L0_ffn_gate_w, L0_ffn_gate_b, L0_ffn_expert_w, L0_ffn_expert_b):
    x = np.ascontiguousarray(np.asarray(L0_attn_x, dtype=np.float32))
    wq = np.ascontiguousarray(np.asarray(L0_attn_q_w, dtype=np.float32))
    wk = np.ascontiguousarray(np.asarray(L0_attn_k_w, dtype=np.float32))
    wv = np.ascontiguousarray(np.asarray(L0_attn_v_w, dtype=np.float32))
    wo = np.ascontiguousarray(np.asarray(L0_attn_o_w, dtype=np.float32))
    gwT = np.ascontiguousarray(np.asarray(L0_ffn_gate_w, dtype=np.float32).T)
    gb = np.ascontiguousarray(np.asarray(L0_ffn_gate_b, dtype=np.float32)[None, :])
    weT = np.ascontiguousarray(
        np.asarray(L0_ffn_expert_w, dtype=np.float32).transpose(0, 2, 1).reshape(E * D, D))
    be = np.ascontiguousarray(np.asarray(L0_ffn_expert_b, dtype=np.float32))
    iotap = np.arange(128, dtype=np.float32)[:, None]
    iota8 = np.arange(E, dtype=np.float32)[None, :]
    d0mv, g640v, t0mv, gt640v, idf16, idbf = _consts()
    in_maps = []
    for c in range(8):
        in_maps.append({
            "x": x[2 * c:2 * c + 2],
            "wq": wq, "wk": wk, "wv": wv, "wo": wo,
            "gwT": gwT, "gb": gb, "weT": weT, "be": be,
            "idf16": idf16, "idbf": idbf, "d0m": d0mv, "g640": g640v,
            "t0m": t0mv, "gt640": gt640v,
            "iotap": iotap, "iota8": iota8,
        })
    return in_maps


def kernel(L0_attn_x, L0_attn_q_w, L0_attn_k_w, L0_attn_v_w, L0_attn_o_w,
           L0_ffn_gate_w, L0_ffn_gate_b, L0_ffn_expert_w, L0_ffn_expert_b):
    in_maps = make_in_maps(L0_attn_x, L0_attn_q_w, L0_attn_k_w, L0_attn_v_w, L0_attn_o_w,
                           L0_ffn_gate_w, L0_ffn_gate_b, L0_ffn_expert_w, L0_ffn_expert_b)
    nc = Builder().build()
    res = run_bass_kernel_spmd(nc, in_maps, core_ids=list(range(8)))
    out = np.concatenate([res.results[c]["out"] for c in range(8)], axis=0)
    return np.ascontiguousarray(out.astype(np.float32))


# version 27
# speedup vs baseline: 1.0741x; 1.0061x over previous
import numpy as np
import ml_dtypes

import concourse.bass as bass
import concourse.mybir as mybir
from concourse.bass_utils import run_bass_kernel_spmd
from contextlib import ExitStack

F32 = mybir.dt.float32
BF16 = mybir.dt.bfloat16
F16 = mybir.dt.float16
AF = mybir.ActivationFunctionType
ALU = mybir.AluOpType
AX = mybir.AxisListType.X

D = 1024
S = 2048
NPAIR = 8
NT = 16
NKC = 8
E = 8
EPS = 1e-5
MASKVAL = -57344.0
_RATIO = 2.0 ** (-8.0 / 16)
SLOPES = [float(np.float32(_RATIO ** (h + 1))) for h in range(16)]


class Builder:
    def __init__(self, tap=None):
        self.tap = tap
        nc = bass.Bass("TRN2", target_bir_lowering=False, debug=False)
        self.nc = nc
        self.x_h = nc.dram_tensor("x", [2, S, D], F32, kind="ExternalInput").ap()
        self.wq_h = nc.dram_tensor("wq", [D, D], F32, kind="ExternalInput").ap()
        self.wk_h = nc.dram_tensor("wk", [D, D], F32, kind="ExternalInput").ap()
        self.wv_h = nc.dram_tensor("wv", [D, D], F32, kind="ExternalInput").ap()
        self.wo_h = nc.dram_tensor("wo", [D, D], F32, kind="ExternalInput").ap()
        self.gwT_h = nc.dram_tensor("gwT", [D, E], F32, kind="ExternalInput").ap()
        self.gb_h = nc.dram_tensor("gb", [1, E], F32, kind="ExternalInput").ap()
        self.weT_h = nc.dram_tensor("weT", [E * D, D], F32, kind="ExternalInput").ap()
        self.be_h = nc.dram_tensor("be", [E, D], F32, kind="ExternalInput").ap()
        self.iotap_h = nc.dram_tensor("iotap", [128, 1], F32, kind="ExternalInput").ap()
        self.iota8_h = nc.dram_tensor("iota8", [1, E], F32, kind="ExternalInput").ap()
        self.idf16_h = nc.dram_tensor("idf16", [128, 128], F16, kind="ExternalInput").ap()
        self.idbf_h = nc.dram_tensor("idbf", [128, 128], BF16, kind="ExternalInput").ap()
        self.d0m_h = nc.dram_tensor("d0m", [128, 2048], F16, kind="ExternalInput").ap()
        self.g640_h = nc.dram_tensor("g640", [128, 128], F16, kind="ExternalInput").ap()
        self.t0m_h = nc.dram_tensor("t0m", [128, 2048], F32, kind="ExternalInput").ap()
        self.gt640_h = nc.dram_tensor("gt640", [128, 128], F32, kind="ExternalInput").ap()
        self.out_h = nc.dram_tensor("out", [2, S, D], F32, kind="ExternalOutput").ap()
        if tap is not None:
            self.dbg_h = nc.dram_tensor("dbg", [128, 16384], F32, kind="ExternalOutput").ap()

    def build(self):
        nc = self.nc
        with ExitStack() as st_:
            e = st_.enter_context
            self.idf16 = e(nc.sbuf_tensor("idf16s", [128, 128], F16))
            self.d0m = e(nc.sbuf_tensor("d0ms", [128, 2048], F16))
            self.g640 = e(nc.sbuf_tensor("g640s", [128, 128], F16))
            self.t0m = e(nc.sbuf_tensor("t0ms", [128, 2048], F32))
            self.gt640 = e(nc.sbuf_tensor("gt640s", [128, 128], F32))
            self.gwT = e(nc.sbuf_tensor("gwTs", [128, NKC * E], F32))
            self.gb = e(nc.sbuf_tensor("gbs", [1, E], F32))
            self.iotap = e(nc.sbuf_tensor("iotap_s", [128, 1], F32))
            self.iota8 = e(nc.sbuf_tensor("iota8_s", [1, E], F32))
            self.onesbf = e(nc.sbuf_tensor("onesbf", [128, 1], F16))
            self.ones1f = e(nc.sbuf_tensor("ones1f", [1, 128], F32))
            self.one11 = e(nc.sbuf_tensor("one11", [1, 1], F32))
            self.xnT = e(nc.sbuf_tensor("xnT", [128, NKC * S], F16))
            self.x2 = e(nc.sbuf_tensor("x2", [128, NT * D], F32))
            self.ostg = [e(nc.sbuf_tensor(f"ostg{i}", [128, 128], F16)) for i in range(2)]
            self.wpair = [[e(nc.sbuf_tensor(f"wp{i}{m}", [128, NKC * 128], F16))
                           for m in range(3)] for i in range(2)]
            self.wbuf = e(nc.sbuf_tensor("wbuf", [128, NKC * D], F16))
            self.wbuf2 = e(nc.sbuf_tensor("wbuf2", [128, NKC * D], F16))
            self.qt = e(nc.sbuf_tensor("qts", [128, S], F16))
            self.kt = e(nc.sbuf_tensor("kts", [128, S], F16))
            self.vp = e(nc.sbuf_tensor("vps", [128, NT * 130], F16))
            self.spp = [[e(nc.sbuf_tensor(f"spp{par}{h}", [128, 2048], F16)) for h in range(2)]
                        for par in range(2)]
            self.ftmp = [e(nc.sbuf_tensor(f"ftmp{i}", [128, D], F32)) for i in range(2)]
            self.pT = [e(nc.sbuf_tensor(f"pT{h}", [128, 2048], F16)) for h in range(2)]
            self.recip = e(nc.sbuf_tensor("recips", [128, 4], F32))
            self.xtile = self.ftmp
            self.bnst = e(nc.sbuf_tensor("bnst", [128, 12 * NT], F32))
            self.mvall = e(nc.sbuf_tensor("mvall", [128, 2 * NT], F32))
            self.rstd = e(nc.sbuf_tensor("rstd", [128, NT], F32))
            self.negmr = e(nc.sbuf_tensor("negmr", [128, NT], F32))
            self.lntmp = e(nc.sbuf_tensor("lntmp", [128, NT], F32))
            self.xntm = e(nc.sbuf_tensor("xntm", [128, D], F16))
            self.pooledT = e(nc.sbuf_tensor("pooledT", [128, 8], F32))
            self.lgt = e(nc.sbuf_tensor("lgt", [1, E], F32))
            self.gsc = e(nc.sbuf_tensor("gsc", [1, 8], F32))
            self.gates = e(nc.sbuf_tensor("gates", [1, E], F32))
            self.gsel = e(nc.sbuf_tensor("gsel", [128, 2], F32))
            self.idxw = e(nc.sbuf_tensor("idxw", [128, 2 * NKC], mybir.dt.uint32))
            self.idxwf = e(nc.sbuf_tensor("idxwf", [128, 2 * NKC], F32))
            self.idxb = e(nc.sbuf_tensor("idxb", [128, 2], mybir.dt.uint32))
            self.wexp = [self.wbuf, self.wbuf2]
            self.bbc = e(nc.sbuf_tensor("bbc", [128, D], F16))
            self.pooled_sb = self.ftmp[0][0:1, 0:D]
            self.ps_s = e(nc.psum_tensor("ps_s", [128, 2048], F32))
            self.ps_t = [e(nc.psum_tensor(f"ps_t{h}", [128, 1024], F16)) for h in range(2)]
            self.ps_o = [e(nc.psum_tensor(f"ps_o{h}", [128, 512], F32)) for h in range(2)]

            class Led:
                def __init__(s, name, dma=False):
                    s.h = e(nc.semaphore(name))
                    s.n = 0
                    s.step = 16 if dma else 1

            self.dx = Led("dx", dma=True)
            self.dw = Led("dw", dma=True)
            self.ts = Led("ts")
            self.vs = Led("vs")
            self.ss = Led("ss")
            self.so = Led("so", dma=True)
            self.dxx = Led("dxx", dma=True)
            self.xs4 = [Led(f"xs{i}", dma=True) for i in range(4)]

            self.block = e(nc.Block())
            self.pso_free = (self.vs, 0)
            self.exp_last = [0, 0]
            self.v_last_stt = 0
            self.s_last_oevac = 0
            self.done = False
            self.xload_base = {}

            self._consts()
            for b in range(2):
                self._seq(b)
                if self.done:
                    break

            SO_END = self.so.n

            @self.block.sync
            def _(sync):
                sync.wait_ge(self.so.h, SO_END)
        return nc

    def ot_ap(self, pp, cols):
        base = self.wbuf if pp < 4 else self.wbuf2
        off = 2048 * (pp % 4)
        return base[:, off + cols[0]:off + cols[1]]

    def tap_dump(self, sbuf_ap, nrows=128):
        dbg = self.dbg_h
        vsn, ssn, tsn = self.vs.n, self.ss.n, self.ts.n
        dxn = self.dx.n
        fs = min(sbuf_ap.shape[-1], 16384)

        @self.block.gpsimd
        def _(gp):
            gp.wait_ge(self.vs.h, vsn)
            gp.wait_ge(self.ss.h, ssn)
            gp.wait_ge(self.ts.h, tsn)
            gp.dma_start(out=dbg[0:nrows, 0:fs], in_=sbuf_ap[0:nrows, 0:fs]).then_inc(self.so.h, 16)
        self.so.n += 16
        self.done = True

    def _consts(self):
        dx, vs = self.dx, self.vs

        @self.block.sync
        def _(sync):
            sync.dma_start(out=self.idf16[:], in_=self.idf16_h[:, :]).then_inc(dx.h, 16)
            sync.dma_start(out=self.d0m[:], in_=self.d0m_h[:, :]).then_inc(dx.h, 16)
            sync.dma_start(out=self.g640[:], in_=self.g640_h[:, :]).then_inc(dx.h, 16)
            sync.dma_start(out=self.t0m[:], in_=self.t0m_h[:, :]).then_inc(dx.h, 16)
            sync.dma_start(out=self.gt640[:], in_=self.gt640_h[:, :]).then_inc(dx.h, 16)
            sync.dma_start(out=self.gwT[:].rearrange("p (kc e) -> p kc e", kc=NKC),
                           in_=self.gwT_h.rearrange("(kc p) e -> p kc e", p=128)).then_inc(dx.h, 16)
            sync.dma_start(out=self.gb[:], in_=self.gb_h[:, :]).then_inc(dx.h, 16)
            sync.dma_start(out=self.iotap[:], in_=self.iotap_h[:, :]).then_inc(dx.h, 16)
            sync.dma_start(out=self.iota8[:], in_=self.iota8_h[:, :]).then_inc(dx.h, 16)
        dx.n += 9 * 16
        self.CONST_DX = dx.n

        CONST_DX = dx.n

        @self.block.vector
        def _(v):
            v.wait_ge(dx.h, CONST_DX)
            v.memset(self.onesbf[:], 1.0)
            v.memset(self.ones1f[:], 1.0)
            v.memset(self.one11[:], 1.0)
            vpv = self.vp[:].rearrange("p (t c) -> p t c", c=130)
            v.memset(vpv[:, :, 64:65], 1.0)
            m = v.memset(vpv[:, :, 129:130], 1.0)
            m.then_inc(vs.h, 1)
        vs.n += 1
        self.VS_CONST = vs.n

    def _seq(self, b):
        nc, block = self.nc, self.block
        dx, dw, ts, vs, ss, so = self.dx, self.dw, self.ts, self.vs, self.ss, self.so
        wbig = lambda w: w.rearrange("(kc p) d -> p (kc d)", p=128)

        TS_B0, VS_B0, SS_B0, SO_B0 = ts.n, vs.n, ss.n, so.n

        self.TS_B0 = TS_B0
        DW_QKV = 0

        VA0 = vs.n
        SA0 = ss.n
        pst0 = self.ps_t[0]

        dxx = self.dxx
        xs4 = self.xs4
        if b == 0:
            XB = dxx.n

            @block.sync
            def _(sync):
                for t in range(NT):
                    if t >= 8:
                        sync.wait_ge(vs.h, VA0 + (t - 7))
                    sync.dma_start(out=self.x2[:, D * t:D * (t + 1)],
                                   in_=self.x_h[b, 128 * t:128 * (t + 1), :]).then_inc(xs4[t % 4].h, 16)
            for i in range(4):
                xs4[i].n += 4 * 16
            self.xload_base[0] = XB
        XB = self.xload_base[b]

        @block.vector
        def _(v):
            for t in range(NT):
                if b == 0:
                    v.wait_ge(xs4[t % 4].h, 16 * min(t // 4 + 2, 4))
                elif t == 0:
                    v.wait_ge(dxx.h, XB + 16 * NT)
                xt = self.x2[:, D * t:D * (t + 1)]
                v.bn_stats(out=self.bnst[:, 12 * t:12 * t + 6], in_=xt[:, 0:512])
                v.bn_stats(out=self.bnst[:, 12 * t + 6:12 * t + 12], in_=xt[:, 512:1024]).then_inc(vs.h, 1)
            v.drain()
            for t in range(NT):
                v.bn_aggr(out=self.mvall[:, 2 * t:2 * t + 2], in_=self.bnst[:, 12 * t:12 * t + 12])
            v.drain()
            v.tensor_scalar(out=self.lntmp[:], in0=self.mvall[:, 1:2 * NT:2],
                            scalar1=EPS, scalar2=None, op0=ALU.add)
            v.drain()
            v.reciprocal(out=self.rstd[:], in_=self.lntmp[:]).then_inc(vs.h, 1)
        vs.n += NT + 1
        VA_END = vs.n

        @block.scalar
        def _(s):
            s.wait_ge(vs.h, VA_END)
            s.activation(out=self.rstd[:], in_=self.rstd[:], func=AF.Sqrt).then_inc(ss.h, 1)
        ss.n += 1
        SS_RSTD = ss.n

        @block.vector
        def _(v):
            v.wait_ge(ss.h, SS_RSTD)
            v.scalar_tensor_tensor(out=self.negmr[:], in0=self.mvall[:, 0:2 * NT:2],
                                   scalar=-1.0, in1=self.rstd[:],
                                   op0=ALU.mult, op1=ALU.mult).then_inc(vs.h, 1)
        vs.n += 1
        VS_NEGMR = vs.n

        SB0 = ss.n
        TB0 = ts.n
        pst0 = self.ps_t[0]

        @block.scalar
        def _(s):
            for t in range(NT):
                if t == 0:
                    s.wait_ge(vs.h, VS_NEGMR)
                    s.wait_ge(ts.h, TS_B0)
                s.activation(out=self.xntm[:], in_=self.x2[:, D * t:D * (t + 1)], func=AF.Identity,
                             scale=self.rstd[:, t:t + 1], bias=self.negmr[:, t:t + 1]).then_inc(ss.h, 1)
                s.wait_ge(ts.h, TB0 + t + 1)
                dst = self.xnT[:].rearrange("p (kc s) -> p kc s", kc=NKC)[:, :, 128 * t:128 * (t + 1)]
                src = pst0[:, 0:1024].rearrange("p (kc f) -> p kc f", kc=NKC)
                s.copy(out=dst, in_=src).then_inc(ss.h, 1)
        ss.n += 2 * NT
        SB_END = ss.n

        @block.tensor
        def _(t_):
            for t in range(NT):
                t_.wait_ge(ss.h, SB0 + 2 * t + 1)
                if t >= 1:
                    t_.wait_ge(ss.h, SB0 + 2 * t)
                for kc in range(NKC):
                    mm = t_.transpose(out=pst0[:, 128 * kc:128 * (kc + 1)],
                                      in_=self.xntm[:, 128 * kc:128 * (kc + 1)],
                                      identity=self.idf16[:])
                mm.then_inc(ts.h, 1)
        ts.n += NT
        TS_LN1_END = ts.n

        if self.tap == "xnT" and b == 0:
            self.tap_dump(self.xnT)
            return

        self._attention(b, DW_QKV, SB_END, TS_LN1_END)
        if self.done:
            return
        self._oproj(b, SO_B0)
        if self.done:
            return
        self._ln2_gates(b)
        if self.done:
            return
        self._ffn(b)
        if self.done:
            return

        VF0 = self.VF0
        SO_BASE = so.n
        XBN = self.dxx.n
        dxx = self.dxx

        @block.sync
        def _(sync):
            for t in range(NT):
                sync.wait_ge(vs.h, VF0 + 1 * (2 * NT) + 2 * t + 2)
                sync.dma_start(out=self.out_h[b, 128 * t:128 * (t + 1), :],
                               in_=self.x2[:, D * t:D * (t + 1)]).then_inc(so.h, 16)
                if b == 0:
                    sync.wait_ge(so.h, SO_BASE + 16 * (t + 1))
                    sync.dma_start(out=self.x2[:, D * t:D * (t + 1)],
                                   in_=self.x_h[1, 128 * t:128 * (t + 1), :]).then_inc(dxx.h, 16)
        so.n += NT * 16
        if b == 0:
            dxx.n += NT * 16
            self.xload_base[1] = XBN

    def _attention(self, b, DW_QKV, SS_LN1_END, TS_LN1_END):
        block = self.block
        dx, dw, ts, vs, ss = self.dx, self.dw, self.ts, self.vs, self.ss
        ps_s, ps_t, ps_o = self.ps_s, self.ps_t, self.ps_o
        qt, kt, vp, spp, pT = self.qt, self.kt, self.vp, self.spp, self.pT

        if not hasattr(self, "proj_end_marks"):
            self.proj_end_marks = {}
        wsrc = [self.wq_h, self.wk_h, self.wv_h]
        DWB = dw.n
        for pp in range(NPAIR):
            prev_mark = self.proj_end_marks.get(pp - 2, self.TS_B0)

            @block.gpsimd
            def _(gp):
                gp.wait_ge(ts.h, prev_mark)
                for m in range(3):
                    src_ap = wsrc[m][:, 128 * pp:128 * (pp + 1)].rearrange(
                        "(kc p) d -> p kc d", p=128)
                    dst_ap = self.wpair[pp % 2][m][:].rearrange("p (kc d) -> p kc d", kc=NKC)
                    gp.dma_start(out=dst_ap, in_=src_ap).then_inc(dw.h, 16)
            dw.n += 48

            TP0 = ts.n
            SP0 = ss.n
            V_LAST_STT = self.v_last_stt
            PSO_FREE = self.pso_free
            S_LAST_OEVAC = self.s_last_oevac
            DW_PAIR = DWB + 48 * min(pp + 2, NPAIR)
            wp = self.wpair[pp % 2]

            @block.tensor
            def _(t_):
                t_.wait_ge(dw.h, DW_PAIR)
                if pp == 0:
                    t_.wait_ge(ss.h, SS_LN1_END)
                for i in range(8):
                    wmat = wp[0] if i < 4 else wp[1]
                    tc4 = i % 4
                    if i >= 4:
                        t_.wait_ge(ss.h, SP0 + (i - 4) + 1)
                    elif pp > 0:
                        t_.wait_ge(vs.h, V_LAST_STT)
                    bank = ps_s[:, 512 * tc4:512 * (tc4 + 1)]
                    for kc in range(NKC):
                        mm = t_.matmul(out=bank,
                                       lhsT=wmat[:, 128 * kc:128 * (kc + 1)],
                                       rhs=self.xnT[:, S * kc + 512 * tc4:S * kc + 512 * (tc4 + 1)],
                                       start=(kc == 0), stop=(kc == NKC - 1))
                    mm.then_inc(ts.h, 1)
                for tc in range(NT):
                    if tc >= 2:
                        t_.wait_ge(ss.h, SP0 + 8 + (tc - 2) + 1)
                    else:
                        led, cnt = PSO_FREE
                        t_.wait_ge(led.h, cnt)
                        t_.wait_ge(ss.h, S_LAST_OEVAC)
                        if pp == 0:
                            t_.wait_ge(vs.h, self.VS_CONST)
                    bank = ps_o[tc % 2][:, 0:128]
                    for kc in range(NKC):
                        mm = t_.matmul(out=bank,
                                       lhsT=self.xnT[:, S * kc + 128 * tc:S * kc + 128 * (tc + 1)],
                                       rhs=wp[2][:, 128 * kc:128 * (kc + 1)],
                                       start=(kc == 0), stop=(kc == NKC - 1))
                    mm.then_inc(ts.h, 1)
            ts.n += 24
            self.proj_end_marks[pp] = ts.n

            @block.scalar
            def _(s):
                for i in range(8):
                    s.wait_ge(ts.h, TP0 + i + 1)
                    dst = (qt if i < 4 else kt)[:, 512 * (i % 4):512 * (i % 4 + 1)]
                    if i < 4:
                        s.activation(out=dst, in_=ps_s[:, 512 * (i % 4):512 * (i % 4) + 512],
                                     func=AF.Copy, scale=0.125).then_inc(ss.h, 1)
                    else:
                        s.copy(out=dst, in_=ps_s[:, 512 * (i % 4):512 * (i % 4) + 512]).then_inc(ss.h, 1)
                for tc in range(NT):
                    s.wait_ge(ts.h, TP0 + 8 + tc + 1)
                    src = ps_o[tc % 2][:, 0:128].rearrange("p (h f) -> p h f", h=2)
                    dst = vp[:, 130 * tc:130 * (tc + 1)].rearrange("p (h f) -> p h f", f=65)[:, :, 0:64]
                    s.copy(out=dst, in_=src).then_inc(ss.h, 1)
            ss.n += 24
            SS_PROJ_END = ss.n

            slope_min = SLOPES[2 * pp + 1]
            wb = int(np.ceil((44.0 / slope_min) / 128.0)) + 1
            nchs, nblks, blos, tailbs = [], [], [], []
            for ti in range(NT):
                b_lo = max(0, (ti + 1) - wb)
                nblk_eff = (ti + 1) - b_lo
                tailb = ((nblk_eff - 1) % 4) + 1
                nf = (nblk_eff - tailb) // 4
                blos.append(b_lo)
                nchs.append(nf + 1)
                nblks.append(nblk_eff)
                tailbs.append(tailb)

            def group_cols(ti, b):
                blo, nf, tailb = blos[ti], nchs[ti] - 1, tailbs[ti]
                Bt = blo + 4 * nf
                if b >= Bt:
                    return 512 * nf + 128 * (ti - b)
                c = (b - blo) // 4
                g_hi = blo + 4 * c + 3
                return 512 * c + 128 * (g_hi - b)

            t = ts.n
            t_qk, t_pv, t_otr = {}, {}, {}
            t_qk[0] = t; t += nchs[0]
            for ti in range(NT):
                if ti + 1 < NT:
                    t_qk[ti + 1] = t; t += nchs[ti + 1]
                if ti >= 1:
                    t_otr[ti - 1] = t; t += 1
                t_pv[ti] = t; t += 2
            t_otr[NT - 1] = t; t += 1
            TS_END = t

            v = vs.n
            v_stt, v_rec, v_evac = {}, {}, {}
            v_stt[0] = v; v += 2 * nchs[0]
            for ti in range(NT):
                if ti >= 1:
                    v_evac[ti - 1] = v; v += 2
                if ti + 1 < NT:
                    v_stt[ti + 1] = v; v += 2 * nchs[ti + 1]
                v_rec[ti] = v; v += 2
            v_evac[NT - 1] = v; v += 2
            VS_END = v

            s = ss.n
            s_exp, s_otev = {}, {}
            for ti in range(NT):
                if ti >= 1:
                    s_otev[ti - 1] = s; s += 1
                s_exp[ti] = s; s += 2
            s_otev[NT - 1] = s; s += 1
            SS_END = s

            qk_wait, qk_pair = {}, {}
            if not hasattr(self, "sbank_mark2"):
                self.sbank_mark2 = [0, 0]
            sbm = list(self.sbank_mark2)
            gc_ = getattr(self, "gchunk", 0)
            for ti in range(NT):
                for c in range(nchs[ti]):
                    pr = gc_ % 2
                    qk_pair[(ti, c)] = pr
                    qk_wait[(ti, c)] = sbm[pr]
                    sbm[pr] = min(v_stt[ti] + 2 * c + 3, v_stt[ti] + 2 * nchs[ti])
                    gc_ += 1
            self.sbank_mark2 = sbm
            self.gchunk = gc_

            if not hasattr(self, "pso_free_v"):
                self.pso_free_v = [0, 0]
            PSO_FREE_V = list(self.pso_free_v)
            OTR_PREV = getattr(self, "otr_prev", 0)
            OTEV_PREV = getattr(self, "otev_prev", 0)
            qk_ss_seen = [0]

            def emit_qk(t_, ti):
                blo, nf, tailb, nch = blos[ti], nchs[ti] - 1, tailbs[ti], nchs[ti]
                for c in range(nch):
                    t_.wait_ge(vs.h, qk_wait[(ti, c)])
                    bank_free = SP0 + 8
                    qt_need = SP0 + (128 * ti) // 512 + 1
                    if c < nf:
                        bs = list(range(blo + 4 * c, blo + 4 * c + 4))
                    else:
                        bs = list(range(blo + 4 * nf, ti + 1))
                    kt_need = SP0 + 4 + (128 * (bs[-1] + 1) - 1) // 512 + 1
                    ssw = max(bank_free, qt_need, kt_need)
                    if ssw > qk_ss_seen[0]:
                        t_.wait_ge(ss.h, ssw)
                        qk_ss_seen[0] = ssw
                    lo = 1024 * qk_pair[(ti, c)]
                    for b in bs:
                        colb = group_cols(ti, b) - 512 * c
                        mm = t_.matmul(out=ps_s[:, lo + colb:lo + colb + 128],
                                       lhsT=kt[0:64, 128 * b:128 * (b + 1)],
                                       rhs=qt[0:64, 128 * ti:128 * (ti + 1)],
                                       start=True, stop=True, tile_position=(0, 0))
                        mm = t_.matmul(out=ps_s[:, lo + 512 + colb:lo + 512 + colb + 128],
                                       lhsT=kt[64:128, 128 * b:128 * (b + 1)],
                                       rhs=qt[64:128, 128 * ti:128 * (ti + 1)],
                                       start=True, stop=True, tile_position=(64, 0))
                    mm.then_inc(ts.h, 1)

            @block.tensor
            def _(t_):
                emit_qk(t_, 0)
                for ti in range(NT):
                    if ti + 1 < NT:
                        emit_qk(t_, ti + 1)
                    if ti >= 1:
                        t_.wait_ge(vs.h, v_evac[ti - 1] + 2)
                        pfree = (s_otev[ti - 2] + 1) if ti >= 2 else OTEV_PREV
                        if pfree > 0:
                            t_.wait_ge(ss.h, pfree)
                        t_.transpose(out=self.ps_t[0][:, 0:128], in_=self.ostg[(ti - 1) % 2][:],
                                     identity=self.idf16[:]).then_inc(ts.h, 1)
                    nblk, blo = nblks[ti], blos[ti]
                    for hh in range(2):
                        t_.wait_ge(ss.h, s_exp[ti] + hh + 1)
                        t_.wait_ge(vs.h, PSO_FREE_V[hh])
                        pso = ps_o[hh]
                        for bb in range(nblk):
                            ab = bb + blo
                            col = group_cols(ti, ab)
                            mm = t_.matmul(out=pso[:, 0:65],
                                           lhsT=pT[hh][:, col:col + 128],
                                           rhs=vp[:, 130 * ab + 65 * hh:130 * ab + 65 * hh + 65],
                                           start=(bb == 0), stop=(bb == nblk - 1))
                        mm.then_inc(ts.h, 1)
                    PSO_FREE_V[0] = v_evac[ti] + 1
                    PSO_FREE_V[1] = v_evac[ti] + 2
                t_.wait_ge(vs.h, v_evac[NT - 1] + 2)
                t_.wait_ge(ss.h, s_otev[NT - 2] + 1)
                t_.transpose(out=self.ps_t[0][:, 0:128], in_=self.ostg[(NT - 1) % 2][:],
                             identity=self.idf16[:]).then_inc(ts.h, 1)
            ts.n = TS_END
            self.pso_free_v = list(PSO_FREE_V)

            @block.vector
            def _(v_):
                def emit_stt(ti):
                    blo, nf, tailb, nch = blos[ti], nchs[ti] - 1, tailbs[ti], nchs[ti]
                    for c in range(nch):
                        v_.wait_ge(ts.h, t_qk[ti] + c + 1)
                        lo = 1024 * qk_pair[(ti, c)]
                        sppb = 512 * c
                        for hh in range(2):
                            h = 2 * pp + hh
                            if h % 2 == 1:
                                tabf, tabg = self.d0m, self.g640
                                sc = 2.0 ** (-((h - 1) // 2 + 1))
                            else:
                                tabf, tabg = self.t0m, self.gt640
                                sc = 2.0 ** (-(h // 2))
                            if c < nf:
                                g_hi = blo + 4 * c + 3
                                u0 = 128 * (ti - g_hi)
                                m = v_.scalar_tensor_tensor(out=spp[ti % 2][hh][:, sppb:sppb + 512],
                                                            in0=tabf[:, u0:u0 + 512], scalar=sc,
                                                            in1=ps_s[:, lo + 512 * hh:lo + 512 * hh + 512],
                                                            op0=ALU.mult, op1=ALU.add)
                            else:
                                m = v_.scalar_tensor_tensor(out=spp[ti % 2][hh][:, sppb:sppb + 128],
                                                            in0=tabg[:], scalar=sc,
                                                            in1=ps_s[:, lo + 512 * hh:lo + 512 * hh + 128],
                                                            op0=ALU.mult, op1=ALU.add)
                                if tailb > 1:
                                    w = 128 * (tailb - 1)
                                    m = v_.scalar_tensor_tensor(out=spp[ti % 2][hh][:, sppb + 128:sppb + 128 + w],
                                                                in0=tabf[:, 128:128 + w], scalar=sc,
                                                                in1=ps_s[:, lo + 512 * hh + 128:lo + 512 * hh + 128 + w],
                                                                op0=ALU.mult, op1=ALU.add)
                            m.then_inc(vs.h, 1)

                def emit_evac(ti):
                    v_.wait_ge(vs.h, v_rec[ti] + 2)
                    if ti >= 2:
                        v_.wait_ge(ts.h, t_otr[ti - 2] + 1)
                    else:
                        v_.wait_ge(ts.h, OTR_PREV)
                    for hh in range(2):
                        v_.tensor_scalar(out=self.ostg[ti % 2][:, 64 * hh:64 * hh + 64],
                                         in0=ps_o[hh][:, 0:64],
                                         scalar1=self.recip[:, 2 * (ti % 2) + hh:2 * (ti % 2) + hh + 1],
                                         scalar2=None, op0=ALU.mult).then_inc(vs.h, 1)

                def emit_rec(ti):
                    for hh in range(2):
                        v_.wait_ge(ts.h, t_pv[ti] + hh + 1)
                        v_.reciprocal(out=self.recip[:, 2 * (ti % 2) + hh:2 * (ti % 2) + hh + 1],
                                      in_=ps_o[hh][:, 64:65]).then_inc(vs.h, 1)

                emit_stt(0)
                for ti in range(NT):
                    if ti >= 1:
                        emit_evac(ti - 1)
                    if ti + 1 < NT:
                        emit_stt(ti + 1)
                    emit_rec(ti)
                emit_evac(NT - 1)
            vs.n = VS_END
            self.v_last_stt = v_stt[NT - 1] + 2 * nchs[NT - 1]
            self.otr_prev = t_otr[NT - 1] + 1

            @block.scalar
            def _(s_):
                for ti in range(NT):
                    if ti >= 1:
                        s_.wait_ge(ts.h, t_otr[ti - 1] + 1)
                        s_.copy(out=self.ot_ap(pp, (128 * (ti - 1), 128 * ti)),
                                in_=self.ps_t[0][:, 0:128]).then_inc(ss.h, 1)
                    for hh in range(2):
                        s_.wait_ge(vs.h, v_stt[ti] + 2 * (nchs[ti] - 1) + hh + 1)
                        s_.activation(out=pT[hh][:, 0:128 * nblks[ti]],
                                      in_=spp[ti % 2][hh][:, 0:128 * nblks[ti]],
                                      func=AF.Exp).then_inc(ss.h, 1)
                s_.wait_ge(ts.h, t_otr[NT - 1] + 1)
                s_.copy(out=self.ot_ap(pp, (128 * (NT - 1), 128 * NT)),
                        in_=self.ps_t[0][:, 0:128]).then_inc(ss.h, 1)
            ss.n = SS_END
            self.s_last_oevac = SS_END
            self.otev_prev = s_otev[NT - 1] + 1
            self.pso_free = (self.vs, v_evac[NT - 1] + 2)
            self.spp_free = [s_exp[14] + 2, s_exp[15] + 2]

        if self.tap == "obuf" and b == 0:
            self.tap_dump(self.wbuf)
        if self.tap == "obuf2" and b == 0:
            self.tap_dump(self.wbuf2)

    def _oproj(self, b, SO_B0):
        block = self.block
        dx, dw, ts, vs, ss = self.dx, self.dw, self.ts, self.vs, self.ss
        ps_s, ps_o = self.ps_s, self.ps_o

        TS_ATTN_END = ts.n
        SS_ATTN_END = ss.n
        V_LAST_STT = self.v_last_stt
        wo_parts = [self.spp[0][0], self.spp[0][1], self.spp[1][0], self.spp[1][1]]

        SPP_FREE = self.spp_free

        @block.gpsimd
        def _(gp):
            for q in range(4):
                gp.wait_ge(ss.h, SPP_FREE[0] if q < 2 else SPP_FREE[1])
                srcw = self.wo_h[256 * q:256 * (q + 1), :].rearrange("(kc p) d -> p kc d", p=128)
                dstw = wo_parts[q][:].rearrange("p (kc d) -> p kc d", kc=2)
                gp.dma_start(out=dstw, in_=srcw).then_inc(dw.h, 16)
        dw.n += 64
        DW_O = dw.n

        def wo_ap(kc, cols):
            part = wo_parts[kc // 2]
            off = D * (kc % 2)
            return part[:, off + cols[0]:off + cols[1]]

        VC0 = vs.n

        TAO0 = ts.n

        @block.tensor
        def _(t_):
            t_.wait_ge(dw.h, DW_O)
            t_.wait_ge(vs.h, V_LAST_STT)
            t_.wait_ge(ss.h, SS_ATTN_END)
            for tc in range(NT):
                lo = 1024 * (tc % 2)
                if tc >= 2:
                    t_.wait_ge(vs.h, VC0 + (tc - 2) + 1)
                for half in range(2):
                    for kc in range(NKC):
                        mm = t_.matmul(out=ps_s[:, lo + 512 * half:lo + 512 * half + 512],
                                       lhsT=self.ot_ap(kc, (128 * tc, 128 * (tc + 1))),
                                       rhs=wo_ap(kc, (512 * half, 512 * half + 512)),
                                       start=(kc == 0), stop=(kc == NKC - 1))
                mm.then_inc(ts.h, 1)
        ts.n += NT

        @block.vector
        def _(v):
            for tc in range(NT):
                v.wait_ge(ts.h, TAO0 + tc + 1)
                lo = 1024 * (tc % 2)
                v.tensor_tensor(out=self.x2[:, D * tc:D * tc + 512],
                                in0=ps_s[:, lo:lo + 512],
                                in1=self.x2[:, D * tc:D * tc + 512], op=ALU.add)
                v.tensor_tensor(out=self.x2[:, D * tc + 512:D * tc + 1024],
                                in0=ps_s[:, lo + 512:lo + 1024],
                                in1=self.x2[:, D * tc + 512:D * tc + 1024], op=ALU.add).then_inc(vs.h, 1)
        vs.n += NT
        self.VC_END = vs.n
        self.TAO_END = ts.n

        if self.tap == "x2attn" and b == 0:
            self.tap_dump(self.x2)

    def _ln2_gates(self, b):
        block = self.block
        dx, dw, ts, vs, ss = self.dx, self.dw, self.ts, self.vs, self.ss
        ps_s, ps_o = self.ps_s, self.ps_o
        pst0 = self.ps_t[0]
        VC_END = self.VC_END

        VD0 = vs.n

        VC0 = VC_END - NT

        @block.vector
        def _(v):
            for t in range(NT):
                v.wait_ge(vs.h, VC0 + t + 1)
                xt = self.x2[:, D * t:D * (t + 1)]
                v.bn_stats(out=self.bnst[:, 12 * t:12 * t + 6], in_=xt[:, 0:512])
                v.bn_stats(out=self.bnst[:, 12 * t + 6:12 * t + 12], in_=xt[:, 512:1024]).then_inc(vs.h, 1)
            v.drain()
            for t in range(NT):
                v.bn_aggr(out=self.mvall[:, 2 * t:2 * t + 2], in_=self.bnst[:, 12 * t:12 * t + 12])
            v.drain()
            v.tensor_scalar(out=self.lntmp[:], in0=self.mvall[:, 1:2 * NT:2],
                            scalar1=EPS, scalar2=None, op0=ALU.add)
            v.drain()
            v.reciprocal(out=self.rstd[:], in_=self.lntmp[:]).then_inc(vs.h, 1)
        vs.n += NT + 1
        VD_STATS = vs.n

        @block.scalar
        def _(s):
            s.wait_ge(vs.h, VD_STATS)
            s.activation(out=self.rstd[:], in_=self.rstd[:], func=AF.Sqrt).then_inc(ss.h, 1)
        ss.n += 1
        SS_RSTD2 = ss.n

        @block.vector
        def _(v):
            v.wait_ge(ss.h, SS_RSTD2)
            v.scalar_tensor_tensor(out=self.negmr[:], in0=self.mvall[:, 0:2 * NT:2],
                                   scalar=-1.0, in1=self.rstd[:],
                                   op0=ALU.mult, op1=ALU.mult).then_inc(vs.h, 1)
        vs.n += 1
        VD_NEGMR = vs.n

        SD0 = ss.n
        TD0 = ts.n
        S_LAST_OEVAC = self.s_last_oevac

        TAO_END = self.TAO_END

        @block.scalar
        def _(s):
            for t in range(NT):
                if t == 0:
                    s.wait_ge(vs.h, VD_NEGMR)
                    s.wait_ge(ts.h, TAO_END)
                else:
                    s.wait_ge(ts.h, TD0 + 2 * t)
                s.activation(out=self.xntm[:], in_=self.x2[:, D * t:D * (t + 1)], func=AF.Identity,
                             scale=self.rstd[:, t:t + 1], bias=self.negmr[:, t:t + 1]).then_inc(ss.h, 1)
                s.wait_ge(ts.h, TD0 + 2 * t + 1)
                dst = self.xnT[:].rearrange("p (kc s) -> p kc s", kc=NKC)[:, :, 128 * t:128 * (t + 1)]
                src = pst0[:, 0:1024].rearrange("p (kc f) -> p kc f", kc=NKC)
                s.copy(out=dst, in_=src).then_inc(ss.h, 1)
        ss.n += 2 * NT
        SD_END = ss.n

        @block.tensor
        def _(t_):
            for t in range(NT):
                t_.wait_ge(ss.h, SD0 + 2 * t + 1)
                if t == 0:
                    t_.wait_ge(vs.h, VC_END)
                    t_.wait_ge(ss.h, S_LAST_OEVAC)
                else:
                    t_.wait_ge(ss.h, SD0 + 2 * t)
                for kc in range(NKC):
                    mm = t_.transpose(out=pst0[:, 128 * kc:128 * (kc + 1)],
                                      in_=self.xntm[:, 128 * kc:128 * (kc + 1)],
                                      identity=self.idf16[:])
                mm.then_inc(ts.h, 1)
                for half in range(2):
                    mm = t_.matmul(out=ps_o[half][0:1, 0:512],
                                   lhsT=self.onesbf[:], rhs=self.xntm[:, 512 * half:512 * half + 512],
                                   start=(t == 0), stop=(t == NT - 1), skip_group_check=True)
                mm.then_inc(ts.h, 1)
        ts.n += 2 * NT
        TD_END = ts.n

        @block.scalar
        def _(s):
            s.wait_ge(ts.h, TD_END)
            s.mul(out=self.pooled_sb[:, 0:512], in_=ps_o[0][0:1, 0:512], mul=1.0 / S)
            s.mul(out=self.pooled_sb[:, 512:1024], in_=ps_o[1][0:1, 0:512], mul=1.0 / S).then_inc(ss.h, 1)
        ss.n += 1
        SS_POOL = ss.n

        @block.tensor
        def _(t_):
            t_.wait_ge(ss.h, SS_POOL)
            for c in range(NKC):
                mm = t_.matmul(out=ps_s[:, c:c + 1], lhsT=self.pooled_sb[:, 128 * c:128 * (c + 1)],
                               rhs=self.one11[:], start=True, stop=True)
            mm.then_inc(ts.h, 1)
        ts.n += 1
        TS_PT = ts.n

        @block.vector
        def _(v):
            v.wait_ge(ts.h, TS_PT)
            v.tensor_copy(out=self.pooledT[:], in_=ps_s[:, 0:8]).then_inc(vs.h, 1)
        vs.n += 1
        VS_PT = vs.n

        @block.tensor
        def _(t_):
            t_.wait_ge(vs.h, VS_PT)
            for c in range(NKC):
                mm = t_.matmul(out=ps_o[0][0:1, 0:8], lhsT=self.pooledT[:, c:c + 1],
                               rhs=self.gwT[:, E * c:E * (c + 1)],
                               start=(c == 0), stop=(c == NKC - 1))
            mm.then_inc(ts.h, 1)
        ts.n += 1
        TS_LG = ts.n

        @block.vector
        def _(v):
            v.wait_ge(ts.h, TS_LG)
            v.tensor_tensor(out=self.lgt[:], in0=ps_o[0][0:1, 0:8], in1=self.gb[:], op=ALU.add)
            v.drain()
            v.reduce_max(out=self.gsc[:, 0:1], in_=self.lgt[:], axis=AX)
            v.drain()
            v.tensor_scalar(out=self.gates[:], in0=self.lgt[:], scalar1=self.gsc[:, 0:1],
                            scalar2=-1e30, op0=ALU.is_ge, op1=ALU.mult)
            v.drain()
            v.tensor_tensor(out=self.gates[:], in0=self.gates[:], in1=self.lgt[:], op=ALU.add)
            v.drain()
            v.reduce_max(out=self.gsc[:, 1:2], in_=self.gates[:], axis=AX)
            v.drain()
            v.tensor_scalar(out=self.gates[:], in0=self.lgt[:], scalar1=self.gsc[:, 1:2],
                            scalar2=-1e30, op0=ALU.is_lt, op1=ALU.mult)
            v.drain()
            v.tensor_tensor(out=self.gates[:], in0=self.gates[:], in1=self.lgt[:], op=ALU.add)
            v.drain()
            v.tensor_scalar(out=self.gates[:], in0=self.gates[:], scalar1=self.gsc[:, 0:1],
                            scalar2=None, op0=ALU.subtract).then_inc(vs.h, 1)
        vs.n += 1
        VS_ARG = vs.n

        @block.scalar
        def _(s):
            s.wait_ge(vs.h, VS_ARG)
            s.activation(out=self.gates[:], in_=self.gates[:], func=AF.Exp).then_inc(ss.h, 1)
        ss.n += 1
        SS_EXPG = ss.n

        @block.vector
        def _(v):
            v.wait_ge(ss.h, SS_EXPG)
            v.reduce_sum(out=self.gsc[:, 2:3], in_=self.gates[:], axis=AX)
            v.drain()
            v.reciprocal(out=self.gsc[:, 3:4], in_=self.gsc[:, 2:3])
            v.drain()
            v.tensor_scalar(out=self.gates[:], in0=self.gates[:], scalar1=self.gsc[:, 3:4],
                            scalar2=None, op0=ALU.mult).then_inc(vs.h, 1)
        vs.n += 1
        VS_GATES = vs.n

        VS_SEL0 = vs.n

        @block.vector
        def _(v):
            v.wait_ge(vs.h, VS_GATES)
            v.reduce_max(out=self.gsc[:, 4:5], in_=self.gates[:], axis=AX)
            v.drain()
            v.tensor_scalar(out=self.lgt[:], in0=self.gates[:], scalar1=self.gsc[:, 4:5],
                            scalar2=None, op0=ALU.is_ge)
            v.drain()
            v.tensor_tensor(out=self.lgt[:], in0=self.lgt[:], in1=self.iota8[:], op=ALU.mult)
            v.drain()
            v.reduce_sum(out=self.gsc[:, 5:6], in_=self.lgt[:], axis=AX)
            v.drain()
            v.tensor_scalar(out=self.lgt[:], in0=self.gates[:], scalar1=self.gsc[:, 4:5],
                            scalar2=-2.0, op0=ALU.is_ge, op1=ALU.mult)
            v.drain()
            v.tensor_tensor(out=self.lgt[:], in0=self.lgt[:], in1=self.gates[:], op=ALU.add)
            v.drain()
            v.reduce_max(out=self.gsc[:, 6:7], in_=self.lgt[:], axis=AX)
            v.drain()
            v.tensor_scalar(out=self.lgt[:], in0=self.lgt[:], scalar1=self.gsc[:, 6:7],
                            scalar2=None, op0=ALU.is_ge)
            v.drain()
            v.tensor_tensor(out=self.lgt[:], in0=self.lgt[:], in1=self.iota8[:], op=ALU.mult)
            v.drain()
            v.reduce_sum(out=self.gsc[:, 7:8], in_=self.lgt[:], axis=AX)
            v.drain()
            v.tensor_copy(out=self.gates[:, 0:1], in_=self.gsc[:, 5:6])
            v.tensor_copy(out=self.gates[:, 1:2], in_=self.gsc[:, 7:8])
            v.tensor_copy(out=self.gates[:, 2:3], in_=self.gsc[:, 4:5])
            v.drain()
            v.nop().then_inc(vs.h, 1)
        vs.n += 1
        VS_SEL = vs.n

        @block.tensor
        def _(t_):
            t_.wait_ge(vs.h, VS_SEL)
            t_.matmul(out=ps_o[0][:, 0:3], lhsT=self.ones1f[:], rhs=self.gates[:, 0:3],
                      start=True, stop=True).then_inc(ts.h, 1)
        ts.n += 1
        TS_SELB = ts.n

        @block.vector
        def _(v):
            v.wait_ge(ts.h, TS_SELB)
            v.tensor_copy(out=self.gsel[:, 0:1], in_=ps_o[0][:, 2:3])
            v.tensor_scalar(out=self.gsel[:, 1:2], in0=ps_o[0][:, 2:3], scalar1=-1.0,
                            scalar2=1.0, op0=ALU.mult, op1=ALU.add)
            for s in range(2):
                v.tensor_scalar(out=self.idxwf[:, NKC * s:NKC * s + 1], in0=ps_o[0][:, s:s + 1],
                                scalar1=float(D), scalar2=None, op0=ALU.mult)
            v.drain()
            for s in range(2):
                v.tensor_tensor(out=self.idxwf[:, NKC * s:NKC * s + 1],
                                in0=self.idxwf[:, NKC * s:NKC * s + 1],
                                in1=self.iotap[:], op=ALU.add)
            v.drain()
            for s in range(2):
                for kc in range(1, NKC):
                    v.tensor_scalar(out=self.idxwf[:, NKC * s + kc:NKC * s + kc + 1],
                                    in0=self.idxwf[:, NKC * s:NKC * s + 1],
                                    scalar1=float(128 * kc), scalar2=None, op0=ALU.add)
            v.drain()
            v.tensor_copy(out=self.idxw[:], in_=self.idxwf[:])
            v.tensor_copy(out=self.idxb[:, 0:1], in_=ps_o[0][:, 0:1])
            v.tensor_copy(out=self.idxb[:, 1:2], in_=ps_o[0][:, 1:2])
            v.drain()
            v.nop().then_inc(vs.h, 1)
        vs.n += 1
        self.VS_IDX = vs.n
        self.pso_free = (self.vs, self.VS_IDX)

        if self.tap == "gates" and b == 0:
            self.tap_dump(self.gsel)

    def _ffn(self, b):
        block = self.block
        dx, dw, ts, vs, ss = self.dx, self.dw, self.ts, self.vs, self.ss
        ps_s, ps_o = self.ps_s, self.ps_o
        VS_IDX = self.VS_IDX
        TAO_END = self.TAO_END
        NS = 2

        T0 = ts.n
        V0 = vs.n
        S0 = ss.n
        DW0 = dw.n
        self.VF0 = V0
        self.FFN_VPE = 2 * NT

        tpe = NT
        vpe = 2 * NT
        spe = NT
        IOA = bass.IndirectOffsetOnAxis

        @block.gpsimd
        def _(gp):
            for s in range(NS):
                gp.wait_ge(ts.h, TAO_END)
                gp.wait_ge(vs.h, VS_IDX)
                if s == 1:
                    gp.wait_ge(vs.h, V0 + 2 * (NT - 1) + 1)
                for kc in range(NKC):
                    gp.indirect_dma_start(
                        out=self.wexp[s][:, D * kc:D * (kc + 1)], out_offset=None,
                        in_=self.weT_h[:, :],
                        in_offset=IOA(ap=self.idxw[:, NKC * s + kc:NKC * s + kc + 1], axis=0),
                    ).then_inc(dw.h, 16)
                gp.indirect_dma_start(
                    out=self.bbc[:], out_offset=None, in_=self.be_h[:, :],
                    in_offset=IOA(ap=self.idxb[:, s:s + 1], axis=0),
                ).then_inc(dw.h, 16)
        dw.n += NS * (NKC + 1) * 16

        @block.tensor
        def _(t_):
            for s in range(NS):
                t_.wait_ge(dw.h, DW0 + 16 * (NKC + 1) * (s + 1))
                if s == 0:
                    t_.wait_ge(vs.h, VS_IDX)
                for tc in range(NT):
                    lo = 1024 * (tc % 2)
                    if s == 0 and tc < 2:
                        pass
                    elif tc >= 2:
                        t_.wait_ge(vs.h, V0 + s * vpe + 2 * (tc - 2) + 1)
                    else:
                        t_.wait_ge(vs.h, V0 + (s - 1) * vpe + 2 * (NT - 2 + tc) + 1)
                    w = self.wexp[s]
                    for half in range(2):
                        for kc in range(NKC):
                            mm = t_.matmul(out=ps_s[:, lo + 512 * half:lo + 512 * half + 512],
                                           lhsT=self.xnT[:, S * kc + 128 * tc:S * kc + 128 * (tc + 1)],
                                           rhs=w[:, D * kc + 512 * half:D * kc + 512 * half + 512],
                                           start=(kc == 0), stop=(kc == NKC - 1))
                    mm.then_inc(ts.h, 1)
        ts.n += NS * tpe

        @block.vector
        def _(v):
            v.wait_ge(vs.h, VS_IDX)
            for s in range(NS):
                for tc in range(NT):
                    lo = 1024 * (tc % 2)
                    v.wait_ge(ts.h, T0 + s * tpe + tc + 1)
                    if tc == 0:
                        v.wait_ge(dw.h, DW0 + 16 * (NKC + 1) * (s + 1))
                    if tc >= 2:
                        v.wait_ge(ss.h, S0 + s * spe + (tc - 2) + 1)
                    elif s > 0:
                        v.wait_ge(ss.h, S0 + (s - 1) * spe + (NT - 2 + tc) + 1)
                    v.tensor_tensor(out=self.ftmp[tc % 2][:, 0:512],
                                    in0=ps_s[:, lo:lo + 512], in1=self.bbc[:, 0:512], op=ALU.add)
                    v.tensor_tensor(out=self.ftmp[tc % 2][:, 512:1024],
                                    in0=ps_s[:, lo + 512:lo + 1024], in1=self.bbc[:, 512:1024],
                                    op=ALU.add).then_inc(vs.h, 1)
                    v.wait_ge(ss.h, S0 + s * spe + tc + 1)
                    v.scalar_tensor_tensor(out=self.x2[:, D * tc:D * (tc + 1)],
                                           in0=self.ftmp[tc % 2][:], scalar=self.gsel[:, s:s + 1],
                                           in1=self.x2[:, D * tc:D * (tc + 1)],
                                           op0=ALU.mult, op1=ALU.add).then_inc(vs.h, 1)
        vs.n += NS * vpe

        @block.scalar
        def _(s_):
            for s in range(NS):
                for tc in range(NT):
                    s_.wait_ge(vs.h, V0 + s * vpe + 2 * tc + 1)
                    s_.activation(out=self.ftmp[tc % 2][:], in_=self.ftmp[tc % 2][:],
                                  func=AF.Gelu).then_inc(ss.h, 1)
        ss.n += NS * spe



def _consts():
    p = np.arange(128)[:, None].astype(np.float32)
    g = np.arange(2048)[None, :].astype(np.float32)
    d0mv = (p - g).astype(np.float32)
    f = np.arange(128)[None, :].astype(np.float32)
    g640v = np.where(p <= f, p - f, MASKVAL).astype(np.float32)
    r32 = np.float32(_RATIO)
    t0mv = (d0mv * r32).astype(np.float32)
    gt640v = (g640v * r32).astype(np.float32)
    idf16 = np.eye(128, dtype=np.float16)
    idbf = np.eye(128, dtype=ml_dtypes.bfloat16)
    return d0mv.astype(np.float16), g640v.astype(np.float16), t0mv, gt640v, idf16, idbf


def make_in_maps(L0_attn_x, L0_attn_q_w, L0_attn_k_w, L0_attn_v_w, L0_attn_o_w,
                 L0_ffn_gate_w, L0_ffn_gate_b, L0_ffn_expert_w, L0_ffn_expert_b):
    x = np.ascontiguousarray(np.asarray(L0_attn_x, dtype=np.float32))
    wq = np.ascontiguousarray(np.asarray(L0_attn_q_w, dtype=np.float32))
    wk = np.ascontiguousarray(np.asarray(L0_attn_k_w, dtype=np.float32))
    wv = np.ascontiguousarray(np.asarray(L0_attn_v_w, dtype=np.float32))
    wo = np.ascontiguousarray(np.asarray(L0_attn_o_w, dtype=np.float32))
    gwT = np.ascontiguousarray(np.asarray(L0_ffn_gate_w, dtype=np.float32).T)
    gb = np.ascontiguousarray(np.asarray(L0_ffn_gate_b, dtype=np.float32)[None, :])
    weT = np.ascontiguousarray(
        np.asarray(L0_ffn_expert_w, dtype=np.float32).transpose(0, 2, 1).reshape(E * D, D))
    be = np.ascontiguousarray(np.asarray(L0_ffn_expert_b, dtype=np.float32))
    iotap = np.arange(128, dtype=np.float32)[:, None]
    iota8 = np.arange(E, dtype=np.float32)[None, :]
    d0mv, g640v, t0mv, gt640v, idf16, idbf = _consts()
    in_maps = []
    for c in range(8):
        in_maps.append({
            "x": x[2 * c:2 * c + 2],
            "wq": wq, "wk": wk, "wv": wv, "wo": wo,
            "gwT": gwT, "gb": gb, "weT": weT, "be": be,
            "idf16": idf16, "idbf": idbf, "d0m": d0mv, "g640": g640v,
            "t0m": t0mv, "gt640": gt640v,
            "iotap": iotap, "iota8": iota8,
        })
    return in_maps


def kernel(L0_attn_x, L0_attn_q_w, L0_attn_k_w, L0_attn_v_w, L0_attn_o_w,
           L0_ffn_gate_w, L0_ffn_gate_b, L0_ffn_expert_w, L0_ffn_expert_b):
    in_maps = make_in_maps(L0_attn_x, L0_attn_q_w, L0_attn_k_w, L0_attn_v_w, L0_attn_o_w,
                           L0_ffn_gate_w, L0_ffn_gate_b, L0_ffn_expert_w, L0_ffn_expert_b)
    nc = Builder().build()
    res = run_bass_kernel_spmd(nc, in_maps, core_ids=list(range(8)))
    out = np.concatenate([res.results[c]["out"] for c in range(8)], axis=0)
    return np.ascontiguousarray(out.astype(np.float32))


# version 29
# speedup vs baseline: 1.0827x; 1.0080x over previous
import numpy as np
import ml_dtypes

import concourse.bass as bass
import concourse.mybir as mybir
from concourse.bass_utils import run_bass_kernel_spmd
from contextlib import ExitStack

F32 = mybir.dt.float32
BF16 = mybir.dt.bfloat16
F16 = mybir.dt.float16
AF = mybir.ActivationFunctionType
ALU = mybir.AluOpType
AX = mybir.AxisListType.X

D = 1024
S = 2048
NPAIR = 8
NT = 16
NKC = 8
E = 8
EPS = 1e-5
MASKVAL = -57344.0
_RATIO = 2.0 ** (-8.0 / 16)
SLOPES = [float(np.float32(_RATIO ** (h + 1))) for h in range(16)]


class Builder:
    def __init__(self, tap=None):
        self.tap = tap
        nc = bass.Bass("TRN2", target_bir_lowering=False, debug=False)
        self.nc = nc
        self.x_h = nc.dram_tensor("x", [2, S, D], F32, kind="ExternalInput").ap()
        self.wq_h = nc.dram_tensor("wq", [D, D], F32, kind="ExternalInput").ap()
        self.wk_h = nc.dram_tensor("wk", [D, D], F32, kind="ExternalInput").ap()
        self.wv_h = nc.dram_tensor("wv", [D, D], F32, kind="ExternalInput").ap()
        self.wo_h = nc.dram_tensor("wo", [D, D], F32, kind="ExternalInput").ap()
        self.gwT_h = nc.dram_tensor("gwT", [D, E], F32, kind="ExternalInput").ap()
        self.gb_h = nc.dram_tensor("gb", [1, E], F32, kind="ExternalInput").ap()
        self.weT_h = nc.dram_tensor("weT", [E * D, D], F32, kind="ExternalInput").ap()
        self.be_h = nc.dram_tensor("be", [E, D], F32, kind="ExternalInput").ap()
        self.iotap_h = nc.dram_tensor("iotap", [128, 1], F32, kind="ExternalInput").ap()
        self.iota8_h = nc.dram_tensor("iota8", [1, E], F32, kind="ExternalInput").ap()
        self.idf16_h = nc.dram_tensor("idf16", [128, 128], F16, kind="ExternalInput").ap()
        self.idbf_h = nc.dram_tensor("idbf", [128, 128], BF16, kind="ExternalInput").ap()
        self.d0m_h = nc.dram_tensor("d0m", [128, 2048], F16, kind="ExternalInput").ap()
        self.g640_h = nc.dram_tensor("g640", [128, 128], F16, kind="ExternalInput").ap()
        self.t0m_h = nc.dram_tensor("t0m", [128, 2048], F32, kind="ExternalInput").ap()
        self.gt640_h = nc.dram_tensor("gt640", [128, 128], F32, kind="ExternalInput").ap()
        self.out_h = nc.dram_tensor("out", [2, S, D], F32, kind="ExternalOutput").ap()
        if tap is not None:
            self.dbg_h = nc.dram_tensor("dbg", [128, 16384], F32, kind="ExternalOutput").ap()

    def build(self):
        nc = self.nc
        with ExitStack() as st_:
            e = st_.enter_context
            self.idf16 = e(nc.sbuf_tensor("idf16s", [128, 128], F16))
            self.d0m = e(nc.sbuf_tensor("d0ms", [128, 2048], F16))
            self.g640 = e(nc.sbuf_tensor("g640s", [128, 128], F16))
            self.t0m = e(nc.sbuf_tensor("t0ms", [128, 2048], F32))
            self.gt640 = e(nc.sbuf_tensor("gt640s", [128, 128], F32))
            self.gwT = e(nc.sbuf_tensor("gwTs", [128, NKC * E], F32))
            self.gb = e(nc.sbuf_tensor("gbs", [1, E], F32))
            self.iotap = e(nc.sbuf_tensor("iotap_s", [128, 1], F32))
            self.iota8 = e(nc.sbuf_tensor("iota8_s", [1, E], F32))
            self.onesbf = e(nc.sbuf_tensor("onesbf", [128, 1], F16))
            self.ones1f = e(nc.sbuf_tensor("ones1f", [1, 128], F32))
            self.one11 = e(nc.sbuf_tensor("one11", [1, 1], F32))
            self.xnT = e(nc.sbuf_tensor("xnT", [128, NKC * S], F16))
            self.x2 = e(nc.sbuf_tensor("x2", [128, NT * D], F32))
            self.ostg = [e(nc.sbuf_tensor(f"ostg{i}", [128, 128], F16)) for i in range(2)]
            self.wpair = [[e(nc.sbuf_tensor(f"wp{i}{m}", [128, NKC * 128], F16))
                           for m in range(3)] for i in range(2)]
            self.wbuf = e(nc.sbuf_tensor("wbuf", [128, NKC * D], F16))
            self.wbuf2 = e(nc.sbuf_tensor("wbuf2", [128, NKC * D], F16))
            self.qt = e(nc.sbuf_tensor("qts", [128, S], F16))
            self.kt = e(nc.sbuf_tensor("kts", [128, S], F16))
            self.vp = e(nc.sbuf_tensor("vps", [128, NT * 130], F16))
            self.spp = [[e(nc.sbuf_tensor(f"spp{par}{h}", [128, 2048], F16)) for h in range(2)]
                        for par in range(2)]
            self.ftmp = [e(nc.sbuf_tensor(f"ftmp{i}", [128, D], F32)) for i in range(2)]
            self.pT = [e(nc.sbuf_tensor(f"pT{h}", [128, 2048], F16)) for h in range(2)]
            self.recip = e(nc.sbuf_tensor("recips", [128, 4], F32))
            self.xtile = self.ftmp
            self.bnst = e(nc.sbuf_tensor("bnst", [128, 12 * NT], F32))
            self.mvall = e(nc.sbuf_tensor("mvall", [128, 2 * NT], F32))
            self.rstd = e(nc.sbuf_tensor("rstd", [128, NT], F32))
            self.negmr = e(nc.sbuf_tensor("negmr", [128, NT], F32))
            self.lntmp = e(nc.sbuf_tensor("lntmp", [128, NT], F32))
            self.xntm = e(nc.sbuf_tensor("xntm", [128, D], F16))
            self.pooledT = e(nc.sbuf_tensor("pooledT", [128, 8], F32))
            self.lgt = e(nc.sbuf_tensor("lgt", [1, E], F32))
            self.gsc = e(nc.sbuf_tensor("gsc", [1, 8], F32))
            self.gates = e(nc.sbuf_tensor("gates", [1, E], F32))
            self.gsel = e(nc.sbuf_tensor("gsel", [128, 2], F32))
            self.idxw = e(nc.sbuf_tensor("idxw", [128, 2 * NKC], mybir.dt.uint32))
            self.idxwf = e(nc.sbuf_tensor("idxwf", [128, 2 * NKC], F32))
            self.idxb = e(nc.sbuf_tensor("idxb", [128, 2], mybir.dt.uint32))
            self.wexp = [self.wbuf, self.wbuf2]
            self.bbc = e(nc.sbuf_tensor("bbc", [128, D], F16))
            self.pooled_sb = self.ftmp[0][0:1, 0:D]
            self.ps_s = e(nc.psum_tensor("ps_s", [128, 2048], F32))
            self.ps_t = [e(nc.psum_tensor(f"ps_t{h}", [128, 1024], F16)) for h in range(2)]
            self.ps_o = [e(nc.psum_tensor(f"ps_o{h}", [128, 512], F32)) for h in range(2)]

            class Led:
                def __init__(s, name, dma=False):
                    s.h = e(nc.semaphore(name))
                    s.n = 0
                    s.step = 16 if dma else 1

            self.dx = Led("dx", dma=True)
            self.dw = Led("dw", dma=True)
            self.ts = Led("ts")
            self.vs = Led("vs")
            self.ss = Led("ss")
            self.so = Led("so", dma=True)
            self.dxx = Led("dxx", dma=True)
            self.xs4 = [Led(f"xs{i}", dma=True) for i in range(4)]

            self.block = e(nc.Block())
            self.pso_free = (self.vs, 0)
            self.exp_last = [0, 0]
            self.v_last_stt = 0
            self.s_last_oevac = 0
            self.done = False
            self.xload_base = {}

            self._consts()
            for b in range(2):
                self._seq(b)
                if self.done:
                    break

            SO_END = self.so.n

            @self.block.sync
            def _(sync):
                sync.wait_ge(self.so.h, SO_END)
        return nc

    def ot_ap(self, pp, cols):
        base = self.wbuf if pp < 4 else self.wbuf2
        off = 2048 * (pp % 4)
        return base[:, off + cols[0]:off + cols[1]]

    def tap_dump(self, sbuf_ap, nrows=128):
        dbg = self.dbg_h
        vsn, ssn, tsn = self.vs.n, self.ss.n, self.ts.n
        dxn = self.dx.n
        fs = min(sbuf_ap.shape[-1], 16384)

        @self.block.gpsimd
        def _(gp):
            gp.wait_ge(self.vs.h, vsn)
            gp.wait_ge(self.ss.h, ssn)
            gp.wait_ge(self.ts.h, tsn)
            gp.dma_start(out=dbg[0:nrows, 0:fs], in_=sbuf_ap[0:nrows, 0:fs]).then_inc(self.so.h, 16)
        self.so.n += 16
        self.done = True

    def _consts(self):
        dx, vs = self.dx, self.vs

        @self.block.sync
        def _(sync):
            sync.dma_start(out=self.idf16[:], in_=self.idf16_h[:, :]).then_inc(dx.h, 16)
            sync.dma_start(out=self.d0m[:], in_=self.d0m_h[:, :]).then_inc(dx.h, 16)
            sync.dma_start(out=self.g640[:], in_=self.g640_h[:, :]).then_inc(dx.h, 16)
            sync.dma_start(out=self.t0m[:], in_=self.t0m_h[:, :]).then_inc(dx.h, 16)
            sync.dma_start(out=self.gt640[:], in_=self.gt640_h[:, :]).then_inc(dx.h, 16)
            sync.dma_start(out=self.gwT[:].rearrange("p (kc e) -> p kc e", kc=NKC),
                           in_=self.gwT_h.rearrange("(kc p) e -> p kc e", p=128)).then_inc(dx.h, 16)
            sync.dma_start(out=self.gb[:], in_=self.gb_h[:, :]).then_inc(dx.h, 16)
            sync.dma_start(out=self.iotap[:], in_=self.iotap_h[:, :]).then_inc(dx.h, 16)
            sync.dma_start(out=self.iota8[:], in_=self.iota8_h[:, :]).then_inc(dx.h, 16)
        dx.n += 9 * 16
        self.CONST_DX = dx.n

        CONST_DX = dx.n

        @self.block.vector
        def _(v):
            v.wait_ge(dx.h, CONST_DX)
            v.memset(self.onesbf[:], 1.0)
            v.memset(self.ones1f[:], 1.0)
            v.memset(self.one11[:], 1.0)
            vpv = self.vp[:].rearrange("p (t c) -> p t c", c=130)
            v.memset(vpv[:, :, 64:65], 1.0)
            m = v.memset(vpv[:, :, 129:130], 1.0)
            m.then_inc(vs.h, 1)
        vs.n += 1
        self.VS_CONST = vs.n

    def _seq(self, b):
        nc, block = self.nc, self.block
        dx, dw, ts, vs, ss, so = self.dx, self.dw, self.ts, self.vs, self.ss, self.so
        wbig = lambda w: w.rearrange("(kc p) d -> p (kc d)", p=128)

        TS_B0, VS_B0, SS_B0, SO_B0 = ts.n, vs.n, ss.n, so.n

        self.TS_B0 = TS_B0
        DW_QKV = 0

        VA0 = vs.n
        SA0 = ss.n
        pst0 = self.ps_t[0]

        dxx = self.dxx
        xs4 = self.xs4
        if b == 0:
            XB = dxx.n

            @block.sync
            def _(sync):
                for t in range(NT):
                    if t >= 8:
                        sync.wait_ge(vs.h, VA0 + (t - 7))
                    sync.dma_start(out=self.x2[:, D * t:D * (t + 1)],
                                   in_=self.x_h[b, 128 * t:128 * (t + 1), :]).then_inc(xs4[t % 4].h, 16)
            for i in range(4):
                xs4[i].n += 4 * 16
            self.xload_base[0] = XB
        XB = self.xload_base[b]

        @block.vector
        def _(v):
            for t in range(NT):
                if b == 0:
                    v.wait_ge(xs4[t % 4].h, 16 * min(t // 4 + 2, 4))
                else:
                    v.wait_ge(xs4[t % 4].h, 128)
                xt = self.x2[:, D * t:D * (t + 1)]
                v.bn_stats(out=self.bnst[:, 12 * t:12 * t + 6], in_=xt[:, 0:512])
                v.bn_stats(out=self.bnst[:, 12 * t + 6:12 * t + 12], in_=xt[:, 512:1024]).then_inc(vs.h, 1)
            v.drain()
            for t in range(NT):
                v.bn_aggr(out=self.mvall[:, 2 * t:2 * t + 2], in_=self.bnst[:, 12 * t:12 * t + 12])
            v.drain()
            v.tensor_scalar(out=self.lntmp[:], in0=self.mvall[:, 1:2 * NT:2],
                            scalar1=EPS, scalar2=None, op0=ALU.add)
            v.drain()
            v.reciprocal(out=self.rstd[:], in_=self.lntmp[:]).then_inc(vs.h, 1)
        vs.n += NT + 1
        VA_END = vs.n

        @block.scalar
        def _(s):
            s.wait_ge(vs.h, VA_END)
            s.activation(out=self.rstd[:], in_=self.rstd[:], func=AF.Sqrt).then_inc(ss.h, 1)
        ss.n += 1
        SS_RSTD = ss.n

        @block.vector
        def _(v):
            v.wait_ge(ss.h, SS_RSTD)
            v.scalar_tensor_tensor(out=self.negmr[:], in0=self.mvall[:, 0:2 * NT:2],
                                   scalar=-1.0, in1=self.rstd[:],
                                   op0=ALU.mult, op1=ALU.mult).then_inc(vs.h, 1)
        vs.n += 1
        VS_NEGMR = vs.n

        SB0 = ss.n
        TB0 = ts.n
        pst0 = self.ps_t[0]

        @block.scalar
        def _(s):
            for t in range(NT):
                if t == 0:
                    s.wait_ge(vs.h, VS_NEGMR)
                    s.wait_ge(ts.h, TS_B0)
                s.activation(out=self.xntm[:], in_=self.x2[:, D * t:D * (t + 1)], func=AF.Identity,
                             scale=self.rstd[:, t:t + 1], bias=self.negmr[:, t:t + 1]).then_inc(ss.h, 1)
                s.wait_ge(ts.h, TB0 + t + 1)
                dst = self.xnT[:].rearrange("p (kc s) -> p kc s", kc=NKC)[:, :, 128 * t:128 * (t + 1)]
                src = pst0[:, 0:1024].rearrange("p (kc f) -> p kc f", kc=NKC)
                s.copy(out=dst, in_=src).then_inc(ss.h, 1)
        ss.n += 2 * NT
        SB_END = ss.n

        @block.tensor
        def _(t_):
            for t in range(NT):
                t_.wait_ge(ss.h, SB0 + 2 * t + 1)
                if t >= 1:
                    t_.wait_ge(ss.h, SB0 + 2 * t)
                for kc in range(NKC):
                    mm = t_.transpose(out=pst0[:, 128 * kc:128 * (kc + 1)],
                                      in_=self.xntm[:, 128 * kc:128 * (kc + 1)],
                                      identity=self.idf16[:])
                mm.then_inc(ts.h, 1)
        ts.n += NT
        TS_LN1_END = ts.n

        if self.tap == "xnT" and b == 0:
            self.tap_dump(self.xnT)
            return

        self._attention(b, DW_QKV, SB_END, TS_LN1_END)
        if self.done:
            return
        self._oproj(b, SO_B0)
        if self.done:
            return
        self._ln2_gates(b)
        if self.done:
            return
        self._ffn(b)
        if self.done:
            return

        VF0 = self.VF0
        SO_BASE = so.n
        XBN = self.dxx.n
        dxx = self.dxx

        @block.sync
        def _(sync):
            for t in range(NT):
                sync.wait_ge(vs.h, VF0 + 1 * (2 * NT) + 2 * t + 2)
                sync.dma_start(out=self.out_h[b, 128 * t:128 * (t + 1), :],
                               in_=self.x2[:, D * t:D * (t + 1)]).then_inc(so.h, 16)
                if b == 0:
                    sync.wait_ge(so.h, SO_BASE + 16 * (t + 1))
                    sync.dma_start(out=self.x2[:, D * t:D * (t + 1)],
                                   in_=self.x_h[1, 128 * t:128 * (t + 1), :]).then_inc(self.xs4[t % 4].h, 16)
        so.n += NT * 16
        if b == 0:
            for i in range(4):
                self.xs4[i].n += 4 * 16
            self.xload_base[1] = XBN

    def _attention(self, b, DW_QKV, SS_LN1_END, TS_LN1_END):
        block = self.block
        dx, dw, ts, vs, ss = self.dx, self.dw, self.ts, self.vs, self.ss
        ps_s, ps_t, ps_o = self.ps_s, self.ps_t, self.ps_o
        qt, kt, vp, spp, pT = self.qt, self.kt, self.vp, self.spp, self.pT

        if not hasattr(self, "proj_end_marks"):
            self.proj_end_marks = {}
        wsrc = [self.wq_h, self.wk_h, self.wv_h]
        DWB = dw.n
        for pp in range(NPAIR):
            prev_mark = self.proj_end_marks.get(pp - 2, self.TS_B0)

            @block.gpsimd
            def _(gp):
                gp.wait_ge(ts.h, prev_mark)
                for m in range(3):
                    src_ap = wsrc[m][:, 128 * pp:128 * (pp + 1)].rearrange(
                        "(kc p) d -> p kc d", p=128)
                    dst_ap = self.wpair[pp % 2][m][:].rearrange("p (kc d) -> p kc d", kc=NKC)
                    gp.dma_start(out=dst_ap, in_=src_ap).then_inc(dw.h, 16)
            dw.n += 48

            TP0 = ts.n
            SP0 = ss.n
            V_LAST_STT = self.v_last_stt
            PSO_FREE = self.pso_free
            S_LAST_OEVAC = self.s_last_oevac
            DW_PAIR = DWB + 48 * min(pp + 2, NPAIR)
            wp = self.wpair[pp % 2]

            @block.tensor
            def _(t_):
                t_.wait_ge(dw.h, DW_PAIR)
                if pp == 0:
                    t_.wait_ge(ss.h, SS_LN1_END)
                for i in range(8):
                    wmat = wp[0] if i < 4 else wp[1]
                    tc4 = i % 4
                    if i >= 4:
                        t_.wait_ge(ss.h, SP0 + (i - 4) + 1)
                    elif pp > 0:
                        t_.wait_ge(vs.h, V_LAST_STT)
                    bank = ps_s[:, 512 * tc4:512 * (tc4 + 1)]
                    for kc in range(NKC):
                        mm = t_.matmul(out=bank,
                                       lhsT=wmat[:, 128 * kc:128 * (kc + 1)],
                                       rhs=self.xnT[:, S * kc + 512 * tc4:S * kc + 512 * (tc4 + 1)],
                                       start=(kc == 0), stop=(kc == NKC - 1))
                    mm.then_inc(ts.h, 1)
                for tc in range(NT):
                    if tc >= 2:
                        t_.wait_ge(ss.h, SP0 + 8 + (tc - 2) + 1)
                    else:
                        led, cnt = PSO_FREE
                        t_.wait_ge(led.h, cnt)
                        t_.wait_ge(ss.h, S_LAST_OEVAC)
                        if pp == 0:
                            t_.wait_ge(vs.h, self.VS_CONST)
                    bank = ps_o[tc % 2][:, 0:128]
                    for kc in range(NKC):
                        mm = t_.matmul(out=bank,
                                       lhsT=self.xnT[:, S * kc + 128 * tc:S * kc + 128 * (tc + 1)],
                                       rhs=wp[2][:, 128 * kc:128 * (kc + 1)],
                                       start=(kc == 0), stop=(kc == NKC - 1))
                    mm.then_inc(ts.h, 1)
            ts.n += 24
            self.proj_end_marks[pp] = ts.n

            @block.scalar
            def _(s):
                for i in range(8):
                    s.wait_ge(ts.h, TP0 + i + 1)
                    dst = (qt if i < 4 else kt)[:, 512 * (i % 4):512 * (i % 4 + 1)]
                    if i < 4:
                        s.activation(out=dst, in_=ps_s[:, 512 * (i % 4):512 * (i % 4) + 512],
                                     func=AF.Copy, scale=0.125).then_inc(ss.h, 1)
                    else:
                        s.copy(out=dst, in_=ps_s[:, 512 * (i % 4):512 * (i % 4) + 512]).then_inc(ss.h, 1)
                for tc in range(NT):
                    s.wait_ge(ts.h, TP0 + 8 + tc + 1)
                    src = ps_o[tc % 2][:, 0:128].rearrange("p (h f) -> p h f", h=2)
                    dst = vp[:, 130 * tc:130 * (tc + 1)].rearrange("p (h f) -> p h f", f=65)[:, :, 0:64]
                    s.copy(out=dst, in_=src).then_inc(ss.h, 1)
            ss.n += 24
            SS_PROJ_END = ss.n

            slope_min = SLOPES[2 * pp + 1]
            wb = int(np.ceil((40.0 / slope_min) / 128.0)) + 1
            nchs, nblks, blos, tailbs = [], [], [], []
            for ti in range(NT):
                b_lo = max(0, (ti + 1) - wb)
                nblk_eff = (ti + 1) - b_lo
                tailb = ((nblk_eff - 1) % 4) + 1
                nf = (nblk_eff - tailb) // 4
                blos.append(b_lo)
                nchs.append(nf + 1)
                nblks.append(nblk_eff)
                tailbs.append(tailb)

            def group_cols(ti, b):
                blo, nf, tailb = blos[ti], nchs[ti] - 1, tailbs[ti]
                Bt = blo + 4 * nf
                if b >= Bt:
                    return 512 * nf + 128 * (ti - b)
                c = (b - blo) // 4
                g_hi = blo + 4 * c + 3
                return 512 * c + 128 * (g_hi - b)

            t = ts.n
            t_qk, t_pv, t_otr = {}, {}, {}
            t_qk[0] = t; t += nchs[0]
            for ti in range(NT):
                if ti + 1 < NT:
                    t_qk[ti + 1] = t; t += nchs[ti + 1]
                if ti >= 1:
                    t_otr[ti - 1] = t; t += 1
                t_pv[ti] = t; t += 2
            t_otr[NT - 1] = t; t += 1
            TS_END = t

            v = vs.n
            v_stt, v_rec, v_evac = {}, {}, {}
            v_stt[0] = v; v += 2 * nchs[0]
            for ti in range(NT):
                if ti >= 1:
                    v_evac[ti - 1] = v; v += 2
                if ti + 1 < NT:
                    v_stt[ti + 1] = v; v += 2 * nchs[ti + 1]
                v_rec[ti] = v; v += 2
            v_evac[NT - 1] = v; v += 2
            VS_END = v

            s = ss.n
            s_exp, s_otev = {}, {}
            for ti in range(NT):
                if ti >= 1:
                    s_otev[ti - 1] = s; s += 1
                s_exp[ti] = s; s += 2
            s_otev[NT - 1] = s; s += 1
            SS_END = s

            qk_wait, qk_pair = {}, {}
            if not hasattr(self, "sbank_mark2"):
                self.sbank_mark2 = [0, 0]
            sbm = list(self.sbank_mark2)
            gc_ = getattr(self, "gchunk", 0)
            for ti in range(NT):
                for c in range(nchs[ti]):
                    pr = gc_ % 2
                    qk_pair[(ti, c)] = pr
                    qk_wait[(ti, c)] = sbm[pr]
                    sbm[pr] = min(v_stt[ti] + 2 * c + 3, v_stt[ti] + 2 * nchs[ti])
                    gc_ += 1
            self.sbank_mark2 = sbm
            self.gchunk = gc_

            if not hasattr(self, "pso_free_v"):
                self.pso_free_v = [0, 0]
            PSO_FREE_V = list(self.pso_free_v)
            OTR_PREV = getattr(self, "otr_prev", 0)
            OTEV_PREV = getattr(self, "otev_prev", 0)
            qk_ss_seen = [0]

            def emit_qk(t_, ti):
                blo, nf, tailb, nch = blos[ti], nchs[ti] - 1, tailbs[ti], nchs[ti]
                for c in range(nch):
                    t_.wait_ge(vs.h, qk_wait[(ti, c)])
                    bank_free = SP0 + 8
                    qt_need = SP0 + (128 * ti) // 512 + 1
                    if c < nf:
                        bs = list(range(blo + 4 * c, blo + 4 * c + 4))
                    else:
                        bs = list(range(blo + 4 * nf, ti + 1))
                    kt_need = SP0 + 4 + (128 * (bs[-1] + 1) - 1) // 512 + 1
                    ssw = max(bank_free, qt_need, kt_need)
                    if ssw > qk_ss_seen[0]:
                        t_.wait_ge(ss.h, ssw)
                        qk_ss_seen[0] = ssw
                    lo = 1024 * qk_pair[(ti, c)]
                    for b in bs:
                        colb = group_cols(ti, b) - 512 * c
                        mm = t_.matmul(out=ps_s[:, lo + colb:lo + colb + 128],
                                       lhsT=kt[0:64, 128 * b:128 * (b + 1)],
                                       rhs=qt[0:64, 128 * ti:128 * (ti + 1)],
                                       start=True, stop=True, tile_position=(0, 0))
                        mm = t_.matmul(out=ps_s[:, lo + 512 + colb:lo + 512 + colb + 128],
                                       lhsT=kt[64:128, 128 * b:128 * (b + 1)],
                                       rhs=qt[64:128, 128 * ti:128 * (ti + 1)],
                                       start=True, stop=True, tile_position=(64, 0))
                    mm.then_inc(ts.h, 1)

            @block.tensor
            def _(t_):
                emit_qk(t_, 0)
                for ti in range(NT):
                    if ti + 1 < NT:
                        emit_qk(t_, ti + 1)
                    if ti >= 1:
                        t_.wait_ge(vs.h, v_evac[ti - 1] + 2)
                        pfree = (s_otev[ti - 2] + 1) if ti >= 2 else OTEV_PREV
                        if pfree > 0:
                            t_.wait_ge(ss.h, pfree)
                        t_.transpose(out=self.ps_t[0][:, 0:128], in_=self.ostg[(ti - 1) % 2][:],
                                     identity=self.idf16[:]).then_inc(ts.h, 1)
                    nblk, blo = nblks[ti], blos[ti]
                    for hh in range(2):
                        t_.wait_ge(ss.h, s_exp[ti] + hh + 1)
                        t_.wait_ge(vs.h, PSO_FREE_V[hh])
                        pso = ps_o[hh]
                        for bb in range(nblk):
                            ab = bb + blo
                            col = group_cols(ti, ab)
                            mm = t_.matmul(out=pso[:, 0:65],
                                           lhsT=pT[hh][:, col:col + 128],
                                           rhs=vp[:, 130 * ab + 65 * hh:130 * ab + 65 * hh + 65],
                                           start=(bb == 0), stop=(bb == nblk - 1))
                        mm.then_inc(ts.h, 1)
                    PSO_FREE_V[0] = v_evac[ti] + 1
                    PSO_FREE_V[1] = v_evac[ti] + 2
                t_.wait_ge(vs.h, v_evac[NT - 1] + 2)
                t_.wait_ge(ss.h, s_otev[NT - 2] + 1)
                t_.transpose(out=self.ps_t[0][:, 0:128], in_=self.ostg[(NT - 1) % 2][:],
                             identity=self.idf16[:]).then_inc(ts.h, 1)
            ts.n = TS_END
            self.pso_free_v = list(PSO_FREE_V)

            @block.vector
            def _(v_):
                def emit_stt(ti):
                    blo, nf, tailb, nch = blos[ti], nchs[ti] - 1, tailbs[ti], nchs[ti]
                    for c in range(nch):
                        v_.wait_ge(ts.h, t_qk[ti] + c + 1)
                        lo = 1024 * qk_pair[(ti, c)]
                        sppb = 512 * c
                        for hh in range(2):
                            h = 2 * pp + hh
                            if h % 2 == 1:
                                tabf, tabg = self.d0m, self.g640
                                sc = 2.0 ** (-((h - 1) // 2 + 1))
                            else:
                                tabf, tabg = self.t0m, self.gt640
                                sc = 2.0 ** (-(h // 2))
                            if c < nf:
                                g_hi = blo + 4 * c + 3
                                u0 = 128 * (ti - g_hi)
                                m = v_.scalar_tensor_tensor(out=spp[ti % 2][hh][:, sppb:sppb + 512],
                                                            in0=tabf[:, u0:u0 + 512], scalar=sc,
                                                            in1=ps_s[:, lo + 512 * hh:lo + 512 * hh + 512],
                                                            op0=ALU.mult, op1=ALU.add)
                            else:
                                m = v_.scalar_tensor_tensor(out=spp[ti % 2][hh][:, sppb:sppb + 128],
                                                            in0=tabg[:], scalar=sc,
                                                            in1=ps_s[:, lo + 512 * hh:lo + 512 * hh + 128],
                                                            op0=ALU.mult, op1=ALU.add)
                                if tailb > 1:
                                    w = 128 * (tailb - 1)
                                    m = v_.scalar_tensor_tensor(out=spp[ti % 2][hh][:, sppb + 128:sppb + 128 + w],
                                                                in0=tabf[:, 128:128 + w], scalar=sc,
                                                                in1=ps_s[:, lo + 512 * hh + 128:lo + 512 * hh + 128 + w],
                                                                op0=ALU.mult, op1=ALU.add)
                            m.then_inc(vs.h, 1)

                def emit_evac(ti):
                    v_.wait_ge(vs.h, v_rec[ti] + 2)
                    if ti >= 2:
                        v_.wait_ge(ts.h, t_otr[ti - 2] + 1)
                    else:
                        v_.wait_ge(ts.h, OTR_PREV)
                    for hh in range(2):
                        v_.tensor_scalar(out=self.ostg[ti % 2][:, 64 * hh:64 * hh + 64],
                                         in0=ps_o[hh][:, 0:64],
                                         scalar1=self.recip[:, 2 * (ti % 2) + hh:2 * (ti % 2) + hh + 1],
                                         scalar2=None, op0=ALU.mult).then_inc(vs.h, 1)

                def emit_rec(ti):
                    for hh in range(2):
                        v_.wait_ge(ts.h, t_pv[ti] + hh + 1)
                        v_.reciprocal(out=self.recip[:, 2 * (ti % 2) + hh:2 * (ti % 2) + hh + 1],
                                      in_=ps_o[hh][:, 64:65]).then_inc(vs.h, 1)

                emit_stt(0)
                for ti in range(NT):
                    if ti >= 1:
                        emit_evac(ti - 1)
                    if ti + 1 < NT:
                        emit_stt(ti + 1)
                    emit_rec(ti)
                emit_evac(NT - 1)
            vs.n = VS_END
            self.v_last_stt = v_stt[NT - 1] + 2 * nchs[NT - 1]
            self.otr_prev = t_otr[NT - 1] + 1

            @block.scalar
            def _(s_):
                for ti in range(NT):
                    if ti >= 1:
                        s_.wait_ge(ts.h, t_otr[ti - 1] + 1)
                        s_.copy(out=self.ot_ap(pp, (128 * (ti - 1), 128 * ti)),
                                in_=self.ps_t[0][:, 0:128]).then_inc(ss.h, 1)
                    for hh in range(2):
                        s_.wait_ge(vs.h, v_stt[ti] + 2 * (nchs[ti] - 1) + hh + 1)
                        s_.activation(out=pT[hh][:, 0:128 * nblks[ti]],
                                      in_=spp[ti % 2][hh][:, 0:128 * nblks[ti]],
                                      func=AF.Exp).then_inc(ss.h, 1)
                s_.wait_ge(ts.h, t_otr[NT - 1] + 1)
                s_.copy(out=self.ot_ap(pp, (128 * (NT - 1), 128 * NT)),
                        in_=self.ps_t[0][:, 0:128]).then_inc(ss.h, 1)
            ss.n = SS_END
            self.s_last_oevac = SS_END
            self.otev_prev = s_otev[NT - 1] + 1
            self.pso_free = (self.vs, v_evac[NT - 1] + 2)
            self.spp_free = [s_exp[14] + 2, s_exp[15] + 2]

        if self.tap == "obuf" and b == 0:
            self.tap_dump(self.wbuf)
        if self.tap == "obuf2" and b == 0:
            self.tap_dump(self.wbuf2)

    def _oproj(self, b, SO_B0):
        block = self.block
        dx, dw, ts, vs, ss = self.dx, self.dw, self.ts, self.vs, self.ss
        ps_s, ps_o = self.ps_s, self.ps_o

        TS_ATTN_END = ts.n
        SS_ATTN_END = ss.n
        V_LAST_STT = self.v_last_stt
        wo_parts = [self.spp[0][0], self.spp[0][1], self.spp[1][0], self.spp[1][1]]

        SPP_FREE = self.spp_free

        @block.gpsimd
        def _(gp):
            for q in range(4):
                gp.wait_ge(ss.h, SPP_FREE[0] if q < 2 else SPP_FREE[1])
                srcw = self.wo_h[256 * q:256 * (q + 1), :].rearrange("(kc p) d -> p kc d", p=128)
                dstw = wo_parts[q][:].rearrange("p (kc d) -> p kc d", kc=2)
                gp.dma_start(out=dstw, in_=srcw).then_inc(dw.h, 16)
        dw.n += 64
        DW_O = dw.n

        def wo_ap(kc, cols):
            part = wo_parts[kc // 2]
            off = D * (kc % 2)
            return part[:, off + cols[0]:off + cols[1]]

        VC0 = vs.n

        TAO0 = ts.n

        @block.tensor
        def _(t_):
            t_.wait_ge(dw.h, DW_O)
            t_.wait_ge(vs.h, V_LAST_STT)
            t_.wait_ge(ss.h, SS_ATTN_END)
            for tc in range(NT):
                lo = 1024 * (tc % 2)
                if tc >= 2:
                    t_.wait_ge(vs.h, VC0 + (tc - 2) + 1)
                for half in range(2):
                    for kc in range(NKC):
                        mm = t_.matmul(out=ps_s[:, lo + 512 * half:lo + 512 * half + 512],
                                       lhsT=self.ot_ap(kc, (128 * tc, 128 * (tc + 1))),
                                       rhs=wo_ap(kc, (512 * half, 512 * half + 512)),
                                       start=(kc == 0), stop=(kc == NKC - 1))
                mm.then_inc(ts.h, 1)
        ts.n += NT

        @block.vector
        def _(v):
            for tc in range(NT):
                v.wait_ge(ts.h, TAO0 + tc + 1)
                lo = 1024 * (tc % 2)
                v.tensor_tensor(out=self.x2[:, D * tc:D * tc + 512],
                                in0=ps_s[:, lo:lo + 512],
                                in1=self.x2[:, D * tc:D * tc + 512], op=ALU.add)
                v.tensor_tensor(out=self.x2[:, D * tc + 512:D * tc + 1024],
                                in0=ps_s[:, lo + 512:lo + 1024],
                                in1=self.x2[:, D * tc + 512:D * tc + 1024], op=ALU.add).then_inc(vs.h, 1)
        vs.n += NT
        self.VC_END = vs.n
        self.TAO_END = ts.n

        if self.tap == "x2attn" and b == 0:
            self.tap_dump(self.x2)

    def _ln2_gates(self, b):
        block = self.block
        dx, dw, ts, vs, ss = self.dx, self.dw, self.ts, self.vs, self.ss
        ps_s, ps_o = self.ps_s, self.ps_o
        pst0 = self.ps_t[0]
        VC_END = self.VC_END

        VD0 = vs.n

        VC0 = VC_END - NT

        @block.vector
        def _(v):
            for t in range(NT):
                v.wait_ge(vs.h, VC0 + t + 1)
                xt = self.x2[:, D * t:D * (t + 1)]
                v.bn_stats(out=self.bnst[:, 12 * t:12 * t + 6], in_=xt[:, 0:512])
                v.bn_stats(out=self.bnst[:, 12 * t + 6:12 * t + 12], in_=xt[:, 512:1024]).then_inc(vs.h, 1)
            v.drain()
            for t in range(NT):
                v.bn_aggr(out=self.mvall[:, 2 * t:2 * t + 2], in_=self.bnst[:, 12 * t:12 * t + 12])
            v.drain()
            v.tensor_scalar(out=self.lntmp[:], in0=self.mvall[:, 1:2 * NT:2],
                            scalar1=EPS, scalar2=None, op0=ALU.add)
            v.drain()
            v.reciprocal(out=self.rstd[:], in_=self.lntmp[:]).then_inc(vs.h, 1)
        vs.n += NT + 1
        VD_STATS = vs.n

        @block.scalar
        def _(s):
            s.wait_ge(vs.h, VD_STATS)
            s.activation(out=self.rstd[:], in_=self.rstd[:], func=AF.Sqrt).then_inc(ss.h, 1)
        ss.n += 1
        SS_RSTD2 = ss.n

        @block.vector
        def _(v):
            v.wait_ge(ss.h, SS_RSTD2)
            v.scalar_tensor_tensor(out=self.negmr[:], in0=self.mvall[:, 0:2 * NT:2],
                                   scalar=-1.0, in1=self.rstd[:],
                                   op0=ALU.mult, op1=ALU.mult).then_inc(vs.h, 1)
        vs.n += 1
        VD_NEGMR = vs.n

        SD0 = ss.n
        TD0 = ts.n
        S_LAST_OEVAC = self.s_last_oevac

        TAO_END = self.TAO_END

        @block.scalar
        def _(s):
            for t in range(NT):
                if t == 0:
                    s.wait_ge(vs.h, VD_NEGMR)
                    s.wait_ge(ts.h, TAO_END)
                else:
                    s.wait_ge(ts.h, TD0 + 2 * t)
                s.activation(out=self.xntm[:], in_=self.x2[:, D * t:D * (t + 1)], func=AF.Identity,
                             scale=self.rstd[:, t:t + 1], bias=self.negmr[:, t:t + 1]).then_inc(ss.h, 1)
                s.wait_ge(ts.h, TD0 + 2 * t + 1)
                dst = self.xnT[:].rearrange("p (kc s) -> p kc s", kc=NKC)[:, :, 128 * t:128 * (t + 1)]
                src = pst0[:, 0:1024].rearrange("p (kc f) -> p kc f", kc=NKC)
                s.copy(out=dst, in_=src).then_inc(ss.h, 1)
        ss.n += 2 * NT
        SD_END = ss.n

        @block.tensor
        def _(t_):
            for t in range(NT):
                t_.wait_ge(ss.h, SD0 + 2 * t + 1)
                if t == 0:
                    t_.wait_ge(vs.h, VC_END)
                    t_.wait_ge(ss.h, S_LAST_OEVAC)
                else:
                    t_.wait_ge(ss.h, SD0 + 2 * t)
                for kc in range(NKC):
                    mm = t_.transpose(out=pst0[:, 128 * kc:128 * (kc + 1)],
                                      in_=self.xntm[:, 128 * kc:128 * (kc + 1)],
                                      identity=self.idf16[:])
                mm.then_inc(ts.h, 1)
                for half in range(2):
                    mm = t_.matmul(out=ps_o[half][0:1, 0:512],
                                   lhsT=self.onesbf[:], rhs=self.xntm[:, 512 * half:512 * half + 512],
                                   start=(t == 0), stop=(t == NT - 1), skip_group_check=True)
                mm.then_inc(ts.h, 1)
        ts.n += 2 * NT
        TD_END = ts.n

        @block.scalar
        def _(s):
            s.wait_ge(ts.h, TD_END)
            s.mul(out=self.pooled_sb[:, 0:512], in_=ps_o[0][0:1, 0:512], mul=1.0 / S)
            s.mul(out=self.pooled_sb[:, 512:1024], in_=ps_o[1][0:1, 0:512], mul=1.0 / S).then_inc(ss.h, 1)
        ss.n += 1
        SS_POOL = ss.n

        @block.tensor
        def _(t_):
            t_.wait_ge(ss.h, SS_POOL)
            for c in range(NKC):
                mm = t_.matmul(out=ps_s[:, c:c + 1], lhsT=self.pooled_sb[:, 128 * c:128 * (c + 1)],
                               rhs=self.one11[:], start=True, stop=True)
            mm.then_inc(ts.h, 1)
        ts.n += 1
        TS_PT = ts.n

        @block.vector
        def _(v):
            v.wait_ge(ts.h, TS_PT)
            v.tensor_copy(out=self.pooledT[:], in_=ps_s[:, 0:8]).then_inc(vs.h, 1)
        vs.n += 1
        VS_PT = vs.n

        @block.tensor
        def _(t_):
            t_.wait_ge(vs.h, VS_PT)
            for c in range(NKC):
                mm = t_.matmul(out=ps_o[0][0:1, 0:8], lhsT=self.pooledT[:, c:c + 1],
                               rhs=self.gwT[:, E * c:E * (c + 1)],
                               start=(c == 0), stop=(c == NKC - 1))
            mm.then_inc(ts.h, 1)
        ts.n += 1
        TS_LG = ts.n

        @block.vector
        def _(v):
            v.wait_ge(ts.h, TS_LG)
            v.tensor_tensor(out=self.lgt[:], in0=ps_o[0][0:1, 0:8], in1=self.gb[:], op=ALU.add)
            v.drain()
            v.reduce_max(out=self.gsc[:, 0:1], in_=self.lgt[:], axis=AX)
            v.drain()
            v.tensor_scalar(out=self.gates[:], in0=self.lgt[:], scalar1=self.gsc[:, 0:1],
                            scalar2=-1e30, op0=ALU.is_ge, op1=ALU.mult)
            v.drain()
            v.tensor_tensor(out=self.gates[:], in0=self.gates[:], in1=self.lgt[:], op=ALU.add)
            v.drain()
            v.reduce_max(out=self.gsc[:, 1:2], in_=self.gates[:], axis=AX)
            v.drain()
            v.tensor_scalar(out=self.gates[:], in0=self.lgt[:], scalar1=self.gsc[:, 1:2],
                            scalar2=-1e30, op0=ALU.is_lt, op1=ALU.mult)
            v.drain()
            v.tensor_tensor(out=self.gates[:], in0=self.gates[:], in1=self.lgt[:], op=ALU.add)
            v.drain()
            v.tensor_scalar(out=self.gates[:], in0=self.gates[:], scalar1=self.gsc[:, 0:1],
                            scalar2=None, op0=ALU.subtract).then_inc(vs.h, 1)
        vs.n += 1
        VS_ARG = vs.n

        @block.scalar
        def _(s):
            s.wait_ge(vs.h, VS_ARG)
            s.activation(out=self.gates[:], in_=self.gates[:], func=AF.Exp).then_inc(ss.h, 1)
        ss.n += 1
        SS_EXPG = ss.n

        @block.vector
        def _(v):
            v.wait_ge(ss.h, SS_EXPG)
            v.reduce_sum(out=self.gsc[:, 2:3], in_=self.gates[:], axis=AX)
            v.drain()
            v.reciprocal(out=self.gsc[:, 3:4], in_=self.gsc[:, 2:3])
            v.drain()
            v.tensor_scalar(out=self.gates[:], in0=self.gates[:], scalar1=self.gsc[:, 3:4],
                            scalar2=None, op0=ALU.mult).then_inc(vs.h, 1)
        vs.n += 1
        VS_GATES = vs.n

        VS_SEL0 = vs.n

        @block.vector
        def _(v):
            v.wait_ge(vs.h, VS_GATES)
            v.reduce_max(out=self.gsc[:, 4:5], in_=self.gates[:], axis=AX)
            v.drain()
            v.tensor_scalar(out=self.lgt[:], in0=self.gates[:], scalar1=self.gsc[:, 4:5],
                            scalar2=None, op0=ALU.is_ge)
            v.drain()
            v.tensor_tensor(out=self.lgt[:], in0=self.lgt[:], in1=self.iota8[:], op=ALU.mult)
            v.drain()
            v.reduce_sum(out=self.gsc[:, 5:6], in_=self.lgt[:], axis=AX)
            v.drain()
            v.tensor_scalar(out=self.lgt[:], in0=self.gates[:], scalar1=self.gsc[:, 4:5],
                            scalar2=-2.0, op0=ALU.is_ge, op1=ALU.mult)
            v.drain()
            v.tensor_tensor(out=self.lgt[:], in0=self.lgt[:], in1=self.gates[:], op=ALU.add)
            v.drain()
            v.reduce_max(out=self.gsc[:, 6:7], in_=self.lgt[:], axis=AX)
            v.drain()
            v.tensor_scalar(out=self.lgt[:], in0=self.lgt[:], scalar1=self.gsc[:, 6:7],
                            scalar2=None, op0=ALU.is_ge)
            v.drain()
            v.tensor_tensor(out=self.lgt[:], in0=self.lgt[:], in1=self.iota8[:], op=ALU.mult)
            v.drain()
            v.reduce_sum(out=self.gsc[:, 7:8], in_=self.lgt[:], axis=AX)
            v.drain()
            v.tensor_copy(out=self.gates[:, 0:1], in_=self.gsc[:, 5:6])
            v.tensor_copy(out=self.gates[:, 1:2], in_=self.gsc[:, 7:8])
            v.tensor_copy(out=self.gates[:, 2:3], in_=self.gsc[:, 4:5])
            v.drain()
            v.nop().then_inc(vs.h, 1)
        vs.n += 1
        VS_SEL = vs.n

        @block.tensor
        def _(t_):
            t_.wait_ge(vs.h, VS_SEL)
            t_.matmul(out=ps_o[0][:, 0:3], lhsT=self.ones1f[:], rhs=self.gates[:, 0:3],
                      start=True, stop=True).then_inc(ts.h, 1)
        ts.n += 1
        TS_SELB = ts.n

        @block.vector
        def _(v):
            v.wait_ge(ts.h, TS_SELB)
            v.tensor_copy(out=self.gsel[:, 0:1], in_=ps_o[0][:, 2:3])
            v.tensor_scalar(out=self.gsel[:, 1:2], in0=ps_o[0][:, 2:3], scalar1=-1.0,
                            scalar2=1.0, op0=ALU.mult, op1=ALU.add)
            for s in range(2):
                v.tensor_scalar(out=self.idxwf[:, NKC * s:NKC * s + 1], in0=ps_o[0][:, s:s + 1],
                                scalar1=float(D), scalar2=None, op0=ALU.mult)
            v.drain()
            for s in range(2):
                v.tensor_tensor(out=self.idxwf[:, NKC * s:NKC * s + 1],
                                in0=self.idxwf[:, NKC * s:NKC * s + 1],
                                in1=self.iotap[:], op=ALU.add)
            v.drain()
            for s in range(2):
                for kc in range(1, NKC):
                    v.tensor_scalar(out=self.idxwf[:, NKC * s + kc:NKC * s + kc + 1],
                                    in0=self.idxwf[:, NKC * s:NKC * s + 1],
                                    scalar1=float(128 * kc), scalar2=None, op0=ALU.add)
            v.drain()
            v.tensor_copy(out=self.idxw[:], in_=self.idxwf[:])
            v.tensor_copy(out=self.idxb[:, 0:1], in_=ps_o[0][:, 0:1])
            v.tensor_copy(out=self.idxb[:, 1:2], in_=ps_o[0][:, 1:2])
            v.drain()
            v.nop().then_inc(vs.h, 1)
        vs.n += 1
        self.VS_IDX = vs.n
        self.pso_free = (self.vs, self.VS_IDX)

        if self.tap == "gates" and b == 0:
            self.tap_dump(self.gsel)

    def _ffn(self, b):
        block = self.block
        dx, dw, ts, vs, ss = self.dx, self.dw, self.ts, self.vs, self.ss
        ps_s, ps_o = self.ps_s, self.ps_o
        VS_IDX = self.VS_IDX
        TAO_END = self.TAO_END
        NS = 2

        T0 = ts.n
        V0 = vs.n
        S0 = ss.n
        DW0 = dw.n
        self.VF0 = V0
        self.FFN_VPE = 2 * NT

        tpe = NT
        vpe = 2 * NT
        spe = NT
        IOA = bass.IndirectOffsetOnAxis

        @block.gpsimd
        def _(gp):
            for s in range(NS):
                gp.wait_ge(ts.h, TAO_END)
                gp.wait_ge(vs.h, VS_IDX)
                if s == 1:
                    gp.wait_ge(vs.h, V0 + 2 * (NT - 1) + 1)
                for kc in range(NKC):
                    gp.indirect_dma_start(
                        out=self.wexp[s][:, D * kc:D * (kc + 1)], out_offset=None,
                        in_=self.weT_h[:, :],
                        in_offset=IOA(ap=self.idxw[:, NKC * s + kc:NKC * s + kc + 1], axis=0),
                    ).then_inc(dw.h, 16)
                gp.indirect_dma_start(
                    out=self.bbc[:], out_offset=None, in_=self.be_h[:, :],
                    in_offset=IOA(ap=self.idxb[:, s:s + 1], axis=0),
                ).then_inc(dw.h, 16)
        dw.n += NS * (NKC + 1) * 16

        @block.tensor
        def _(t_):
            for s in range(NS):
                t_.wait_ge(dw.h, DW0 + 16 * (NKC + 1) * (s + 1))
                if s == 0:
                    t_.wait_ge(vs.h, VS_IDX)
                for tc in range(NT):
                    lo = 1024 * (tc % 2)
                    if s == 0 and tc < 2:
                        pass
                    elif tc >= 2:
                        t_.wait_ge(vs.h, V0 + s * vpe + 2 * (tc - 2) + 1)
                    else:
                        t_.wait_ge(vs.h, V0 + (s - 1) * vpe + 2 * (NT - 2 + tc) + 1)
                    w = self.wexp[s]
                    for half in range(2):
                        for kc in range(NKC):
                            mm = t_.matmul(out=ps_s[:, lo + 512 * half:lo + 512 * half + 512],
                                           lhsT=self.xnT[:, S * kc + 128 * tc:S * kc + 128 * (tc + 1)],
                                           rhs=w[:, D * kc + 512 * half:D * kc + 512 * half + 512],
                                           start=(kc == 0), stop=(kc == NKC - 1))
                    mm.then_inc(ts.h, 1)
        ts.n += NS * tpe

        @block.vector
        def _(v):
            v.wait_ge(vs.h, VS_IDX)
            for s in range(NS):
                for tc in range(NT):
                    lo = 1024 * (tc % 2)
                    v.wait_ge(ts.h, T0 + s * tpe + tc + 1)
                    if tc == 0:
                        v.wait_ge(dw.h, DW0 + 16 * (NKC + 1) * (s + 1))
                    if tc >= 2:
                        v.wait_ge(ss.h, S0 + s * spe + (tc - 2) + 1)
                    elif s > 0:
                        v.wait_ge(ss.h, S0 + (s - 1) * spe + (NT - 2 + tc) + 1)
                    v.tensor_tensor(out=self.ftmp[tc % 2][:, 0:512],
                                    in0=ps_s[:, lo:lo + 512], in1=self.bbc[:, 0:512], op=ALU.add)
                    v.tensor_tensor(out=self.ftmp[tc % 2][:, 512:1024],
                                    in0=ps_s[:, lo + 512:lo + 1024], in1=self.bbc[:, 512:1024],
                                    op=ALU.add).then_inc(vs.h, 1)
                    v.wait_ge(ss.h, S0 + s * spe + tc + 1)
                    v.scalar_tensor_tensor(out=self.x2[:, D * tc:D * (tc + 1)],
                                           in0=self.ftmp[tc % 2][:], scalar=self.gsel[:, s:s + 1],
                                           in1=self.x2[:, D * tc:D * (tc + 1)],
                                           op0=ALU.mult, op1=ALU.add).then_inc(vs.h, 1)
        vs.n += NS * vpe

        @block.scalar
        def _(s_):
            for s in range(NS):
                for tc in range(NT):
                    s_.wait_ge(vs.h, V0 + s * vpe + 2 * tc + 1)
                    s_.activation(out=self.ftmp[tc % 2][:], in_=self.ftmp[tc % 2][:],
                                  func=AF.Gelu).then_inc(ss.h, 1)
        ss.n += NS * spe



def _consts():
    p = np.arange(128)[:, None].astype(np.float32)
    g = np.arange(2048)[None, :].astype(np.float32)
    d0mv = (p - g).astype(np.float32)
    f = np.arange(128)[None, :].astype(np.float32)
    g640v = np.where(p <= f, p - f, MASKVAL).astype(np.float32)
    r32 = np.float32(_RATIO)
    t0mv = (d0mv * r32).astype(np.float32)
    gt640v = (g640v * r32).astype(np.float32)
    idf16 = np.eye(128, dtype=np.float16)
    idbf = np.eye(128, dtype=ml_dtypes.bfloat16)
    return d0mv.astype(np.float16), g640v.astype(np.float16), t0mv, gt640v, idf16, idbf


def make_in_maps(L0_attn_x, L0_attn_q_w, L0_attn_k_w, L0_attn_v_w, L0_attn_o_w,
                 L0_ffn_gate_w, L0_ffn_gate_b, L0_ffn_expert_w, L0_ffn_expert_b):
    x = np.ascontiguousarray(np.asarray(L0_attn_x, dtype=np.float32))
    wq = np.ascontiguousarray(np.asarray(L0_attn_q_w, dtype=np.float32))
    wk = np.ascontiguousarray(np.asarray(L0_attn_k_w, dtype=np.float32))
    wv = np.ascontiguousarray(np.asarray(L0_attn_v_w, dtype=np.float32))
    wo = np.ascontiguousarray(np.asarray(L0_attn_o_w, dtype=np.float32))
    gwT = np.ascontiguousarray(np.asarray(L0_ffn_gate_w, dtype=np.float32).T)
    gb = np.ascontiguousarray(np.asarray(L0_ffn_gate_b, dtype=np.float32)[None, :])
    weT = np.ascontiguousarray(
        np.asarray(L0_ffn_expert_w, dtype=np.float32).transpose(0, 2, 1).reshape(E * D, D))
    be = np.ascontiguousarray(np.asarray(L0_ffn_expert_b, dtype=np.float32))
    iotap = np.arange(128, dtype=np.float32)[:, None]
    iota8 = np.arange(E, dtype=np.float32)[None, :]
    d0mv, g640v, t0mv, gt640v, idf16, idbf = _consts()
    in_maps = []
    for c in range(8):
        in_maps.append({
            "x": x[2 * c:2 * c + 2],
            "wq": wq, "wk": wk, "wv": wv, "wo": wo,
            "gwT": gwT, "gb": gb, "weT": weT, "be": be,
            "idf16": idf16, "idbf": idbf, "d0m": d0mv, "g640": g640v,
            "t0m": t0mv, "gt640": gt640v,
            "iotap": iotap, "iota8": iota8,
        })
    return in_maps


def kernel(L0_attn_x, L0_attn_q_w, L0_attn_k_w, L0_attn_v_w, L0_attn_o_w,
           L0_ffn_gate_w, L0_ffn_gate_b, L0_ffn_expert_w, L0_ffn_expert_b):
    in_maps = make_in_maps(L0_attn_x, L0_attn_q_w, L0_attn_k_w, L0_attn_v_w, L0_attn_o_w,
                           L0_ffn_gate_w, L0_ffn_gate_b, L0_ffn_expert_w, L0_ffn_expert_b)
    nc = Builder().build()
    res = run_bass_kernel_spmd(nc, in_maps, core_ids=list(range(8)))
    out = np.concatenate([res.results[c]["out"] for c in range(8)], axis=0)
    return np.ascontiguousarray(out.astype(np.float32))
